# version 03
# baseline (speedup 1.0000x reference)
import contextlib
import numpy as np
import concourse.bass as bass
import concourse.mybir as mybir
from concourse.bass_utils import run_bass_kernel_spmd

F32 = mybir.dt.float32
BF16 = mybir.dt.bfloat16
I32 = mybir.dt.int32
AF = mybir.ActivationFunctionType
ALU = mybir.AluOpType
AX = mybir.AxisListType

NCORES = 8
B_PER_CORE = 2
SEQ = 2048
TOKENS = B_PER_CORE * SEQ
D = 1024
LN_EPS = 1e-5


def _names(aps):
    out = set()
    for a in aps:
        if isinstance(a, bass.AP):
            out.add(a.tensor.name)
    return out


class Eng:

    def __init__(self, eng):
        self.e = eng
        self.w = set()
        self.r = set()

    def _pre(self, outs, ins, nodep=False):
        on, inn = _names(outs), _names(ins)
        if not nodep and ((self.w & (on | inn)) or (self.r & on)):
            self.e.drain()
            self.w.clear()
            self.r.clear()
        self.w |= on
        self.r |= inn

    def copy(self, out, in_, nodep=False):
        self._pre([out], [in_], nodep)
        return self.e.tensor_copy(out, in_)

    def memset(self, out, val, nodep=False):
        self._pre([out], [], nodep)
        return self.e.memset(out, val)

    def tt(self, out, a, b, op, nodep=False):
        self._pre([out], [a, b], nodep)
        return self.e.tensor_tensor(out, a, b, op)

    def ts(self, out, in0, s1, s2=None, op0=ALU.mult, op1=ALU.bypass, nodep=False):
        self._pre([out], [in0, s1, s2], nodep)
        return self.e.tensor_scalar(out, in0, s1, s2, op0, op1)

    def tss(self, out, in_, s, op, nodep=False):
        self._pre([out], [in_, s], nodep)
        return self.e.tensor_single_scalar(out, in_, s, op)

    def stt(self, out, in0, s, in1, op0, op1, nodep=False):
        self._pre([out], [in0, s, in1], nodep)
        return self.e.scalar_tensor_tensor(out, in0, s, in1, op0, op1)

    def scan(self, out, d0, d1, init, op0=ALU.mult, op1=ALU.add, nodep=False):
        self._pre([out], [d0, d1, init], nodep)
        return self.e.tensor_tensor_scan(out, d0, d1, init, op0, op1)

    def red(self, out, in_, op, absval=None, nodep=False):
        self._pre([out], [in_], nodep)
        return self.e.tensor_reduce(out, in_, AX.X, op, apply_absolute_value=absval)

    def select(self, out, mask, a, b, nodep=False):
        self._pre([out], [mask, a, b], nodep)
        return self.e.select(out, mask, a, b)

    def iota(self, out, pattern, base=None, channel_multiplier=0, nodep=False):
        self._pre([out], [], nodep)
        return self.e.iota(out, pattern, base=base, channel_multiplier=channel_multiplier,
                           allow_small_or_imprecise_dtypes=True)

    def recip(self, out, in_, nodep=False):
        self._pre([out], [in_], nodep)
        return self.e.reciprocal(out, in_)

    def act(self, out, in_, func, bias=0.0, scale=1.0, accum_out=None, nodep=False):
        self._pre([out, accum_out], [in_, bias, scale], nodep)
        return self.e.activation(out, in_, func, bias=bias, scale=scale, accum_out=accum_out)

    def mm(self, out, lhsT, rhs, start=True, stop=True):
        return self.e.matmul(out, lhsT, rhs, start=start, stop=stop)

    def transpose(self, out, in_, iden):
        return self.e.transpose(out, in_, iden)


class Em:

    def __init__(self, name="k", dma_scratch=16384):
        self.nc = bass.Bass("TRN2", target_bir_lowering=False, name=name,
                            dynamic_dma_scratch_size=dma_scratch)
        self.ctx = contextlib.ExitStack()
        self.dm = self.ctx.enter_context(self.nc.semaphore("dm"))
        self.dmc = 0
        self.dmg = self.ctx.enter_context(self.nc.semaphore("dmg"))
        self.dmgc = 0
        self._uid = 0

    def sbuf(self, name, shape, dtype=F32):
        return self.ctx.enter_context(self.nc.sbuf_tensor(name, shape, dtype))

    def constf(self, val, dtype=F32):
        key = (float(val), dtype)
        if not hasattr(self, "_consts"):
            self._consts = {}
        if key not in self._consts:
            t = self.sbuf(f"cst{len(self._consts)}", [128, 1], dtype)
            self.blk(vector=lambda v: v.memset(t[:, :], val))
            self._consts[key] = t
        return self._consts[key][:, :]

    def psum(self, name, shape, dtype=F32):
        return self.ctx.enter_context(self.nc.psum_tensor(name, shape, dtype))

    def dram(self, name, shape, dtype=F32, kind="Internal"):
        return self.nc.dram_tensor(name, shape, dtype, kind=kind).ap()

    def dma(self, eng, dst, src, **kw):
        if getattr(eng, "engine", None) == mybir.EngineType.Pool:
            eng.dma_start(out=dst, in_=src, **kw).then_inc(self.dmg, 16)
            self.dmgc += 16
        else:
            eng.dma_start(out=dst, in_=src, **kw).then_inc(self.dm, 16)
            self.dmc += 16

    def dma_t(self, eng, dst, src):
        eng.dma_start(out=dst, in_=src, transpose=True).then_inc(self.dm, 16)
        self.dmc += 16

    def dwait(self, eng):
        eng.wait_ge(self.dm, self.dmc)
        eng.wait_ge(self.dmg, self.dmgc)

    def blk(self, tensor=None, vector=None, scalar=None, gpsimd=None, sync=None,
            wait_dma=()):
        with self.nc.Block() as b:
            def _wrap(dec, fn, raw=False):
                if fn is None:
                    return
                def body(e):
                    if dec in wait_dma:
                        e.wait_ge(self.dm, self.dmc)
                    fn(e if raw else Eng(e))
                getattr(b, dec)(body)
            _wrap("sync", sync, raw=True)
            _wrap("gpsimd", gpsimd, raw=True)
            _wrap("tensor", tensor)
            _wrap("vector", vector)
            _wrap("scalar", scalar)

    def dma_block(self, fn, engine="sync", wait=True):
        def body(e):
            fn(e)
            if wait:
                self.dwait(e)
        if engine == "sync":
            self.blk(sync=body)
        else:
            self.blk(gpsimd=body)

    def finish(self):
        self.ctx.close()



def emit_ln(E, xs, nt, st, ln_out, D_=D, eps=LN_EPS):
    def v1(v):
        for j in range(nt):
            v.red(st["sum"][:, j:j + 1].unsqueeze(1), xs[:, j, :].unsqueeze(1),
                  ALU.add, nodep=True)
    E.blk(vector=v1)

    def s1(s):
        s.act(st["mean"][:, :], st["sum"][:, :], AF.Copy, scale=1.0 / D_)
        for j in range(nt):
            s.act(st["trash"][:, :], xs[:, j, :], AF.Square,
                  accum_out=st["sqs"][:, j:j + 1])
        s.act(st["msq"][:, :], st["mean"][:, :], AF.Square)
    E.blk(scalar=s1)

    def v2(v):
        v.ts(st["var"][:, :], st["sqs"][:, :], 1.0 / D_, None, ALU.mult)
        v.tt(st["var"][:, :], st["var"][:, :], st["msq"][:, :], ALU.subtract)
    E.blk(vector=v2)

    epsap = E.constf(eps)

    def s2(s):
        s.act(st["var"][:, :], st["var"][:, :], AF.Ln, bias=epsap)
        s.act(st["rstd"][:, :], st["var"][:, :], AF.Exp, scale=-0.5)
    E.blk(scalar=s2)

    def v3(v):
        for j in range(nt):
            v.ts(ln_out[:, j, :], xs[:, j, :], st["mean"][:, j:j + 1],
                 st["rstd"][:, j:j + 1], ALU.subtract, ALU.mult, nodep=True)
    E.blk(vector=v3)


def alloc_ln_stats(E, nt, D_=D, tag=""):
    return {
        "sum": E.sbuf(f"lnsum{tag}", [128, nt]),
        "mean": E.sbuf(f"lnmean{tag}", [128, nt]),
        "sqs": E.sbuf(f"lnsqs{tag}", [128, nt]),
        "msq": E.sbuf(f"lnmsq{tag}", [128, nt]),
        "var": E.sbuf(f"lnvar{tag}", [128, nt]),
        "rstd": E.sbuf(f"lnrstd{tag}", [128, nt]),
        "trash": E.sbuf(f"lntrash{tag}", [128, D_]),
    }


def emit_transpose_bf(E, ln_bf, lnT, nt, nd):
    def f(sy):
        for j in range(nt):
            for c in range(nd):
                E.dma_t(sy, lnT[:, c, j * 128:(j + 1) * 128],
                        ln_bf[:, j, c * 128:(c + 1) * 128])
        E.dwait(sy)
    E.blk(sync=f)



def build_l0_ffn():
    E = Em("l0ffn")
    nc = E.nc
    x_in = E.dram("x", [TOKENS, D], kind="ExternalInput")
    wgT = E.dram("wgT", [1024, 3072], I32, kind="ExternalInput")
    wuT = E.dram("wuT", [1024, 3072], I32, kind="ExternalInput")
    wdT = E.dram("wdT", [3072, 1024], I32, kind="ExternalInput")
    x_out = E.dram("xout", [TOKENS, D], kind="ExternalOutput")

    MEGA = 1024
    NT = MEGA // 128
    NMEGA = TOKENS // MEGA
    FCH = 24
    KCH = 8
    SCALE = 1.0 / 127.0

    xs = E.sbuf("xs", [128, NT, D])
    ln_bf = E.sbuf("lnbf", [128, NT, D], BF16)
    lnT = E.sbuf("lnT", [128, KCH, MEGA], BF16)
    h_bf = E.sbuf("hbf", [128, FCH, MEGA], BF16)
    silu_g = E.sbuf("silug", [128, MEGA])
    wg_str = E.sbuf("wgstr", [128, KCH, 512], BF16)
    wu_str = E.sbuf("wustr", [128, KCH, 512], BF16)
    wd_sb = E.sbuf("wdsb", [128, FCH, 512], BF16)
    st = alloc_ln_stats(E, NT)
    pads = [E.psum(f"p{i}", [128, 512]) for i in range(8)]

    for mg in range(NMEGA):
        t0 = mg * MEGA
        def fx(sy, t0=t0):
            for j in range(NT):
                E.dma(sy, xs[:, j, :], x_in[t0 + j * 128: t0 + (j + 1) * 128, :])
            E.dwait(sy)
        E.blk(sync=fx)

        emit_ln(E, xs, NT, st, ln_bf)
        emit_transpose_bf(E, ln_bf, lnT, NT, KCH)

        for fg in range(FCH // 4):
            def fw(g, fg=fg):
                for k in range(KCH):
                    E.dma(g, wg_str[:, k, :], wgT[k * 128:(k + 1) * 128,
                                                  fg * 512:(fg + 1) * 512])
                    E.dma(g, wu_str[:, k, :], wuT[k * 128:(k + 1) * 128,
                                                  fg * 512:(fg + 1) * 512])
                E.dwait(g)
            E.blk(gpsimd=fw)

            for f_in in range(4):
                f = fg * 4 + f_in
                pg = pads[0:2]
                pu = pads[2:4]

                def fmm(t, f_in=f_in):
                    for k in range(KCH):
                        for th in range(2):
                            t.mm(pg[th][:, :], wg_str[:, k, f_in * 128:(f_in + 1) * 128],
                                 lnT[:, k, th * 512:(th + 1) * 512],
                                 start=(k == 0), stop=(k == KCH - 1))
                            t.mm(pu[th][:, :], wu_str[:, k, f_in * 128:(f_in + 1) * 128],
                                 lnT[:, k, th * 512:(th + 1) * 512],
                                 start=(k == 0), stop=(k == KCH - 1))
                E.blk(tensor=fmm)

                def fsilu(s):
                    for th in range(2):
                        s.act(silu_g[:, th * 512:(th + 1) * 512], pg[th][:, :],
                              AF.Silu, scale=SCALE, nodep=True)
                E.blk(scalar=fsilu)

                def fh(v, f=f):
                    for th in range(2):
                        v.stt(h_bf[:, f, th * 512:(th + 1) * 512], pu[th][:, :],
                              SCALE, silu_g[:, th * 512:(th + 1) * 512],
                              ALU.mult, ALU.mult, nodep=True)
                E.blk(vector=fh)

        for dh in range(2):
            def fwd(g, dh=dh):
                for fk in range(FCH):
                    E.dma(g, wd_sb[:, fk, :], wdT[fk * 128:(fk + 1) * 128,
                                                  dh * 512:(dh + 1) * 512])
                E.dwait(g)
            E.blk(gpsimd=fwd)

            def fdmm(t, dh=dh):
                for fk in range(FCH):
                    for tt in range(NT):
                        t.mm(pads[tt][:, :], h_bf[:, fk, tt * 128:(tt + 1) * 128],
                             wd_sb[:, fk, :], start=(fk == 0), stop=(fk == FCH - 1))
            E.blk(tensor=fdmm)

            def fdev(v, dh=dh):
                for tt in range(NT):
                    v.stt(xs[:, tt, dh * 512:(dh + 1) * 512], pads[tt][:, :], SCALE,
                          xs[:, tt, dh * 512:(dh + 1) * 512], ALU.mult, ALU.add,
                          nodep=True)
            E.blk(vector=fdev)

        def fo(sy, t0=t0):
            for j in range(NT):
                E.dma(sy, x_out[t0 + j * 128: t0 + (j + 1) * 128, :], xs[:, j, :])
            E.dwait(sy)
        E.blk(sync=fo)

    E.finish()
    return E.nc


def launch_l0_ffn(state, inp):
    nc = build_l0_ffn()
    wgT = np.ascontiguousarray(inp["L0_ffn_w_gate"].T)
    wuT = np.ascontiguousarray(inp["L0_ffn_w_up"].T)
    wdT = np.ascontiguousarray(inp["L0_ffn_w_down"].T)
    in_maps = [{"x": state[i], "wgT": wgT, "wuT": wuT, "wdT": wdT}
               for i in range(NCORES)]
    res = run_bass_kernel_spmd(nc, in_maps, core_ids=list(range(NCORES)))
    return [np.asarray(r["xout"]) for r in res.results]



def emit_quant_fp6(E, src_views, dst_views, s_views, scr, nblk, blk=32):
    n = len(src_views)

    def vq(v):
        for i in range(n):
            g = src_views[i].rearrange("p (a b) -> p a b", b=blk)
            v.red(s_views[i].unsqueeze(2), g, ALU.max, absval=True, nodep=True)
        v.e.drain()
        for i in range(n):
            v.ts(s_views[i], s_views[i], 1.0 / 7.5, 1e-30, ALU.mult, ALU.max,
                 nodep=True)
        v.e.drain()
        for i in range(n):
            v.recip(scr["sinv"][i], s_views[i], nodep=True)
        v.e.drain()
        q, absq = scr["q"][:, :], scr["absq"][:, :]
        c2, c4 = scr["c2"][:, :], scr["c4"][:, :]
        m, minv = scr["m"][:, :], scr["minv"][:, :]
        ti, tf = scr["ti"][:, :], scr["tf"][:, :]
        for i in range(n):
            sb = scr["sinv"][i].unsqueeze(2).broadcast_to((128, nblk, blk))
            v.tt(q.rearrange("p (a b) -> p a b", b=blk),
                 src_views[i].rearrange("p (a b) -> p a b", b=blk), sb, ALU.mult)
            v.ts(absq, q, -1.0, None, ALU.mult)
            v.tt(absq, q, absq, ALU.max)
            v.tss(c2, absq, 2.0, ALU.is_ge)
            v.tss(c4, absq, 4.0, ALU.is_ge)
            v.ts(m, c2, -4.0, 8.0, ALU.mult, ALU.add)
            v.stt(m, c4, -2.0, m, ALU.mult, ALU.add)
            v.tt(q, q, m, ALU.mult)
            v.copy(ti, q)
            v.copy(tf, ti)
            v.ts(minv, c2, 0.125, 0.125, ALU.mult, ALU.add)
            v.stt(minv, c4, 0.25, minv, ALU.mult, ALU.add)
            v.tt(tf, tf, minv, ALU.mult)
            sb2 = s_views[i].unsqueeze(2).broadcast_to((128, nblk, blk))
            v.tt(dst_views[i].rearrange("p (a b) -> p a b", b=blk),
                 tf.rearrange("p (a b) -> p a b", b=blk), sb2, ALU.mult)
    E.blk(vector=vq)


def alloc_q_scratch(E, F, tag=""):
    return {
        "q": E.sbuf(f"qq{tag}", [128, F]),
        "absq": E.sbuf(f"qa{tag}", [128, F]),
        "c2": E.sbuf(f"qc2{tag}", [128, F]),
        "c4": E.sbuf(f"qc4{tag}", [128, F]),
        "m": E.sbuf(f"qm{tag}", [128, F]),
        "minv": E.sbuf(f"qmi{tag}", [128, F]),
        "ti": E.sbuf(f"qti{tag}", [128, F], I32),
        "tf": E.sbuf(f"qtf{tag}", [128, F]),
        "sinv": None,
    }


def build_l1_ffn():
    E = Em("l1ffn")
    x_in = E.dram("x", [TOKENS, D], kind="ExternalInput")
    w_in = E.dram("w", [1024, 1024], kind="ExternalInput")
    b_in = E.dram("b", [1024], kind="ExternalInput")
    x_out = E.dram("xout", [TOKENS, D], kind="ExternalOutput")

    MEGA = 1024
    NT = MEGA // 128
    NMEGA = TOKENS // MEGA
    NBLK = 32

    xs = E.sbuf("xs", [128, NT, D])
    lnf = E.sbuf("lnf", [128, NT, D])
    xq_bf = E.sbuf("xqbf", [128, NT, D], BF16)
    xqT = E.sbuf("xqT", [128, 8, MEGA], BF16)
    st = alloc_ln_stats(E, NT)
    scr = alloc_q_scratch(E, D)
    sx = E.sbuf("sx", [128, NT, NBLK])
    scr["sinv"] = None

    wrow = E.sbuf("wrow", [128, 1024])
    wq_bf = E.sbuf("wqbf", [128, 8, 1024], BF16)
    wqT = E.sbuf("wqT", [128, 8, 1024], BF16)
    sw = E.sbuf("sw", [128, 8, NBLK])
    bias_bc = E.sbuf("biasbc", [128, 1024])
    brow = E.sbuf("brow", [1, 1024])
    ones1 = E.sbuf("ones1", [1, 128], BF16)
    brow_bf = E.sbuf("browbf", [1, 1024], BF16)
    pads = [E.psum(f"p{i}", [128, 512]) for i in range(8)]

    def fb0(sy):
        E.dma(sy, brow[:, :], b_in.unsqueeze(0))
        E.dwait(sy)
    E.blk(sync=fb0)

    def fo1(v):
        v.memset(ones1[:, :], 1.0)
        v.copy(brow_bf[:, :], brow[:, :])
    E.blk(vector=fo1)

    sw2 = E.sbuf("sw2", [128, 8, NBLK])
    for c in range(8):
        def fw(sy, c=c):
            E.dma(sy, wrow[:, :], w_in[c * 128:(c + 1) * 128, :])
            E.dwait(sy)
        E.blk(sync=fw)
        scr["sinv"] = [sw[:, c, :]]
        emit_quant_fp6(E, [wrow[:, :]], [wq_bf[:, c, :]],
                       [sw2[:, c, :]], scr, NBLK)

    def ftw(sy):
        for c in range(8):
            for k in range(8):
                E.dma_t(sy, wqT[:, k, c * 128:(c + 1) * 128],
                        wq_bf[:, c, k * 128:(k + 1) * 128])
        E.dwait(sy)
    E.blk(sync=ftw)

    def fbb(t):
        for h in range(2):
            t.mm(pads[h][:, :], ones1[:, :], brow_bf[:, h * 512:(h + 1) * 512])
    E.blk(tensor=fbb)

    def fbe(v):
        for h in range(2):
            v.copy(bias_bc[:, h * 512:(h + 1) * 512], pads[h][:, :], nodep=True)
    E.blk(vector=fbe)

    sxv = E.sbuf("sxv", [128, NT, NBLK])

    for mg in range(NMEGA):
        t0 = mg * MEGA

        def fx(sy, t0=t0):
            for j in range(NT):
                E.dma(sy, xs[:, j, :], x_in[t0 + j * 128:t0 + (j + 1) * 128, :])
            E.dwait(sy)
        E.blk(sync=fx)

        emit_ln(E, xs, NT, st, lnf)

        scr["sinv"] = [sxv[:, j, :] for j in range(NT)]
        emit_quant_fp6(E, [lnf[:, j, :] for j in range(NT)],
                       [xq_bf[:, j, :] for j in range(NT)],
                       [sx[:, j, :] for j in range(NT)], scr, NBLK)

        def ftx(sy):
            for j in range(NT):
                for k in range(8):
                    E.dma_t(sy, xqT[:, k, j * 128:(j + 1) * 128],
                            xq_bf[:, j, k * 128:(k + 1) * 128])
            E.dwait(sy)
        E.blk(sync=ftx)

        def fmm(t):
            for j in range(NT):
                for oh in range(2):
                    p = pads[(j * 2 + oh) % 8]
                    for k in range(8):
                        t.mm(p[:, :], xqT[:, k, j * 128:(j + 1) * 128],
                             wqT[:, k, oh * 512:(oh + 1) * 512],
                             start=(k == 0), stop=(k == 7))
        def fmm_half(t, jlist):
            for j in jlist:
                for oh in range(2):
                    p = pads[((j % 4) * 2 + oh)]
                    for k in range(8):
                        t.mm(p[:, :], xqT[:, k, j * 128:(j + 1) * 128],
                             wqT[:, k, oh * 512:(oh + 1) * 512],
                             start=(k == 0), stop=(k == 7))

        def fev_half(v, jlist):
            for j in jlist:
                for oh in range(2):
                    p = pads[((j % 4) * 2 + oh)]
                    sl = slice(oh * 512, (oh + 1) * 512)
                    v.tt(xs[:, j, sl], xs[:, j, sl], bias_bc[:, sl], ALU.add,
                         nodep=True)
                    v.tt(xs[:, j, sl], xs[:, j, sl], p[:, :], ALU.add)

        for half in range(2):
            jl = list(range(half * 4, half * 4 + 4))
            E.blk(tensor=lambda t, jl=jl: fmm_half(t, jl))
            E.blk(vector=lambda v, jl=jl: fev_half(v, jl))

        def fo(sy, t0=t0):
            for j in range(NT):
                E.dma(sy, x_out[t0 + j * 128:t0 + (j + 1) * 128, :], xs[:, j, :])
            E.dwait(sy)
        E.blk(sync=fo)

    E.finish()
    return E.nc


def launch_l1_ffn(state, inp):
    nc = build_l1_ffn()
    in_maps = [{"x": state[i], "w": inp["L1_ffn_weight"],
                "b": inp["L1_ffn_bias"]} for i in range(NCORES)]
    res = run_bass_kernel_spmd(nc, in_maps, core_ids=list(range(NCORES)))
    return [np.asarray(r["xout"]) for r in res.results]



def emit_identity(E):
    iden = E.sbuf("iden", [128, 128])

    def g1(g):
        g.iota(iden[:, :], [[1, 128]], channel_multiplier=-1)
    E.blk(gpsimd=lambda e: g1(Eng(e)))

    def v1(v):
        v.tss(iden[:, :], iden[:, :], 0.0, ALU.is_equal)
    E.blk(vector=v1)
    return iden


def build_l2_ffn():
    E = Em("l2ffn")
    x_in = E.dram("x", [TOKENS, D], kind="ExternalInput")
    rwT_in = E.dram("rwT", [1024, 8], kind="ExternalInput")
    gw_in = E.dram("gw", [8, 1024, 2048], kind="ExternalInput")
    gb_in = E.dram("gb", [8, 2048], kind="ExternalInput")
    uw_in = E.dram("uw", [8, 1024, 2048], kind="ExternalInput")
    ub_in = E.dram("ub", [8, 2048], kind="ExternalInput")
    dw_in = E.dram("dw", [8, 2048, 1024], kind="ExternalInput")
    db_in = E.dram("db", [8, 1024], kind="ExternalInput")
    x_out = E.dram("xout", [TOKENS, D], kind="ExternalOutput")

    NTT = TOKENS // 128
    Q = 1024
    NQ = TOKENS // Q
    QT = Q // 128

    xs = E.sbuf("xs", [128, 4, D])
    lnT = E.sbuf("lnT", [128, 8, 1024], BF16)
    yacc = E.sbuf("yacc", [128, 8, D])
    h_bf = E.sbuf("hbf", [128, 16, 1024], BF16)
    gw_sl = E.sbuf("gwsl", [128, 8, 1024], BF16)
    uw_sl = E.sbuf("uwsl", [128, 8, 1024], BF16)
    dw_sl = E.sbuf("dwsl", [128, 16, 512], BF16)
    silu_g = E.sbuf("silug", [128, 4, 512])
    wnorm = E.sbuf("wnorm", [128, 8, 8])
    probs = E.sbuf("probs", [128, 8])
    pcol = E.sbuf("pcol", [128, 6])
    pmask = E.sbuf("pmask", [128, 8])
    pm2 = E.sbuf("pm2", [128, 8])
    grow = E.sbuf("grow", [1, 2048], BF16)
    urow = E.sbuf("urow", [1, 2048], BF16)
    drow = E.sbuf("drow", [1, 1024], BF16)
    ones_row = E.sbuf("onesrow", [1, 512], BF16)
    rwT = E.sbuf("rwTsb", [128, 8, 8])
    lnT32 = E.sbuf("lnT32", [128, 8, 128])
    st = alloc_ln_stats(E, 4)
    iden = emit_identity(E)
    pads = [E.psum(f"p{i}", [128, 512]) for i in range(8)]
    prt = pads[7]

    def f0(v):
        v.memset(ones_row[:, :], 1.0)
    E.blk(vector=f0)

    def frw(sy):
        for k in range(8):
            E.dma(sy, rwT[:, k, :], rwT_in[k * 128:(k + 1) * 128, :])
        E.dwait(sy)
    E.blk(sync=frw)

    Q = 1024
    NQ = TOKENS // Q
    QT = Q // 128

    for q in range(NQ):
        for grp in range(2):
            tt0 = q * QT + grp * 4

            def fx(sy, tt0=tt0):
                for j in range(4):
                    E.dma(sy, xs[:, j, :], x_in[(tt0 + j) * 128:(tt0 + j + 1) * 128, :])
                E.dwait(sy)
            E.blk(sync=fx)

            emit_ln(E, xs, 4, st, xs)

            for j in range(4):
                lt = grp * 4 + j

                def ftr2(t, j=j):
                    for c in range(8):
                        t.transpose(pads[c][:, 0:128], xs[:, j, c * 128:(c + 1) * 128],
                                    iden[:, :])
                E.blk(tensor=ftr2)

                def fte(v, lt=lt):
                    for c in range(8):
                        v.copy(lnT[:, c, lt * 128:(lt + 1) * 128], pads[c][:, 0:128],
                               nodep=True)
                        v.copy(lnT32[:, c, :], pads[c][:, 0:128], nodep=True)
                E.blk(vector=fte)

                def frm(t, lt=lt):
                    for k in range(8):
                        t.mm(prt[:, 0:8], lnT32[:, k, :], rwT[:, k, :],
                             start=(k == 0), stop=(k == 7))
                E.blk(tensor=frm)

                def fsm(s):
                    s.act(probs[:, :], prt[:, 0:8], AF.Exp, accum_out=pcol[:, 0:1])
                E.blk(scalar=fsm)

                def fv1(v, lt=lt):
                    v.recip(pcol[:, 1:2], pcol[:, 0:1])
                    v.ts(probs[:, :], probs[:, :], pcol[:, 1:2], None, ALU.mult)
                    v.red(pcol[:, 2:3].unsqueeze(1), probs[:, :].unsqueeze(1), ALU.max)
                    v.tss(pmask[:, :], probs[:, :], pcol[:, 2:3], ALU.is_ge)
                    v.stt(pm2[:, :], pmask[:, :], -1e30, probs[:, :], ALU.mult, ALU.add)
                    v.red(pcol[:, 3:4].unsqueeze(1), pm2[:, :].unsqueeze(1), ALU.max)
                    v.tss(pm2[:, :], pm2[:, :], pcol[:, 3:4], ALU.is_ge)
                    v.tt(pmask[:, :], pmask[:, :], pm2[:, :], ALU.add)
                    v.tt(probs[:, :], probs[:, :], pmask[:, :], ALU.mult)
                    v.red(pcol[:, 4:5].unsqueeze(1), probs[:, :].unsqueeze(1), ALU.add)
                    v.ts(pcol[:, 4:5], pcol[:, 4:5], 1.0, 1e-9, ALU.mult, ALU.add)
                    v.recip(pcol[:, 5:6], pcol[:, 4:5])
                    v.ts(wnorm[:, lt, :], probs[:, :], pcol[:, 5:6], None, ALU.mult)
                E.blk(vector=fv1)

        def fz(v):
            v.memset(yacc[:, :, :], 0.0)
        E.blk(vector=fz)

        for e in range(8):
            def frows(g, e=e):
                E.dma(g, grow[:, :], gb_in[e:e + 1, :])
                E.dma(g, urow[:, :], ub_in[e:e + 1, :])
                E.dma(g, drow[:, :], db_in[e:e + 1, :])
                E.dwait(g)
            E.blk(gpsimd=frows)

            for fh in range(2):
                def fgw(g, e=e, fh=fh):
                    for k in range(8):
                        E.dma(g, gw_sl[:, k, :],
                              gw_in[e, k * 128:(k + 1) * 128,
                                    fh * 1024:(fh + 1) * 1024])
                        E.dma(g, uw_sl[:, k, :],
                              uw_in[e, k * 128:(k + 1) * 128,
                                    fh * 1024:(fh + 1) * 1024])
                    E.dwait(g)
                E.blk(gpsimd=fgw)

                for fgrp in range(4):
                    fs = [fgrp * 2, fgrp * 2 + 1]

                    def fmm(t, fs=fs, fh=fh):
                        for fi, f_in in enumerate(fs):
                            fg_off = f_in * 128
                            fglob = fh * 1024 + fg_off
                            for th in range(2):
                                tban = fi * 4 + th * 2
                                pg, pu = pads[tban], pads[tban + 1]
                                for k in range(8):
                                    t.mm(pg[:, :], gw_sl[:, k, fg_off:fg_off + 128],
                                         lnT[:, k, th * 512:(th + 1) * 512],
                                         start=(k == 0), stop=False)
                                    t.mm(pu[:, :], uw_sl[:, k, fg_off:fg_off + 128],
                                         lnT[:, k, th * 512:(th + 1) * 512],
                                         start=(k == 0), stop=False)
                                t.mm(pg[:, :], grow[:, fglob:fglob + 128],
                                     ones_row[:, :], start=False, stop=True)
                                t.mm(pu[:, :], urow[:, fglob:fglob + 128],
                                     ones_row[:, :], start=False, stop=True)
                    E.blk(tensor=fmm)

                    def fsl(s, fs=fs):
                        for fi in range(2):
                            for th in range(2):
                                s.act(silu_g[:, fi * 2 + th, :],
                                      pads[fi * 4 + th * 2][:, :], AF.Silu,
                                      nodep=True)
                    E.blk(scalar=fsl)

                    def fhm(v, fs=fs, fh=fh):
                        for fi, f_in in enumerate(fs):
                            fglob_c = fh * 8 + f_in
                            for th in range(2):
                                v.tt(h_bf[:, fglob_c, th * 512:(th + 1) * 512],
                                     pads[fi * 4 + th * 2 + 1][:, :],
                                     silu_g[:, fi * 2 + th, :], ALU.mult, nodep=True)
                    E.blk(vector=fhm)

            for dh in range(2):
                def fdw(g, e=e, dh=dh):
                    for fk in range(16):
                        E.dma(g, dw_sl[:, fk, :],
                              dw_in[e, fk * 128:(fk + 1) * 128,
                                    dh * 512:(dh + 1) * 512])
                    E.dwait(g)
                E.blk(gpsimd=fdw)

                def fdm(t, dh=dh):
                    for fk in range(16):
                        for tt in range(QT):
                            t.mm(pads[tt][:, :], h_bf[:, fk, tt * 128:(tt + 1) * 128],
                                 dw_sl[:, fk, :], start=(fk == 0), stop=False)
                    for tt in range(QT):
                        t.mm(pads[tt][:, :], ones_row[:, 0:128],
                             drow[:, dh * 512:(dh + 1) * 512], start=False, stop=True)
                E.blk(tensor=fdm)

                def fde(v, dh=dh, e=e):
                    for tt in range(QT):
                        v.stt(yacc[:, tt, dh * 512:(dh + 1) * 512], pads[tt][:, :],
                              wnorm[:, tt, e:e + 1],
                              yacc[:, tt, dh * 512:(dh + 1) * 512],
                              ALU.mult, ALU.add, nodep=True)
                E.blk(vector=fde)

        for g4 in range(2):
            def fxr(sy, q=q, g4=g4):
                for j in range(4):
                    tt = q * QT + g4 * 4 + j
                    E.dma(sy, xs[:, j, :], x_in[tt * 128:(tt + 1) * 128, :])
                E.dwait(sy)
            E.blk(sync=fxr)

            def fadd(v, g4=g4):
                for j in range(4):
                    v.tt(xs[:, j, :], xs[:, j, :], yacc[:, g4 * 4 + j, :], ALU.add,
                         nodep=True)
            E.blk(vector=fadd)

            def fxo(sy, q=q, g4=g4):
                for j in range(4):
                    tt = q * QT + g4 * 4 + j
                    E.dma(sy, x_out[tt * 128:(tt + 1) * 128, :], xs[:, j, :])
                E.dwait(sy)
            E.blk(sync=fxo)

    E.finish()
    return E.nc


def launch_l2_ffn(state, inp):
    nc = build_l2_ffn()
    rwT = np.ascontiguousarray(inp["L2_ffn_router_w"].T)
    in_maps = [{"x": state[i], "rwT": rwT,
                "gw": inp["L2_ffn_gate_w"], "gb": inp["L2_ffn_gate_b"],
                "uw": inp["L2_ffn_up_w"], "ub": inp["L2_ffn_up_b"],
                "dw": inp["L2_ffn_down_w"], "db": inp["L2_ffn_down_b"]}
               for i in range(NCORES)]
    res = run_bass_kernel_spmd(nc, in_maps, core_ids=list(range(NCORES)))
    return [np.asarray(r["xout"]) for r in res.results]



class Sem:
    def __init__(self, E, name):
        self.h = E.ctx.enter_context(E.nc.semaphore(name))
        self.c = 0

    def inc(self, inst, by=1):
        inst.then_inc(self.h, by)
        self.c += by
        return inst

    def wait(self, eng, val=None):
        eng.wait_ge(self.h, self.c if val is None else val)


def _alibi_slopes(n):
    import math
    def pow2_slopes(m):
        start = 2 ** (-2 ** (-(math.log2(m) - 3)))
        return [start * start ** i for i in range(m)]
    if math.log2(n).is_integer():
        return pow2_slopes(n)
    c = 2 ** math.floor(math.log2(n))
    return pow2_slopes(c) + _alibi_slopes(2 * c)[0::2][:n - c]


def build_l1_attn():
    E = Em("l1attn")
    x_in = E.dram("x", [TOKENS, D], kind="ExternalInput")
    w_in = E.dram("w", [1024, 3072], kind="ExternalInput")
    b_in = E.dram("b", [3072], kind="ExternalInput")
    wo_in = E.dram("wo", [1024, 1024], kind="ExternalInput")
    bo_in = E.dram("bo", [1024], kind="ExternalInput")
    bq_in = E.dram("bq", [4, 128, 512], kind="ExternalInput")
    x_out = E.dram("xout", [TOKENS, D], kind="ExternalOutput")

    H = 16
    L = SEQ
    NT = L // 128
    slopes = _alibi_slopes(H)

    xs = E.sbuf("xs", [128, 2, D])
    ln_bf = E.sbuf("lnbf", [128, 2, D], BF16)
    lnT = E.sbuf("lnT", [128, 8, L], BF16)
    qkvT = E.sbuf("qkvT", [128, 12, L], BF16)
    outT = E.sbuf("outT", [128, 8, L], BF16)
    wstr = E.sbuf("wstr", [128, 8, 512], BF16)
    browf = E.sbuf("brow", [1, 512])
    borow = E.sbuf("borow", [1, 512], BF16)
    borow2 = E.sbuf("borow2", [1, 512], BF16)
    ones_row = E.sbuf("onesr", [1, 512], BF16)
    onesf = E.sbuf("onesf", [1, 512])
    arow = E.sbuf("arow", [1, L])
    negf = E.sbuf("negf", [1, 512])
    bq_t = E.sbuf("bqt", [128, 4, 512])
    expS = E.sbuf("expS", [128, L], BF16)
    expF = E.sbuf("expF", [128, 512])
    expST = E.sbuf("expST", [128, 16, 128], BF16)
    vtok = E.sbuf("vtok", [128, 16, 128], BF16)
    ones_col = E.sbuf("onescol", [128, 1], BF16)
    drow0 = E.sbuf("drow0", [1, 128])
    onesf64 = E.sbuf("onesf64", [1, 64])
    otmp = E.sbuf("otmp", [64, 128])
    wo_sb = E.sbuf("wosb", [128, 8, 1024], BF16)
    st = alloc_ln_stats(E, 2)
    pads = [E.psum(f"p{i}", [128, 512]) for i in range(8)]
    pav = pads[6]
    pbc = pads[7]

    s_s = Sem(E, "ss")
    s_d = Sem(E, "sd")
    s_a = Sem(E, "sa")
    s_t = Sem(E, "st2")
    s_av = Sem(E, "sav")
    s_n1 = Sem(E, "sn1")
    s_n2 = Sem(E, "sn2")
    s_nf = Sem(E, "snf")

    def finit(v):
        v.memset(ones_row[:, :], 1.0)
        v.memset(onesf[:, :], 1.0)
        v.memset(onesf64[:, :], 1.0)
        v.memset(negf[:, :], -1.0)
        v.memset(ones_col[:, :], 1.0)
    E.blk(vector=finit)



    def fld(sy):
        for c in range(4):
            E.dma(sy, bq_t[:, c, :], bq_in[c])
        E.dwait(sy)
    E.blk(sync=fld)

    def fldb(g):
        E.dma(g, borow[:, :], bo_in[0:512].unsqueeze(0))
        E.dma(g, borow2[:, :], bo_in[512:1024].unsqueeze(0))
        for c in range(8):
            E.dma(g, wo_sb[:, c, :], wo_in[c * 128:(c + 1) * 128, :])
        E.dwait(g)
    E.blk(gpsimd=fldb)

    sched = []
    for qt in range(NT):
        nw = qt // 4 + 1
        for w in range(nw):
            sched.append((qt, w, w == nw - 1))
    NWIN = len(sched)
    cumw = {}
    cw = 0
    for qt in range(NT):
        cw += qt // 4 + 1
        cumw[qt] = cw
    cumtr = {}
    ct = 0
    for qt in range(NT):
        ct += qt + 1
        cumtr[qt] = ct

    for s in range(B_PER_CORE):
        base = s * L

        for grp in range(NT // 2):
            tt0 = grp * 2

            def fx(sy, tt0=tt0):
                for j in range(2):
                    E.dma(sy, xs[:, j, :],
                          x_in[base + (tt0 + j) * 128: base + (tt0 + j + 1) * 128, :])
                E.dwait(sy)
            E.blk(sync=fx)

            emit_ln(E, xs, 2, st, ln_bf)

            def ftr(sy, tt0=tt0):
                for j in range(2):
                    for c in range(8):
                        E.dma_t(sy, lnT[:, c, (tt0 + j) * 128:(tt0 + j + 1) * 128],
                                ln_bf[:, j, c * 128:(c + 1) * 128])
                E.dwait(sy)
            E.blk(sync=ftr)

        for hg in range(2):
            for p3 in range(3):
                fbase = p3 * 1024 + hg * 512

                def fw(g, fbase=fbase):
                    for k in range(8):
                        E.dma(g, wstr[:, k, 0:512],
                              w_in[k * 128:(k + 1) * 128, fbase:fbase + 512])
                    E.dwait(g)
                E.blk(gpsimd=fw)

                def fwb(sy, fbase=fbase):
                    E.dma(sy, browf[:, 0:512], b_in[fbase:fbase + 512].unsqueeze(0))
                    E.dwait(sy)
                E.blk(sync=fwb)

                for cpair in range(2):
                    def fmm(t, cpair=cpair, p3=p3):
                        for ci in range(2):
                            cl = cpair * 2 + ci
                            for tw in range(4):
                                p = pads[ci * 4 + tw]
                                for k in range(8):
                                    t.mm(p[:, :], wstr[:, k, cl * 128:(cl + 1) * 128],
                                         lnT[:, k, tw * 512:(tw + 1) * 512],
                                         start=(k == 0), stop=False)
                                t.mm(p[:, :],
                                     browf[:, cl * 128:(cl + 1) * 128],
                                     onesf[:, :], start=False, stop=True)
                    E.blk(tensor=fmm)

                    def fev(s_, cpair=cpair, p3=p3):
                        for ci in range(2):
                            cl = cpair * 2 + ci
                            c = p3 * 4 + cl
                            sc = 0.125 if p3 == 0 else 1.0
                            for tw in range(4):
                                s_.act(qkvT[:, c, tw * 512:(tw + 1) * 512],
                                       pads[ci * 4 + tw][:, :], AF.Copy, scale=sc,
                                       nodep=True)
                    E.blk(scalar=fev)

            for hl in range(8):
                h = hg * 8 + hl
                slope = float(slopes[h])
                qc, qr = hl // 2, (hl % 2) * 64
                kc_, kr = 4 + hl // 2, (hl % 2) * 64
                vc, vr = 8 + hl // 2, (hl % 2) * 64
                oc, orow = h // 2, (h % 2) * 64

                s_s0, s_d0, s_a0 = s_s.c, s_d.c, s_a.c
                s_t0, s_av0 = s_t.c, s_av.c
                s_n10, s_n20, s_nf0 = s_n1.c, s_n2.c, s_nf.c
                diag_idx = [i for i, (_, _, d) in enumerate(sched) if d]

                def grows(e):
                    g = Eng(e)
                    g.iota(arow[:, :], [[1, L]], channel_multiplier=0)
                E.blk(gpsimd=grows)

                def vrows(e):
                    e.ts(arow[:, :], arow[:, :], -slope, None, ALU.mult)
                E.blk(vector=vrows)

                def tsec(t):
                    for i, (qt, w, diag) in enumerate(sched):
                        q0 = qt * 128
                        k0 = w * 512
                        b = pads[i % 6]
                        if i >= 6:
                            t.e.wait_ge(s_a.h, s_a0 + i - 5)
                        if diag:
                            t.mm(b[:, :], qkvT[qr:qr + 64, qc, q0:q0 + 128],
                                 qkvT[kr:kr + 64, kc_, k0:k0 + 512],
                                 start=True, stop=True).then_inc(s_s.h, 1)
                        else:
                            t.mm(b[:, :], qkvT[qr:qr + 64, qc, q0:q0 + 128],
                                 qkvT[kr:kr + 64, kc_, k0:k0 + 512],
                                 start=True, stop=False)
                            t.mm(b[:, :], arow[:, q0:q0 + 128], onesf[:, :],
                                 start=False, stop=False)
                            t.mm(b[:, :], negf[:, 0:128], arow[:, k0:k0 + 512],
                                 start=False, stop=True).then_inc(s_s.h, 1)
                        if diag:
                            t.e.wait_ge(s_t.h, s_t0 + 16 * (16 + cumtr[qt]))
                            if qt > 0:
                                t.e.wait_ge(s_nf.h, s_nf0 + qt)
                            for kk in range(qt + 1):
                                t.mm(pav[0:64, 0:128], vtok[:, kk, vr:vr + 64],
                                     expST[:, kk, :], start=(kk == 0),
                                     stop=(kk == qt))
                                mmden = t.mm(pbc[0:1, 0:128], ones_col[:, :],
                                             expST[:, kk, :], start=(kk == 0),
                                             stop=(kk == qt))
                                if kk == qt:
                                    mmden.then_inc(s_av.h, 1)
                            t.e.wait_ge(s_n1.h, s_n10 + qt + 1)
                            t.mm(pbc[0:64, 0:128], onesf64[:, :],
                                 drow0[:, :]).then_inc(s_n2.h, 1)

                def vsec(e):
                    v = e
                    nd = 0
                    for i, (qt, w, diag) in enumerate(sched):
                        if diag:
                            if nd > 0:
                                v.e.wait_ge(s_a.h, s_a0 + prev_consumed[0])
                            v.e.wait_ge(s_s.h, s_s0 + i + 1)
                            cls = qt % 4
                            v.stt(expF[:, :], bq_t[:, cls, :], slope,
                                  pads[i % 6][:, :], ALU.mult, ALU.add)
                            v.e.drain()
                            v.e.sem_inc(s_d.h, 1)
                            nd += 1
                            prev_consumed[0] = cumw[qt]
                            v.e.wait_ge(s_av.h, s_av0 + qt + 1)
                            v.copy(drow0[:, :], pbc[0:1, 0:128])
                            v.recip(drow0[:, :], drow0[:, :])
                            v.e.drain()
                            v.e.sem_inc(s_n1.h, 1)
                            v.e.wait_ge(s_n2.h, s_n20 + qt + 1)
                            v.copy(otmp[:, :], pav[0:64, 0:128])
                            v.tt(outT[orow:orow + 64, oc, qt * 128:(qt + 1) * 128],
                                 otmp[:, :], pbc[0:64, 0:128], ALU.mult)
                            v.e.drain()
                            v.e.sem_inc(s_nf.h, 1)
                prev_consumed = [0]

                def asec(e):
                    sC = e
                    ndone = 0
                    for i, (qt, w, diag) in enumerate(sched):
                        k0 = w * 512
                        if w == 0 and qt >= 1:
                            sC.e.wait_ge(s_t.h, s_t0 + 16 * (16 + cumtr[qt - 1]))
                        if diag:
                            ndone += 1
                            sC.e.wait_ge(s_d.h, s_d0 + ndone)
                            sC.act(expS[:, k0:k0 + 512], expF[:, :], AF.Exp)
                        else:
                            sC.e.wait_ge(s_s.h, s_s0 + i + 1)
                            sC.act(expS[:, k0:k0 + 512], pads[i % 6][:, :], AF.Exp)
                        sC.e.drain()
                        sC.e.sem_inc(s_a.h, 1)

                def ssec2(sy):
                    for kk in range(16):
                        sy.dma_start(out=vtok[:, kk, :],
                                     in_=qkvT[:, vc, kk * 128:(kk + 1) * 128],
                                     transpose=True).then_inc(s_t.h, 16)
                    for qt in range(NT):
                        sy.wait_ge(s_a.h, s_a0 + cumw[qt])
                        if qt > 0:
                            sy.wait_ge(s_av.h, s_av0 + qt)
                        for kk in range(qt + 1):
                            sy.dma_start(
                                out=expST[:, kk, :],
                                in_=expS[:, kk * 128:(kk + 1) * 128],
                                transpose=True).then_inc(s_t.h, 16)

                E.blk(tensor=tsec, vector=vsec, scalar=asec, sync=ssec2)
                s_s.c += NWIN
                s_d.c += NT
                s_a.c += NWIN
                s_t.c += 16 * (16 + cumtr[NT - 1])
                s_av.c += NT
                s_n1.c += NT
                s_n2.c += NT
                s_nf.c += NT

        for tg in range(8):
            def fxr(sy, tg=tg):
                for j in range(2):
                    tt = tg * 2 + j
                    E.dma(sy, xs[:, j, :],
                          x_in[base + tt * 128: base + (tt + 1) * 128, :])
                E.dwait(sy)
            E.blk(sync=fxr)

            def fom(t, tg=tg):
                for j in range(2):
                    tt = tg * 2 + j
                    for dh in range(2):
                        p = pads[j * 2 + dh]
                        for c in range(8):
                            t.mm(p[:, :], outT[:, c, tt * 128:(tt + 1) * 128],
                                 wo_sb[:, c, dh * 512:(dh + 1) * 512],
                                 start=(c == 0), stop=False)
                        t.mm(p[:, :], ones_row[:, 0:128],
                             (borow if dh == 0 else borow2)[:, :],
                             start=False, stop=True)
            E.blk(tensor=fom)

            def fde(v, tg=tg):
                for j in range(2):
                    for dh in range(2):
                        v.tt(xs[:, j, dh * 512:(dh + 1) * 512],
                             xs[:, j, dh * 512:(dh + 1) * 512],
                             pads[j * 2 + dh][:, :], ALU.add, nodep=True)
            E.blk(vector=fde)

            def fxo(sy, tg=tg):
                for j in range(2):
                    tt = tg * 2 + j
                    E.dma(sy, x_out[base + tt * 128: base + (tt + 1) * 128, :],
                          xs[:, j, :])
                E.dwait(sy)
            E.blk(sync=fxo)

    E.finish()
    return E.nc


def launch_l1_attn(state, inp):
    nc = build_l1_attn()
    bq = np.zeros((4, 128, 512), np.float32)
    f = np.arange(512)[None, :]
    p = np.arange(128)[:, None]
    for c in range(4):
        rel = f - p - 128.0 * c
        bq[c] = np.where(rel <= 0, rel, -1e30)
    in_maps = [{"x": state[i], "w": inp["L1_attn_w_in"], "b": inp["L1_attn_b_in"],
                "wo": inp["L1_attn_w_out"], "bo": inp["L1_attn_b_out"],
                "bq": bq} for i in range(NCORES)]
    res = run_bass_kernel_spmd(nc, in_maps, core_ids=list(range(NCORES)))
    return [np.asarray(r["xout"]) for r in res.results]



def emit_mamba_chunk_scan(E, sems, delta_c, u_c, A_col, B_bc, C_bc, y_out,
                          dA2, xin2, h2, du, ytmp, NS, hinit=None, hsave=None,
                          B_src=None, C_src=None):
    s_da, s_xi, s_sc, s_y = sems["da"], sems["xi"], sems["sc"], sems["y"]
    da0, xi0, sc0, y0 = s_da.c, s_xi.c, s_sc.c, s_y.c
    stream = B_src is not None
    if stream:
        sbp = [sems["b0"], sems["b1"]]
        scp = [sems["c0"], sems["c1"]]
        sbp0 = [sbp[0].c, sbp[1].c]
        scp0 = [scp[0].c, scp[1].c]
        Bv = lambda n: B_bc[n % 2][:, :]
        Cv = lambda n: C_bc[n % 2][:, :]
    else:
        Bv, Cv = B_bc, C_bc
    T = delta_c.shape[-1]

    def asec(sC):
        for n in range(NS):
            if n >= 2:
                sC.e.wait_ge(s_sc.h, sc0 + n - 1)
            sC.act(dA2[n % 2][:, :], delta_c, AF.Exp, scale=A_col(n), nodep=True)
            sC.e.drain()
            sC.e.sem_inc(s_da.h, 1)

    def ssec(sy):
        for n in range(NS):
            if n >= 2:
                sy.wait_ge(s_xi.h, xi0 + n - 1)
                sy.wait_ge(s_y.h, y0 + n - 1)
            sy.dma_start(out=B_bc[n % 2][:, :], in_=B_src(n)).then_inc(
                sbp[n % 2].h, 16)
            sy.dma_start(out=C_bc[n % 2][:, :], in_=C_src(n)).then_inc(
                scp[n % 2].h, 16)

    def gsec(e):
        g = Eng(e)
        g.tt(du[:, :], delta_c, u_c, ALU.mult)
        g.e.drain()
        for n in range(NS):
            if n >= 2:
                e.wait_ge(s_sc.h, sc0 + n - 1)
            if stream:
                e.wait_ge(sbp[n % 2].h, sbp0[n % 2] + 16 * (n // 2 + 1))
            g.tt(xin2[n % 2][:, :], du[:, :], Bv(n), ALU.mult, nodep=True)
            e.drain()
            e.sem_inc(s_xi.h, 1)

    def vsec(v):
        for n in range(NS):
            v.e.wait_ge(s_da.h, da0 + n + 1)
            v.e.wait_ge(s_xi.h, xi0 + n + 1)
            init = 0.0 if hinit is None else hinit(n)
            v.scan(h2[n % 2][:, :], dA2[n % 2][:, :], xin2[n % 2][:, :], init)
            v.e.drain()
            v.e.sem_inc(s_sc.h, 1)
            if stream:
                v.e.wait_ge(scp[n % 2].h, scp0[n % 2] + 16 * (n // 2 + 1))
            if n == 0:
                v.tt(y_out, h2[0][:, :], Cv(0), ALU.mult)
            else:
                v.tt(ytmp[:, :], h2[n % 2][:, :], Cv(n), ALU.mult)
                v.tt(y_out, y_out, ytmp[:, :], ALU.add)
            if hsave is not None:
                v.copy(hsave(n), h2[n % 2][:, T - 1:T], nodep=True)
            v.e.drain()
            v.e.sem_inc(s_y.h, 1)

    if stream:
        E.blk(scalar=asec, vector=vsec, gpsimd=gsec, sync=ssec)
        for n in range(NS):
            sbp[n % 2].c += 16
            scp[n % 2].c += 16
    else:
        E.blk(scalar=asec, vector=vsec, gpsimd=gsec)
    s_da.c += NS
    s_xi.c += NS
    s_sc.c += NS
    s_y.c += NS



def build_l2_attn():
    E = Em("l2attn")
    x_in = E.dram("x", [TOKENS, D], kind="ExternalInput")
    ipT_in = E.dram("ipT", [1024, 4096], kind="ExternalInput")
    cw_in = E.dram("cw", [128, 64], kind="ExternalInput")
    cb_in = E.dram("cb", [128, 16], kind="ExternalInput")
    xpwT_in = E.dram("xpwT", [2048, 80], kind="ExternalInput")
    dtwT_in = E.dram("dtwT", [64, 2048], kind="ExternalInput")
    dtb_in = E.dram("dtb", [2048], kind="ExternalInput")
    alog_in = E.dram("alog", [128, 128], kind="ExternalInput")
    ut_in = E.dram("ut", [8, 2048], kind="ExternalInput")
    v_in = E.dram("vp", [8, 8], kind="ExternalInput")
    ls_in = E.dram("ls", [1, 1], kind="ExternalInput")
    dcol_in = E.dram("dcol", [128, 16], kind="ExternalInput")
    opT_in = E.dram("opT", [2048, 1024], kind="ExternalInput")
    x_out = E.dram("xout", [TOKENS, D], kind="ExternalOutput")

    NC = 16
    T = SEQ
    TH = T // 2

    xz_dram = E.dram("xzscr", [32, 128, T], BF16)
    u_dram = E.dram("uscr", [NC, 128, T], BF16)
    y_dram = E.dram("yscr", [NC, 128, T], BF16)
    bc_dram = E.dram("bcscr", [16, T], BF16)

    xs = E.sbuf("xs", [128, 2, D])
    st = alloc_ln_stats(E, 2)
    ln_bf = E.sbuf("lnbf", [128, 2, D], BF16)
    delta_c = E.sbuf("deltac", [128, TH])
    du = E.sbuf("du", [128, TH])
    dA2 = [E.sbuf(f"dA{i}", [128, TH]) for i in range(2)]
    xin2 = [E.sbuf(f"xin{i}", [128, TH]) for i in range(2)]
    h2 = [E.sbuf(f"hh{i}", [128, TH]) for i in range(2)]
    ytmp = E.sbuf("ytmp", [128, TH])
    y_c = E.sbuf("yc", [128, TH])
    hsave_t = E.sbuf("hsavet", [128, NC, 8])
    B_bc = E.sbuf("Bbc", [128, 8, TH], BF16)
    C_bc = E.sbuf("Cbc", [128, 8, TH], BF16)
    zs = E.sbuf("zs", [128, TH])
    zbf = E.sbuf("zbf", [128, TH], BF16)
    ybf = E.sbuf("ybf", [128, TH], BF16)
    u_c = E.sbuf("uc", [128, TH], BF16)
    xc_bf = E.sbuf("xcbf", [128, 3 + T], BF16)
    uacc = E.sbuf("uacc", [128, T])
    A_t = E.sbuf("At", [128, NC, 8])
    dcol_t = E.sbuf("dcolt", [128, NC])
    cw_t = E.sbuf("cwt", [128, NC * 4])
    cb_t = E.sbuf("cbt", [128, NC])
    ls_col = E.sbuf("lscol", [128, 1])
    ls_one = E.sbuf("lsone", [1, 1], BF16)
    onesb = E.sbuf("onesb", [1, 128], BF16)
    dtb_row = E.sbuf("dtbrow", [1, 2048])
    onesf = E.sbuf("onesf2", [1, 512])
    dtlr = E.sbuf("dtlr", [64, T], BF16)
    bc_t = E.sbuf("bct", [16, T], BF16)
    bcrow2 = [E.sbuf(f"bcrow{i}", [1, TH], BF16) for i in range(2)]
    dtw_sb = E.sbuf("dtwsb", [64, 2048], BF16)
    xpw_sb = E.sbuf("xpwsb", [128, 16, 80], BF16)
    ipsb = E.sbuf("ipsb", [128, 8, 256], BF16)
    zstage = E.sbuf("zstage", [128, T], BF16)
    u_sb2 = [E.sbuf(f"usb{i}", [128, T], BF16) for i in range(2)]
    utsb = E.sbuf("utsb", [8, 2048], BF16)
    alog_sb = E.sbuf("alogsb", [128, 128])
    vsb = E.sbuf("vsb", [8, 8], BF16)
    pads = [E.psum(f"p{i}", [128, 512]) for i in range(8)]
    sems = {k: Sem(E, f"sm_{k}") for k in ["da", "xi", "sc", "y", "r0", "r1",
                                           "m", "e2", "xd0", "xd1", "xm"]}

    def f0(v):
        v.memset(onesb[:, :], 1.0)
        v.memset(onesf[:, :], 1.0)
        v.memset(xc_bf[:, 0:3], 0.0)
    E.blk(vector=f0)
    E.constf(LN_EPS)

    def fld(sy):
        E.dma(sy, cw_t[:, :], cw_in)
        E.dma(sy, cb_t[:, :], cb_in)
        E.dma(sy, dcol_t[:, :], dcol_in)
        E.dma(sy, dtb_row[:, :], dtb_in.unsqueeze(0))
        E.dma(sy, alog_sb[:, :], alog_in)
        E.dwait(sy)
    E.blk(sync=fld)

    def fldg(g):
        E.dma(g, ls_one[:, :], ls_in)
        E.dma(g, utsb[:, :], ut_in)
        E.dma(g, vsb[:, :], v_in)
        E.dma(g, dtw_sb[:, :], dtwT_in)
        for c in range(16):
            E.dma(g, xpw_sb[:, c, :], xpwT_in[c * 128:(c + 1) * 128, :])
        E.dwait(g)
    E.blk(gpsimd=fldg)

    def fls(t):
        t.mm(pads[0][:, 0:1], onesb[:, :], ls_one[:, :])
    E.blk(tensor=fls)

    def flse(v):
        v.ts(ls_col[:, :], pads[0][:, 0:1], 1.0 / (8.0 ** 0.5), None, ALU.mult)
    E.blk(vector=flse)

    for cg in range(2):
        def fuv(t, cg=cg):
            for ci in range(8):
                c = cg * 8 + ci
                t.mm(pads[ci][:, 0:8], utsb[:, c * 128:(c + 1) * 128], vsb[:, :])
        E.blk(tensor=fuv)

        def fexp(sC, cg=cg):
            for ci in range(8):
                c = cg * 8 + ci
                sC.act(A_t[:, c, :], alog_sb[:, c * 8:(c + 1) * 8], AF.Exp,
                       nodep=True)
        E.blk(scalar=fexp)

        def fcomb(v, cg=cg):
            for ci in range(8):
                c = cg * 8 + ci
                v.stt(A_t[:, c, :], pads[ci][:, 0:8], ls_col[:, :],
                      A_t[:, c, :], ALU.mult, ALU.subtract, nodep=True)
        E.blk(vector=fcomb)

    for s in range(B_PER_CORE):
        base = s * T

        with E.nc.sbuf_tensor(f"lnT{s}", [128, 8, T], BF16) as lnT:
            for grp in range(8):
                tt0 = grp * 2

                def fx(sy, tt0=tt0):
                    for j in range(2):
                        E.dma(sy, xs[:, j, :],
                              x_in[base + (tt0 + j) * 128:
                                   base + (tt0 + j + 1) * 128, :])
                    E.dwait(sy)
                E.blk(sync=fx)

                emit_ln(E, xs, 2, st, ln_bf)

                def ftr(sy, tt0=tt0):
                    for j in range(2):
                        for c in range(8):
                            E.dma_t(sy, lnT[:, c, (tt0 + j) * 128:
                                            (tt0 + j + 1) * 128],
                                    ln_bf[:, j, c * 128:(c + 1) * 128])
                    E.dwait(sy)
                E.blk(sync=ftr)

            for eg in range(16):
                def fwz(g, eg=eg):
                    for k in range(8):
                        E.dma(g, ipsb[:, k, :],
                              ipT_in[k * 128:(k + 1) * 128,
                                     eg * 256:(eg + 1) * 256])
                    E.dwait(g)
                E.blk(gpsimd=fwz)

                for ci in range(2):
                    def fmm(t, eg=eg, ci=ci):
                        for tw in range(4):
                            p = pads[ci * 4 + tw]
                            for k in range(8):
                                t.mm(p[:, :], ipsb[:, k, ci * 128:(ci + 1) * 128],
                                     lnT[:, k, tw * 512:(tw + 1) * 512],
                                     start=(k == 0), stop=(k == 7))
                    E.blk(tensor=fmm)

                    def fev(sC, ci=ci):
                        for tw in range(4):
                            sC.act(zstage[:, tw * 512:(tw + 1) * 512],
                                   pads[ci * 4 + tw][:, :], AF.Copy, nodep=True)
                    E.blk(scalar=fev)

                    def fzo(sy, eg=eg, ci=ci):
                        E.dma(sy, xz_dram[eg * 2 + ci], zstage[:, :])
                        E.dwait(sy)
                    E.blk(sync=fzo)

        for c in range(NC):
            def fxc(sy, c=c):
                E.dma(sy, xc_bf[:, 3:3 + T], xz_dram[c])
                E.dwait(sy)
            E.blk(sync=fxc)

            def fcv(v, c=c):
                v.ts(uacc[:, :], xc_bf[:, 3:3 + T], cw_t[:, c * 4 + 3:c * 4 + 4],
                     None, ALU.mult)
                for k in range(3):
                    v.stt(uacc[:, :], xc_bf[:, k:k + T],
                          cw_t[:, c * 4 + k:c * 4 + k + 1], uacc[:, :],
                          ALU.mult, ALU.add)
            E.blk(vector=fcv)

            def fsu(sC, c=c):
                sC.act(zstage[:, :], uacc[:, :], AF.Silu, bias=cb_t[:, c:c + 1])
            E.blk(scalar=fsu)

            def fuo(sy, c=c):
                E.dma(sy, u_dram[c], zstage[:, :])
                E.dwait(sy)
            E.blk(sync=fuo)

        sxp = [sems["xd0"], sems["xd1"]]
        sm = sems["xm"]
        xdp0 = [sxp[0].c, sxp[1].c]
        xm0 = sm.c

        def fxd_s(sy):
            for k in range(NC):
                if k >= 2:
                    sy.wait_ge(sm.h, xm0 + k - 1)
                sy.dma_start(out=u_sb2[k % 2][:, :],
                             in_=u_dram[k]).then_inc(sxp[k % 2].h, 16)

        def fxd_t(t):
            for k in range(NC):
                t.e.wait_ge(sxp[k % 2].h, xdp0[k % 2] + 16 * (k // 2 + 1))
                for tw in range(4):
                    last = (tw == 3)
                    mm = t.mm(pads[tw][0:80, :], xpw_sb[:, k, :],
                              u_sb2[k % 2][:, tw * 512:(tw + 1) * 512],
                              start=(k == 0), stop=(k == NC - 1))
                    if last:
                        mm.then_inc(sm.h, 1)
        E.blk(sync=fxd_s, tensor=fxd_t)
        sxp[0].c += 16 * (NC // 2)
        sxp[1].c += 16 * (NC // 2)
        sm.c += NC

        def fxd_e(sC):
            for tw in range(4):
                sC.act(dtlr[:, tw * 512:(tw + 1) * 512], pads[tw][0:64, :],
                       AF.Copy, nodep=True)
        E.blk(scalar=fxd_e)

        def fxd_e2(v):
            for tw in range(4):
                v.copy(bc_t[:, tw * 512:(tw + 1) * 512], pads[tw][64:80, :],
                       nodep=True)
        E.blk(vector=fxd_e2)

        def fbco(sy):
            E.dma(sy, bc_dram, bc_t[:, :])
            E.dwait(sy)
        E.blk(sync=fbco)

        for th in range(2):
            t0 = th * TH
            srp = [sems["r0"], sems["r1"]]
            sm2, se2 = sems["m"], sems["e2"]
            rp0 = [srp[0].c, srp[1].c]
            m0, e0 = sm2.c, se2.c

            def fbc_g(g):
                for i in range(16):
                    if i >= 2:
                        g.wait_ge(se2.h, e0 + i - 1)
                    g.dma_start(out=bcrow2[i % 2][:, :],
                                in_=bc_dram[i:i + 1, t0:t0 + TH]).then_inc(
                                    srp[i % 2].h, 16)

            def fbc_t(t):
                for i in range(16):
                    t.e.wait_ge(srp[i % 2].h, rp0[i % 2] + 16 * (i // 2 + 1))
                    for w2 in range(2):
                        mm = t.mm(pads[(i % 2) * 2 + w2][:, :], onesb[:, :],
                                  bcrow2[i % 2][:, w2 * 512:(w2 + 1) * 512])
                        if w2 == 1:
                            mm.then_inc(sm2.h, 1)

            def fbc_v(v):
                for i in range(16):
                    v.e.wait_ge(sm2.h, m0 + i + 1)
                    dst = B_bc if i < 8 else C_bc
                    for w2 in range(2):
                        v.copy(dst[:, i % 8, w2 * 512:(w2 + 1) * 512],
                               pads[(i % 2) * 2 + w2][:, :], nodep=True)
                    v.e.drain()
                    v.e.sem_inc(se2.h, 1)
            E.blk(gpsimd=fbc_g, tensor=fbc_t, vector=fbc_v)
            srp[0].c += 128
            srp[1].c += 128
            sm2.c += 16
            se2.c += 16

            for c in range(NC):
                def fzu(sy, c=c, t0=t0):
                    E.dma(sy, u_c[:, :], u_dram[c][:, t0:t0 + TH])
                    E.dma(sy, zbf[:, :], xz_dram[16 + c][:, t0:t0 + TH])
                    E.dwait(sy)
                E.blk(sync=fzu)

                def fdm(t, c=c, t0=t0):
                    for tw in range(2):
                        p = pads[4 + tw]
                        t.mm(p[:, :], dtw_sb[:, c * 128:(c + 1) * 128],
                             dtlr[:, t0 + tw * 512:t0 + (tw + 1) * 512],
                             start=True, stop=False)
                        t.mm(p[:, :], dtb_row[:, c * 128:(c + 1) * 128],
                             onesf[:, :], start=False, stop=True)
                E.blk(tensor=fdm)

                def fsp(sC, c=c):
                    for tw in range(2):
                        sC.act(delta_c[:, tw * 512:(tw + 1) * 512],
                               pads[4 + tw][:, :], AF.Exp, nodep=True)
                    sC.e.drain()
                    for tw in range(2):
                        sC.act(delta_c[:, tw * 512:(tw + 1) * 512],
                               delta_c[:, tw * 512:(tw + 1) * 512],
                               AF.Ln, bias=1.0, nodep=True)
                E.blk(scalar=fsp)

                hin = None
                hsv = None
                if th == 0:
                    hsv = lambda n, c=c: hsave_t[:, c, n:n + 1]
                else:
                    hin = lambda n, c=c: hsave_t[:, c, n:n + 1]
                emit_mamba_chunk_scan(
                    E, sems, delta_c[:, :], u_c[:, :],
                    lambda n, c=c: A_t[:, c, n:n + 1],
                    lambda n: B_bc[:, n, :], lambda n: C_bc[:, n, :],
                    y_c[:, :], dA2, xin2, h2, du, ytmp, 8,
                    hinit=hin, hsave=hsv)

                def fsz(sC, c=c):
                    sC.act(zs[:, :], zbf[:, :], AF.Silu)
                E.blk(scalar=fsz)

                def fgt(v, c=c):
                    v.stt(y_c[:, :], u_c[:, :], dcol_t[:, c:c + 1], y_c[:, :],
                          ALU.mult, ALU.add)
                    v.tt(ybf[:, :], y_c[:, :], zs[:, :], ALU.mult)
                E.blk(vector=fgt)

                def fyo(sy, c=c, t0=t0):
                    E.dma(sy, y_dram[c][:, t0:t0 + TH], ybf[:, :])
                    E.dwait(sy)
                E.blk(sync=fyo)

        with E.nc.sbuf_tensor(f"opsb{s}", [128, 16, 512], BF16) as op_sb, \
             E.nc.sbuf_tensor(f"ysb{s}", [128, 16, 256], BF16) as ysb:
          for dh in range(2):
              def fop(g, dh=dh):
                  for c in range(16):
                      E.dma(g, op_sb[:, c, :],
                            opT_in[c * 128:(c + 1) * 128,
                                   dh * 512:(dh + 1) * 512])
                  E.dwait(g)
              E.blk(gpsimd=fop)

              for tp in range(8):
                  def fyr(sy, tp=tp, dh=dh):
                      for c in range(16):
                          E.dma(sy, ysb[:, c, :],
                                y_dram[c][:, tp * 256:(tp + 1) * 256])
                      for j in range(2):
                          tt = tp * 2 + j
                          E.dma(sy, xs[:, j, dh * 512:(dh + 1) * 512],
                                x_in[base + tt * 128:base + (tt + 1) * 128,
                                     dh * 512:(dh + 1) * 512])
                      E.dwait(sy)
                  E.blk(sync=fyr)

                  def fomm(t, tp=tp):
                      for j in range(2):
                          p = pads[j]
                          for c in range(16):
                              t.mm(p[:, :], ysb[:, c, j * 128:(j + 1) * 128],
                                   op_sb[:, c, :], start=(c == 0), stop=(c == 15))
                  E.blk(tensor=fomm)

                  def foev(v, tp=tp, dh=dh):
                      for j in range(2):
                          v.tt(xs[:, j, dh * 512:(dh + 1) * 512],
                               xs[:, j, dh * 512:(dh + 1) * 512],
                               pads[j][:, :], ALU.add, nodep=True)
                  E.blk(vector=foev)

                  def foo(sy, tp=tp, dh=dh):
                      for j in range(2):
                          tt = tp * 2 + j
                          E.dma(sy, x_out[base + tt * 128:base + (tt + 1) * 128,
                                          dh * 512:(dh + 1) * 512],
                                xs[:, j, dh * 512:(dh + 1) * 512])
                      E.dwait(sy)
                  E.blk(sync=foo)

    E.finish()
    return E.nc


def launch_l2_attn(state, inp):
    nc = build_l2_attn()
    ipT = np.ascontiguousarray(inp["L2_attn_in_proj_w"].T)
    cw = np.ascontiguousarray(
        inp["L2_attn_conv_w"].reshape(16, 128, 4).transpose(1, 0, 2).reshape(128, 64))
    cb = np.ascontiguousarray(inp["L2_attn_conv_b"].reshape(16, 128).T)
    xpwT = np.ascontiguousarray(inp["L2_attn_x_proj_w"].T)
    dtwT = np.ascontiguousarray(inp["L2_attn_dt_proj_w"].T)
    alog = np.ascontiguousarray(
        inp["L2_attn_A_log"].reshape(16, 128, 8).transpose(1, 0, 2).reshape(128, 128))
    ut = np.ascontiguousarray(inp["L2_attn_U_param"].T)
    dcol = np.ascontiguousarray(inp["L2_attn_D"].reshape(16, 128).T)
    opT = np.ascontiguousarray(inp["L2_attn_out_proj_w"].T)
    ls = np.asarray(inp["L2_attn_low_rank_scale"], np.float32).reshape(1, 1)
    in_maps = [{"x": state[i], "ipT": ipT, "cw": cw, "cb": cb, "xpwT": xpwT,
                "dtwT": dtwT, "dtb": inp["L2_attn_dt_proj_b"], "alog": alog,
                "ut": ut, "vp": inp["L2_attn_V_param"], "ls": ls,
                "dcol": dcol, "opT": opT} for i in range(NCORES)]
    res = run_bass_kernel_spmd(nc, in_maps, core_ids=list(range(NCORES)))
    return [np.asarray(r["xout"]) for r in res.results]



L3_DEBUG = False
L3_LAYERS = 4

def build_l3_attn():
    E = Em("l3attn")
    x_in = E.dram("x", [TOKENS, D], kind="ExternalInput")
    ip_in = E.dram("ip", [4, 1024, 4096], kind="ExternalInput")
    cw_in = E.dram("cw", [4, 128, 64], kind="ExternalInput")
    cb_in = E.dram("cb", [4, 128, 16], kind="ExternalInput")
    xpw_in = E.dram("xpw", [4, 2048, 96], kind="ExternalInput")
    dtw_in = E.dram("dtw", [4, 64, 2048], kind="ExternalInput")
    dtb_in = E.dram("dtb", [4, 2048], kind="ExternalInput")
    alog_in = E.dram("alog", [4, 128, 256], kind="ExternalInput")
    dcol_in = E.dram("dcol", [4, 128, 16], kind="ExternalInput")
    ow_in = E.dram("ow", [4, 2048, 1024], kind="ExternalInput")
    lnw_in = E.dram("lnw", [4, 1024], kind="ExternalInput")
    lnb_in = E.dram("lnb", [4, 1024], kind="ExternalInput")
    x_out = E.dram("xout", [TOKENS, D], kind="ExternalOutput")

    NC = 16
    NS = 16
    T = SEQ
    TH = T // 2

    xi_dram = E.dram("xiscr", [TOKENS, D])
    dbg = E.dram("dbg", [5, TOKENS, D], kind="ExternalOutput") if L3_DEBUG else None
    xz_dram = E.dram("xzscr", [32, 128, T], BF16)
    u_dram = E.dram("uscr", [NC, 128, T], BF16)
    y_dram = E.dram("yscr", [NC, 128, T], BF16)
    bc_dram = E.dram("bcscr", [32, T], BF16)
    bcb_dram = E.dram("bcbscr", [32, 128, T], BF16)

    xs = E.sbuf("xs", [128, 2, D])
    st = alloc_ln_stats(E, 2)
    ln_bf = E.sbuf("lnbf", [128, 2, D], BF16)
    delta_c = E.sbuf("deltac", [128, TH])
    du = E.sbuf("du", [128, TH])
    dA2 = [E.sbuf(f"dA{i}", [128, TH]) for i in range(2)]
    xin2 = [E.sbuf(f"xin{i}", [128, TH]) for i in range(2)]
    h2 = [E.sbuf(f"hh{i}", [128, TH]) for i in range(2)]
    ytmp = E.sbuf("ytmp", [128, TH])
    y_c = E.sbuf("yc", [128, TH])
    hsave_t = E.sbuf("hsavet", [128, NC, NS])
    Bpl = [E.sbuf(f"Bpl{i}", [128, TH], BF16) for i in range(2)]
    Cpl = [E.sbuf(f"Cpl{i}", [128, TH], BF16) for i in range(2)]
    pstage = [E.sbuf(f"pst{i}", [128, T], BF16) for i in range(2)]
    zs = E.sbuf("zs", [128, TH])
    zbf = E.sbuf("zbf", [128, TH], BF16)
    ybf = E.sbuf("ybf", [128, TH], BF16)
    u_c = E.sbuf("uc", [128, TH], BF16)
    xc_bf = E.sbuf("xcbf", [128, 3 + T], BF16)
    uacc = E.sbuf("uacc", [128, TH])
    A_t = E.sbuf("At", [128, NC, NS])
    dcol_t = E.sbuf("dcolt", [128, NC])
    cw_t = E.sbuf("cwt", [128, NC * 4])
    cb_t = E.sbuf("cbt", [128, NC])
    onesb = E.sbuf("onesb", [1, 128], BF16)
    dtb_row = E.sbuf("dtbrow", [1, 2048])
    onesf = E.sbuf("onesf2", [1, 512])
    dtlr = E.sbuf("dtlr", [64, T], BF16)
    bc_t = E.sbuf("bct", [32, T], BF16)
    bcrow2 = [E.sbuf(f"bcrow{i}", [1, T], BF16) for i in range(2)]
    dtw_sb = E.sbuf("dtwsb", [64, 2048], BF16)
    xpw_sb = E.sbuf("xpwsb", [128, 16, 96], BF16)
    ipsb = E.sbuf("ipsb", [128, 8, 256], BF16)
    zstage = E.sbuf("zstage", [128, T], BF16)
    u_sb2 = [E.sbuf(f"usb{i}", [128, T], BF16) for i in range(2)]
    alog_sb = E.sbuf("alogsb", [128, 256])
    lnw_bc = E.sbuf("lnwbc", [128, D])
    lnb_bc = E.sbuf("lnbbc", [128, D])
    lnwrow = E.sbuf("lnwrow", [1, D], BF16)
    pads = [E.psum(f"p{i}", [128, 512]) for i in range(8)]
    sems = {k: Sem(E, f"sm_{k}") for k in ["da", "xi", "sc", "y", "r0", "r1",
                                           "m", "e2", "xd0", "xd1", "xm",
                                           "b0", "b1", "c0", "c1", "p0", "p1"]}

    def f0(v):
        v.memset(onesb[:, :], 1.0)
        v.memset(onesf[:, :], 1.0)
        v.memset(xc_bf[:, 0:3], 0.0)
    E.blk(vector=f0)
    E.constf(LN_EPS)

    for s in range(B_PER_CORE):
        base = s * T
        for grp in range(8):
            tt0 = grp * 2

            def fx(sy, tt0=tt0, base=base):
                for j in range(2):
                    E.dma(sy, xs[:, j, :],
                          x_in[base + (tt0 + j) * 128:
                               base + (tt0 + j + 1) * 128, :])
                E.dwait(sy)
            E.blk(sync=fx)

            emit_ln(E, xs, 2, st, xs)

            def fxo(sy, tt0=tt0, base=base):
                for j in range(2):
                    E.dma(sy, xi_dram[base + (tt0 + j) * 128:
                                      base + (tt0 + j + 1) * 128, :],
                          xs[:, j, :])
                    if L3_DEBUG:
                        E.dma(sy, dbg[0, base + (tt0 + j) * 128:
                                      base + (tt0 + j + 1) * 128, :],
                              xs[:, j, :])
                E.dwait(sy)
            E.blk(sync=fxo)

    for layer in range(L3_LAYERS):
        def fld(sy, layer=layer):
            E.dma(sy, cw_t[:, :], cw_in[layer])
            E.dma(sy, cb_t[:, :], cb_in[layer])
            E.dma(sy, dcol_t[:, :], dcol_in[layer])
            E.dma(sy, dtb_row[:, :], dtb_in[layer:layer + 1, :])
            E.dma(sy, alog_sb[:, :], alog_in[layer])
            E.dwait(sy)
        E.blk(sync=fld)

        def fldg(g, layer=layer):
            E.dma(g, dtw_sb[:, :], dtw_in[layer])
            for c in range(16):
                E.dma(g, xpw_sb[:, c, :], xpw_in[layer, c * 128:(c + 1) * 128, :])
            E.dma(g, lnwrow[:, :], lnw_in[layer:layer + 1, :])
            E.dwait(g)
        E.blk(gpsimd=fldg)

        def fA(sC):
            for c in range(NC):
                sC.act(A_t[:, c, :], alog_sb[:, c * NS:(c + 1) * NS], AF.Exp,
                       nodep=True)
        E.blk(scalar=fA)

        def fA2(v):
            v.ts(A_t[:, :, :], A_t[:, :, :], -1.0, None, ALU.mult)
        E.blk(vector=fA2)

        def flw(t):
            for dh in range(2):
                t.mm(pads[dh][:, :], onesb[:, :],
                     lnwrow[:, dh * 512:(dh + 1) * 512])
        E.blk(tensor=flw)

        def flwe(v):
            for dh in range(2):
                v.copy(lnw_bc[:, dh * 512:(dh + 1) * 512], pads[dh][:, :],
                       nodep=True)
        E.blk(vector=flwe)

        def fldb(g, layer=layer):
            E.dma(g, lnwrow[:, :], lnb_in[layer:layer + 1, :])
            E.dwait(g)
        E.blk(gpsimd=fldb)

        def flb(t):
            for dh in range(2):
                t.mm(pads[dh][:, :], onesb[:, :],
                     lnwrow[:, dh * 512:(dh + 1) * 512])
        E.blk(tensor=flb)

        def flbe(v):
            for dh in range(2):
                v.copy(lnb_bc[:, dh * 512:(dh + 1) * 512], pads[dh][:, :],
                       nodep=True)
        E.blk(vector=flbe)

        for s in range(B_PER_CORE):
            base = s * T

            with E.nc.sbuf_tensor(f"lnT{layer}_{s}", [128, 8, T], BF16) as lnT:
                for grp in range(8):
                    tt0 = grp * 2

                    def fx(sy, tt0=tt0, base=base):
                        for j in range(2):
                            E.dma(sy, xs[:, j, :],
                                  xi_dram[base + (tt0 + j) * 128:
                                          base + (tt0 + j + 1) * 128, :])
                        E.dwait(sy)
                    E.blk(sync=fx)

                    def fcv2(v):
                        for j in range(2):
                            v.copy(ln_bf[:, j, :], xs[:, j, :], nodep=True)
                    E.blk(vector=fcv2)

                    def ftr(sy, tt0=tt0):
                        for j in range(2):
                            for c in range(8):
                                E.dma_t(sy, lnT[:, c, (tt0 + j) * 128:
                                                (tt0 + j + 1) * 128],
                                        ln_bf[:, j, c * 128:(c + 1) * 128])
                        E.dwait(sy)
                    E.blk(sync=ftr)

                for eg in range(16):
                    def fwz(g, eg=eg, layer=layer):
                        for k in range(8):
                            E.dma(g, ipsb[:, k, :],
                                  ip_in[layer, k * 128:(k + 1) * 128,
                                        eg * 256:(eg + 1) * 256])
                        E.dwait(g)
                    E.blk(gpsimd=fwz)

                    for ci in range(2):
                        def fmm(t, eg=eg, ci=ci):
                            for tw in range(4):
                                p = pads[ci * 4 + tw]
                                for k in range(8):
                                    t.mm(p[:, :],
                                         ipsb[:, k, ci * 128:(ci + 1) * 128],
                                         lnT[:, k, tw * 512:(tw + 1) * 512],
                                         start=(k == 0), stop=(k == 7))
                        E.blk(tensor=fmm)

                        def fev(sC, ci=ci):
                            for tw in range(4):
                                sC.act(zstage[:, tw * 512:(tw + 1) * 512],
                                       pads[ci * 4 + tw][:, :], AF.Copy,
                                       nodep=True)
                        E.blk(scalar=fev)

                        def fzo(sy, eg=eg, ci=ci):
                            E.dma(sy, xz_dram[eg * 2 + ci], zstage[:, :])
                            E.dwait(sy)
                        E.blk(sync=fzo)


            for c in range(NC):
                def fxc(sy, c=c):
                    E.dma(sy, xc_bf[:, 3:3 + T], xz_dram[c])
                    E.dwait(sy)
                E.blk(sync=fxc)

                for th2 in range(2):
                    o = th2 * TH

                    def fcv(v, c=c, o=o):
                        v.ts(uacc[:, :], xc_bf[:, 3 + o:3 + o + TH],
                             cw_t[:, c * 4 + 3:c * 4 + 4], None, ALU.mult)
                        for k in range(3):
                            v.stt(uacc[:, :], xc_bf[:, k + o:k + o + TH],
                                  cw_t[:, c * 4 + k:c * 4 + k + 1], uacc[:, :],
                                  ALU.mult, ALU.add)
                    E.blk(vector=fcv)

                    def fsu(sC, c=c, o=o):
                        sC.act(zstage[:, o:o + TH], uacc[:, :], AF.Silu,
                               bias=cb_t[:, c:c + 1])
                    E.blk(scalar=fsu)

                def fuo(sy, c=c):
                    E.dma(sy, u_dram[c], zstage[:, :])
                    E.dwait(sy)
                E.blk(sync=fuo)

            sxp = [sems["xd0"], sems["xd1"]]
            sm = sems["xm"]
            xdp0 = [sxp[0].c, sxp[1].c]
            xm0 = sm.c

            def fxd_s(sy):
                for k in range(NC):
                    if k >= 2:
                        sy.wait_ge(sm.h, xm0 + k - 1)
                    sy.dma_start(out=u_sb2[k % 2][:, :],
                                 in_=u_dram[k]).then_inc(sxp[k % 2].h, 16)

            def fxd_t(t):
                for k in range(NC):
                    t.e.wait_ge(sxp[k % 2].h, xdp0[k % 2] + 16 * (k // 2 + 1))
                    for tw in range(4):
                        mm = t.mm(pads[tw][0:96, :], xpw_sb[:, k, :],
                                  u_sb2[k % 2][:, tw * 512:(tw + 1) * 512],
                                  start=(k == 0), stop=(k == NC - 1))
                        if tw == 3:
                            mm.then_inc(sm.h, 1)
            E.blk(sync=fxd_s, tensor=fxd_t)
            sxp[0].c += 16 * (NC // 2)
            sxp[1].c += 16 * (NC // 2)
            sm.c += NC

            def fxd_e(sC):
                for tw in range(4):
                    sC.act(dtlr[:, tw * 512:(tw + 1) * 512], pads[tw][0:64, :],
                           AF.Copy, nodep=True)
            E.blk(scalar=fxd_e)

            def fxd_e2(v):
                for tw in range(4):
                    v.copy(bc_t[:, tw * 512:(tw + 1) * 512], pads[tw][64:96, :],
                           nodep=True)
            E.blk(vector=fxd_e2)

            def fbco(sy):
                E.dma(sy, bc_dram, bc_t[:, :])
                E.dwait(sy)
            E.blk(sync=fbco)

            srp = [sems["r0"], sems["r1"]]
            sm2, se2 = sems["m"], sems["e2"]
            spp = [sems["p0"], sems["p1"]]
            rp0 = [srp[0].c, srp[1].c]
            m0, e0 = sm2.c, se2.c
            pp0 = [spp[0].c, spp[1].c]

            def fbc_g(g):
                for i in range(32):
                    if i >= 2:
                        g.wait_ge(se2.h, e0 + i - 1)
                    g.dma_start(out=bcrow2[i % 2][:, :],
                                in_=bc_dram[i:i + 1, :]).then_inc(
                                    srp[i % 2].h, 16)

            def fbc_t(t):
                for i in range(32):
                    t.e.wait_ge(srp[i % 2].h, rp0[i % 2] + 16 * (i // 2 + 1))
                    if i >= 2:
                        t.e.wait_ge(se2.h, e0 + i - 1)
                    for tw in range(4):
                        mm = t.mm(pads[(i % 2) * 4 + tw][:, :], onesb[:, :],
                                  bcrow2[i % 2][:, tw * 512:(tw + 1) * 512])
                        if tw == 3:
                            mm.then_inc(sm2.h, 1)

            def fbc_v(v):
                for i in range(32):
                    v.e.wait_ge(sm2.h, m0 + i + 1)
                    if i >= 2:
                        v.e.wait_ge(spp[i % 2].h, pp0[i % 2] + 16 * (i // 2))
                    for tw in range(4):
                        v.copy(pstage[i % 2][:, tw * 512:(tw + 1) * 512],
                               pads[(i % 2) * 4 + tw][:, :], nodep=True)
                    v.e.drain()
                    v.e.sem_inc(se2.h, 1)

            def fbc_s(sy):
                for i in range(32):
                    sy.wait_ge(se2.h, e0 + i + 1)
                    sy.dma_start(out=bcb_dram[i], in_=pstage[i % 2][:, :]
                                 ).then_inc(spp[i % 2].h, 16)
                sy.wait_ge(spp[0].h, pp0[0] + 256)
                sy.wait_ge(spp[1].h, pp0[1] + 256)
            E.blk(gpsimd=fbc_g, tensor=fbc_t, vector=fbc_v, sync=fbc_s)
            srp[0].c += 256
            srp[1].c += 256
            sm2.c += 32
            se2.c += 32
            spp[0].c += 256
            spp[1].c += 256

            for th in range(2):
                t0 = th * TH
                for c in range(NC):
                    def fzu(sy, c=c, t0=t0):
                        E.dma(sy, u_c[:, :], u_dram[c][:, t0:t0 + TH])
                        E.dma(sy, zbf[:, :], xz_dram[16 + c][:, t0:t0 + TH])
                        E.dwait(sy)
                    E.blk(sync=fzu)

                    def fdm(t, c=c, t0=t0):
                        for tw in range(2):
                            p = pads[4 + tw]
                            t.mm(p[:, :], dtw_sb[:, c * 128:(c + 1) * 128],
                                 dtlr[:, t0 + tw * 512:t0 + (tw + 1) * 512],
                                 start=True, stop=False)
                            t.mm(p[:, :], dtb_row[:, c * 128:(c + 1) * 128],
                                 onesf[:, :], start=False, stop=True)
                    E.blk(tensor=fdm)

                    def fsp(sC, c=c):
                        for tw in range(2):
                            sC.act(delta_c[:, tw * 512:(tw + 1) * 512],
                                   pads[4 + tw][:, :], AF.Exp, nodep=True)
                        sC.e.drain()
                        for tw in range(2):
                            sC.act(delta_c[:, tw * 512:(tw + 1) * 512],
                                   delta_c[:, tw * 512:(tw + 1) * 512],
                                   AF.Ln, bias=1.0, nodep=True)
                    E.blk(scalar=fsp)

                    hin = None
                    hsv = None
                    if th == 0:
                        hsv = lambda n, c=c: hsave_t[:, c, n:n + 1]
                    else:
                        hin = lambda n, c=c: hsave_t[:, c, n:n + 1]
                    emit_mamba_chunk_scan(
                        E, sems, delta_c[:, :], u_c[:, :],
                        lambda n, c=c: A_t[:, c, n:n + 1],
                        Bpl, Cpl, y_c[:, :], dA2, xin2, h2, du, ytmp, NS,
                        hinit=hin, hsave=hsv,
                        B_src=lambda n, t0=t0: bcb_dram[n][:, t0:t0 + TH],
                        C_src=lambda n, t0=t0: bcb_dram[16 + n][:, t0:t0 + TH])

                    def fsz(sC, c=c):
                        sC.act(zs[:, :], zbf[:, :], AF.Silu)
                    E.blk(scalar=fsz)

                    def fgt(v, c=c):
                        v.stt(y_c[:, :], u_c[:, :], dcol_t[:, c:c + 1], y_c[:, :],
                              ALU.mult, ALU.add)
                        v.tt(ybf[:, :], y_c[:, :], zs[:, :], ALU.mult)
                    E.blk(vector=fgt)

                    def fyo(sy, c=c, t0=t0):
                        E.dma(sy, y_dram[c][:, t0:t0 + TH], ybf[:, :])
                        E.dwait(sy)
                    E.blk(sync=fyo)

            with E.nc.sbuf_tensor(f"owsb{layer}{s}", [128, 16, 1024], BF16) as ow_sb, \
                 E.nc.sbuf_tensor(f"ysb{layer}{s}", [128, 16, 256], BF16) as ysb:
                def fop(g, layer=layer):
                    for c in range(16):
                        E.dma(g, ow_sb[:, c, :],
                              ow_in[layer, c * 128:(c + 1) * 128, :])
                    E.dwait(g)
                E.blk(gpsimd=fop)

                for tp in range(8):
                    def fyr(sy, tp=tp, base=base):
                        for c in range(16):
                            E.dma(sy, ysb[:, c, :],
                                  y_dram[c][:, tp * 256:(tp + 1) * 256])
                        for j in range(2):
                            tt = tp * 2 + j
                            E.dma(sy, xs[:, j, :],
                                  xi_dram[base + tt * 128:
                                          base + (tt + 1) * 128, :])
                        E.dwait(sy)
                    E.blk(sync=fyr)

                    def fomm(t, tp=tp):
                        for j in range(2):
                            for dh in range(2):
                                p = pads[4 + j * 2 + dh]
                                for c in range(16):
                                    t.mm(p[:, :],
                                         ysb[:, c, j * 128:(j + 1) * 128],
                                         ow_sb[:, c, dh * 512:(dh + 1) * 512],
                                         start=(c == 0), stop=(c == 15))
                    E.blk(tensor=fomm)

                    def foev(v, tp=tp):
                        for j in range(2):
                            for dh in range(2):
                                v.tt(xs[:, j, dh * 512:(dh + 1) * 512],
                                     xs[:, j, dh * 512:(dh + 1) * 512],
                                     pads[4 + j * 2 + dh][:, :], ALU.add,
                                     nodep=True)
                    E.blk(vector=foev)

                    emit_ln(E, xs, 2, st, xs)

                    def faf(v):
                        for j in range(2):
                            v.tt(xs[:, j, :], xs[:, j, :], lnw_bc[:, :],
                                 ALU.mult, nodep=True)
                        v.e.drain()
                        for j in range(2):
                            v.tt(xs[:, j, :], xs[:, j, :], lnb_bc[:, :],
                                 ALU.add, nodep=True)
                    E.blk(vector=faf)

                    def fxio(sy, tp=tp, base=base, layer=layer):
                        for j in range(2):
                            tt = tp * 2 + j
                            E.dma(sy, xi_dram[base + tt * 128:
                                              base + (tt + 1) * 128, :],
                                  xs[:, j, :])
                            if L3_DEBUG:
                                E.dma(sy, dbg[layer + 1, base + tt * 128:
                                              base + (tt + 1) * 128, :],
                                      xs[:, j, :])
                        E.dwait(sy)
                    E.blk(sync=fxio)

    for s in range(B_PER_CORE):
        base = s * T
        for tp in range(8):
            def ffr(sy, tp=tp, base=base):
                for j in range(2):
                    tt = tp * 2 + j
                    E.dma(sy, xs[:, j, :],
                          x_in[base + tt * 128:base + (tt + 1) * 128, :])
                xi0 = base + tp * 256
                E.dma(sy, uacc[:, :].rearrange("p (j f) -> p j f", j=1)
                      if False else uacc[:, :],
                      xi_dram[xi0:xi0 + 128, :].rearrange("p f -> p f")
                      if False else xi_dram[xi0:xi0 + 128, :])
                E.dma(sy, zs[:, :], xi_dram[xi0 + 128:xi0 + 256, :])
                E.dwait(sy)
            E.blk(sync=ffr)

            def ffa(v):
                v.tt(xs[:, 0, :], xs[:, 0, :], uacc[:, :], ALU.add, nodep=True)
                v.tt(xs[:, 1, :], xs[:, 1, :], zs[:, :], ALU.add, nodep=True)
            E.blk(vector=ffa)

            def ffo(sy, tp=tp, base=base):
                for j in range(2):
                    tt = tp * 2 + j
                    E.dma(sy, x_out[base + tt * 128:base + (tt + 1) * 128, :],
                          xs[:, j, :])
                E.dwait(sy)
            E.blk(sync=ffo)
    E.finish()
    return E.nc


def launch_l3_attn(state, inp):
    nc = build_l3_attn()
    def cols(a, w):
        L, DI = a.shape
        return np.ascontiguousarray(a.reshape(L, DI // 128, 128).transpose(0, 2, 1))
    cw = np.ascontiguousarray(
        inp["L3_attn_conv_w"].reshape(4, 16, 128, 4).transpose(0, 2, 1, 3).reshape(4, 128, 64))
    cb = np.ascontiguousarray(
        inp["L3_attn_conv_b"].reshape(4, 16, 128).transpose(0, 2, 1))
    alog = np.ascontiguousarray(
        inp["L3_attn_A_log"].reshape(4, 16, 128, 16).transpose(0, 2, 1, 3).reshape(4, 128, 256))
    dcol = np.ascontiguousarray(
        inp["L3_attn_D"].reshape(4, 16, 128).transpose(0, 2, 1))
    in_maps = [{"x": state[i], "ip": inp["L3_attn_in_proj_w"], "cw": cw,
                "cb": cb, "xpw": inp["L3_attn_x_proj_w"],
                "dtw": inp["L3_attn_dt_proj_w"], "dtb": inp["L3_attn_dt_proj_b"],
                "alog": alog, "dcol": dcol, "ow": inp["L3_attn_out_proj_w"],
                "lnw": inp["L3_attn_ln_w"], "lnb": inp["L3_attn_ln_b"]}
               for i in range(NCORES)]
    res = run_bass_kernel_spmd(nc, in_maps, core_ids=list(range(NCORES)))
    return [np.asarray(r["xout"]) for r in res.results]



def build_l3_ffn():
    E = Em("l3ffn", dma_scratch=8192)
    x_in = E.dram("x", [TOKENS, D], kind="ExternalInput")
    wqkv_in = E.dram("wqkv", [2, 1024, 1536], kind="ExternalInput")
    qs_in = E.dram("qs", [2, 128, 64], kind="ExternalInput")
    ks_in = E.dram("ks", [2, 128, 64], kind="ExternalInput")
    wo_in = E.dram("wo", [2, 512, 1024], kind="ExternalInput")
    lng_in = E.dram("lng", [2, 1024], kind="ExternalInput")
    lnbf_in = E.dram("lnbf", [2, 1024], kind="ExternalInput")
    w1a_in = E.dram("w1a", [2, 1024, 2816], kind="ExternalInput")
    w1g_in = E.dram("w1g", [2, 1024, 2816], kind="ExternalInput")
    b1a_in = E.dram("b1a", [2, 2816], kind="ExternalInput")
    b1g_in = E.dram("b1g", [2, 2816], kind="ExternalInput")
    w2p_in = E.dram("w2p", [2, 2816, 1024], kind="ExternalInput")
    b2_in = E.dram("b2", [2, 1024], kind="ExternalInput")
    rot_in = E.dram("rot", [4, 2048, 64], kind="ExternalInput")
    msk_in = E.dram("msk", [8, 128, 512], kind="ExternalInput")
    x_out = E.dram("xout", [TOKENS, D], kind="ExternalOutput")

    T = SEQ
    NT = 16
    FC = 22

    xi_dram = E.dram("xiscr", [TOKENS, D])
    h2_dram = E.dram("h2scr", [FC, 128, T], BF16)

    xs = E.sbuf("xs", [128, 2, D])
    xsh = E.sbuf("xsh", [128, 2, 512])
    st = alloc_ln_stats(E, 2)
    ln_bf = E.sbuf("lnbf2", [128, 2, D], BF16)
    rot_tmp = E.sbuf("rottmp", [128, 512])
    sqs = E.sbuf("sqs", [128, 512])
    n2 = E.sbuf("n2", [128, 16])
    rstd = E.sbuf("rstd", [128, 16])
    qT = E.sbuf("qT", [128, 4, T], BF16)
    kT = E.sbuf("kT", [128, 4, T], BF16)
    v_tm = E.sbuf("vtm", [128, NT, 512], BF16)
    o_tm = E.sbuf("otm", [128, NT, 512], BF16)
    oT = E.sbuf("oT", [128, 4, T], BF16)
    rot_t = E.sbuf("rott", [128, 4, 64], BF16)
    qs_t = E.sbuf("qst", [128, 64])
    ks_t = E.sbuf("kst", [128, 64])
    qkv_bf = E.sbuf("qkvbf", [128, 1024], BF16)
    obuf = E.sbuf("obuf", [128, 128], BF16)
    fin_t = E.sbuf("fint", [128, 1024])
    den = E.sbuf("den", [128, 2])
    onesf = E.sbuf("onesf", [1, 512])
    onesbw = E.sbuf("onesbw", [1, 512], BF16)
    onesb = E.sbuf("onesb", [1, 128], BF16)
    ones_col = E.sbuf("onescol", [128, 1], BF16)
    lnwrow = E.sbuf("lnwrow", [1, D], BF16)
    pads = [E.psum(f"p{i}", [128, 512]) for i in range(8)]

    def f0(v):
        v.memset(onesf[:, :], 1.0)
        v.memset(onesbw[:, :], 1.0)
        v.memset(onesb[:, :], 1.0)
        v.memset(ones_col[:, :], 1.0)
    E.blk(vector=f0)
    E.constf(LN_EPS)

    for s in range(B_PER_CORE):
        base = s * T
        for grp in range(8):
            tt0 = grp * 2

            def fx(sy, tt0=tt0, base=base):
                for j in range(2):
                    E.dma(sy, xs[:, j, :],
                          x_in[base + (tt0 + j) * 128:base + (tt0 + j + 1) * 128, :])
                E.dwait(sy)
            E.blk(sync=fx)
            emit_ln(E, xs, 2, st, xs)

            def fxo(sy, tt0=tt0, base=base):
                for j in range(2):
                    E.dma(sy, xi_dram[base + (tt0 + j) * 128:
                                      base + (tt0 + j + 1) * 128, :], xs[:, j, :])
                E.dwait(sy)
            E.blk(sync=fxo)

    def load_shifted(tag, base, src):
        for grp in range(8):
            tt0 = grp * 2

            def fx(sy, tt0=tt0, base=base):
                for j in range(2):
                    r0 = base + (tt0 + j) * 128
                    E.dma(sy, xs[:, j, :], src[r0:r0 + 128, :])
                    if tt0 + j == 0:
                        E.dma(sy, xsh[1:128, j, :], src[r0:r0 + 127, 512:1024])
                    else:
                        E.dma(sy, xsh[:, j, :], src[r0 - 1:r0 + 127, 512:1024])
                E.dwait(sy)
            E.blk(sync=fx)
            yield grp

    for layer in range(2):
        def flq(g, layer=layer):
            E.dma(g, qs_t[:, :], qs_in[layer])
            E.dma(g, ks_t[:, :], ks_in[layer])
            E.dwait(g)
        E.blk(gpsimd=flq)

        for s in range(B_PER_CORE):
            base = s * T

            with E.nc.sbuf_tensor(f"qT{layer}{s}", [128, 4, T], BF16) as qT, \
                 E.nc.sbuf_tensor(f"kT{layer}{s}", [128, 4, T], BF16) as kT, \
                 E.nc.sbuf_tensor(f"vtm{layer}{s}", [128, NT, 512], BF16) as v_tm, \
                 E.nc.sbuf_tensor(f"oT{layer}{s}", [128, 4, T], BF16) as oT:
              with E.nc.sbuf_tensor(f"wq{layer}{s}", [128, 8, 1536], BF16) as wq_sb, \
                   E.nc.sbuf_tensor(f"hT2{layer}{s}", [128, 8, 128], BF16) as hT2, \
                   E.nc.sbuf_tensor(f"qtm{layer}{s}", [128, 1536], F32) as qkv_tm_t:
                qkv_tm = qkv_tm_t
                def fwq(g, layer=layer):
                    for k in range(8):
                        E.dma(g, wq_sb[:, k, :],
                              wqkv_in[layer, k * 128:(k + 1) * 128, :])
                    E.dwait(g)
                E.blk(gpsimd=fwq)

                for grp in load_shifted("a", base, xi_dram):
                    tt0 = grp * 2

                    def fcv(v, tt0=tt0):
                        if tt0 == 0:
                            v.memset(xsh[0:1, 0, :], 0.0, nodep=True)
                            v.e.drain()
                        for j in range(2):
                            v.copy(ln_bf[:, j, 0:512], xs[:, j, 0:512], nodep=True)
                            v.copy(ln_bf[:, j, 512:1024], xsh[:, j, :], nodep=True)
                    E.blk(vector=fcv)

                    for j in range(2):
                        tt = tt0 + j

                        def ftr(sy, j=j):
                            for c in range(8):
                                E.dma_t(sy, hT2[:, c, :],
                                        ln_bf[:, j, c * 128:(c + 1) * 128])
                            E.dwait(sy)
                        E.blk(sync=ftr)

                        def fqm(t):
                            for fc in range(3):
                                p = pads[fc]
                                for k in range(8):
                                    t.mm(p[:, :], hT2[:, k, :],
                                         wq_sb[:, k, fc * 512:(fc + 1) * 512],
                                         start=(k == 0), stop=(k == 7))
                        E.blk(tensor=fqm)

                        def fqe(v):
                            for fc in range(3):
                                v.copy(qkv_tm[:, fc * 512:(fc + 1) * 512],
                                       pads[fc][:, :], nodep=True)
                        E.blk(vector=fqe)

                        def frt(g, tt=tt):
                            for r in range(4):
                                E.dma(g, rot_t[:, r, :],
                                      rot_in[r, tt * 128:(tt + 1) * 128, :])
                            E.dwait(g)
                        E.blk(gpsimd=frt)

                        def fn1(v):
                            for side in range(2):
                                qv = qkv_tm[:, side * 512:side * 512 + 512]\
                                    .rearrange("p (h d) -> p h d", h=8)
                                v.tt(sqs[:, :].rearrange("p (h d) -> p h d", h=8),
                                     qv, qv, ALU.mult)
                                v.e.drain()
                                v.red(n2[:, side * 8:side * 8 + 8].unsqueeze(2),
                                      sqs[:, :].rearrange("p (h d) -> p h d", h=8),
                                      ALU.add)
                                v.e.drain()
                        E.blk(vector=fn1)

                        def fs1(sC):
                            sC.act(n2[:, :], n2[:, :], AF.Sqrt)
                        E.blk(scalar=fs1)

                        def fn2(v):
                            v.ts(n2[:, :], n2[:, :], 1.0, 1e-12, ALU.mult, ALU.max)
                            v.recip(rstd[:, :], n2[:, :])
                            v.ts(rstd[:, 0:8], rstd[:, 0:8], 8.0, None, ALU.mult)
                        E.blk(vector=fn2)

                        def fn3(v):
                            for side in range(2):
                                off = side * 512
                                sct = qs_t if side == 0 else ks_t
                                qv3 = qkv_tm[:, off:off + 512].rearrange(
                                    "p (h d) -> p h d", h=8)
                                rst3 = rstd[:, side * 8:side * 8 + 8]\
                                    .unsqueeze(2).broadcast_to((128, 8, 64))
                                sc3 = sct[:, :].unsqueeze(1)\
                                    .broadcast_to((128, 8, 64))
                                v.tt(qv3, qv3, rst3, ALU.mult)
                                v.e.drain()
                                v.tt(qv3, qv3, sc3, ALU.mult)
                                v.e.drain()
                                v.copy(rot_tmp[:, :], qkv_tm[:, off:off + 512])
                                v.e.drain()
                                r3 = rot_tmp[:, :].rearrange("p (h d) -> p h d", h=8)
                                cosr = rot_t[:, side * 2, :]
                                sinr = rot_t[:, side * 2 + 1, :]
                                cl = cosr[:, 0:32].unsqueeze(1)\
                                    .broadcast_to((128, 8, 32))
                                ch = cosr[:, 32:64].unsqueeze(1)\
                                    .broadcast_to((128, 8, 32))
                                sl = sinr[:, 0:32].unsqueeze(1)\
                                    .broadcast_to((128, 8, 32))
                                sh = sinr[:, 32:64].unsqueeze(1)\
                                    .broadcast_to((128, 8, 32))
                                s3 = sqs[:, :].rearrange("p (h d) -> p h d", h=8)
                                v.tt(qv3[:, :, 0:32], r3[:, :, 0:32], cl, ALU.mult)
                                v.tt(s3[:, :, 0:32], r3[:, :, 32:64], sl, ALU.mult,
                                     nodep=True)
                                v.e.drain()
                                v.tt(qv3[:, :, 0:32], qv3[:, :, 0:32],
                                     s3[:, :, 0:32], ALU.subtract)
                                v.tt(qv3[:, :, 32:64], r3[:, :, 32:64], ch,
                                     ALU.mult, nodep=True)
                                v.tt(s3[:, :, 32:64], r3[:, :, 0:32], sh,
                                     ALU.mult, nodep=True)
                                v.e.drain()
                                v.tt(qv3[:, :, 32:64], qv3[:, :, 32:64],
                                     s3[:, :, 32:64], ALU.add)
                                v.e.drain()
                        E.blk(vector=fn3)

                        def fcb(v, tt=tt):
                            v.copy(qkv_bf[:, :], qkv_tm[:, 0:1024])
                            v.copy(v_tm[:, tt, :], qkv_tm[:, 1024:1536],
                                   nodep=True)
                        E.blk(vector=fcb)

                        def ftq(sy, tt=tt):
                            for c in range(4):
                                E.dma_t(sy, qT[:, c, tt * 128:(tt + 1) * 128],
                                        qkv_bf[:, c * 128:(c + 1) * 128])
                                E.dma_t(sy, kT[:, c, tt * 128:(tt + 1) * 128],
                                        qkv_bf[:, 512 + c * 128:
                                               512 + (c + 1) * 128])
                            E.dwait(sy)
                        E.blk(sync=ftq)

              with E.nc.sbuf_tensor(f"eS{layer}{s}", [128, 2, 1024], BF16) as expS, \
                   E.nc.sbuf_tensor(f"eT{layer}{s}", [128, 16, 128], BF16) as expST, \
                   E.nc.sbuf_tensor(f"mk{layer}{s}", [128, 8, 512], BF16) as msk_t:
                def fldm(g):
                    for c in range(8):
                        E.dma(g, msk_t[:, c, :], msk_in[c])
                    E.dwait(g)
                E.blk(gpsimd=fldm)

                for hg in range(4):
                    for qt in range(NT):
                        w = qt // 4
                        wparts = [1] if w == 0 else [0, 1]

                        def fsc(t, hg=hg, qt=qt, w=w, wparts=wparts):
                            for hi in range(2):
                                h = hg * 2 + hi
                                c, rb = h // 2, (h % 2) * 64
                                for wp in wparts:
                                    k0 = (w - 1 + wp) * 512
                                    p = pads[hi * 2 + wp]
                                    t.mm(p[:, :],
                                         qT[rb:rb + 64, c, qt * 128:(qt + 1) * 128],
                                         kT[rb:rb + 64, c, k0:k0 + 512],
                                         start=True, stop=True)
                        E.blk(tensor=fsc)

                        def fmk(v, qt=qt, wparts=wparts):
                            for hi in range(2):
                                for wp in wparts:
                                    cls = (qt % 4) + (4 if wp == 0 else 0)
                                    p = pads[hi * 2 + wp]
                                    v.tt(p[:, :], p[:, :],
                                         msk_t[:, cls, :], ALU.add, nodep=True)
                        E.blk(vector=fmk)

                        def fex(sC, wparts=wparts):
                            for hi in range(2):
                                for wp in wparts:
                                    sC.act(expS[:, hi, wp * 512:(wp + 1) * 512],
                                           pads[hi * 2 + wp][:, :],
                                           AF.Exp, nodep=True)
                        E.blk(scalar=fex)

                        def ftx(sy, wparts=wparts):
                            for hi in range(2):
                                for wp in wparts:
                                    for kc in range(4):
                                        E.dma_t(
                                            sy,
                                            expST[:, hi * 8 + wp * 4 + kc, :],
                                            expS[:, hi, wp * 512 + kc * 128:
                                                 wp * 512 + (kc + 1) * 128])
                            E.dwait(sy)
                        E.blk(sync=ftx)

                        def fav(t, hg=hg, w=w, wparts=wparts):
                            for hi in range(2):
                                h = hg * 2 + hi
                                first = True
                                for wp in wparts:
                                    for kc in range(4):
                                        ktile = (w - 1 + wp) * 4 + kc
                                        e = expST[:, hi * 8 + wp * 4 + kc, :]
                                        last = (wp == wparts[-1] and kc == 3)
                                        t.mm(pads[4][:, hi * 64:(hi + 1) * 64],
                                             e, v_tm[:, ktile,
                                                     h * 64:(h + 1) * 64],
                                             start=first, stop=last)
                                        t.mm(pads[5][:, hi:hi + 1], e,
                                             ones_col[:, :], start=first,
                                             stop=last)
                                        first = False
                        E.blk(tensor=fav)

                        def fno(v):
                            v.copy(den[:, 0:2], pads[5][:, 0:2])
                            v.e.drain()
                            v.recip(den[:, 0:2], den[:, 0:2])
                            v.e.drain()
                            for hi in range(2):
                                v.ts(obuf[:, hi * 64:(hi + 1) * 64],
                                     pads[4][:, hi * 64:(hi + 1) * 64],
                                     den[:, hi:hi + 1], None, ALU.mult,
                                     nodep=True)
                        E.blk(vector=fno)

                        def fob(sy, hg=hg, qt=qt):
                            E.dma_t(sy, oT[:, hg, qt * 128:(qt + 1) * 128],
                                    obuf[:, :])
                            E.dwait(sy)
                        E.blk(sync=fob)

              with E.nc.sbuf_tensor(f"wo{layer}{s}", [128, 4, 1024], BF16) as wosb:
                def fwol(g, layer=layer):
                    for c in range(4):
                        E.dma(g, wosb[:, c, :], wo_in[layer, c * 128:(c + 1) * 128, :])
                    E.dwait(g)
                E.blk(gpsimd=fwol)

                for tp in range(8):
                    def fxr(sy, tp=tp, base=base):
                        for j in range(2):
                            tt = tp * 2 + j
                            E.dma(sy, xs[:, j, :],
                                  xi_dram[base + tt * 128:base + (tt + 1) * 128, :])
                        E.dwait(sy)
                    E.blk(sync=fxr)

                    def fwo(t, tp=tp):
                        for j in range(2):
                            for dh in range(2):
                                p = pads[4 + j * 2 + dh]
                                for c in range(4):
                                    t.mm(p[:, :],
                                         oT[:, c, (tp * 2 + j) * 128:
                                            (tp * 2 + j + 1) * 128],
                                         wosb[:, c, dh * 512:(dh + 1) * 512],
                                         start=(c == 0), stop=(c == 3))
                    E.blk(tensor=fwo)

                    def fwe(v, tp=tp):
                        for j in range(2):
                            for dh in range(2):
                                v.tt(xs[:, j, dh * 512:(dh + 1) * 512],
                                     xs[:, j, dh * 512:(dh + 1) * 512],
                                     pads[4 + j * 2 + dh][:, :], ALU.add,
                                     nodep=True)
                    E.blk(vector=fwe)

                    def fxw(sy, tp=tp, base=base):
                        for j in range(2):
                            tt = tp * 2 + j
                            E.dma(sy, xi_dram[base + tt * 128:
                                              base + (tt + 1) * 128, :],
                                  xs[:, j, :])
                        E.dwait(sy)
                    E.blk(sync=fxw)

            with E.nc.sbuf_tensor(f"hlnT{layer}{s}", [128, 8, T], BF16) as hlnT, \
                 E.nc.sbuf_tensor(f"wsa{layer}{s}", [128, 8, 512], BF16) as wstr_a, \
                 E.nc.sbuf_tensor(f"wsg{layer}{s}", [128, 8, 512], BF16) as wstr_g, \
                 E.nc.sbuf_tensor(f"br{layer}{s}", [1, 2816], BF16) as brow, \
                 E.nc.sbuf_tensor(f"br2{layer}{s}", [1, 2816], BF16) as brow2, \
                 E.nc.sbuf_tensor(f"gl{layer}{s}", [128, 4, 512], F32) as gelu_t, \
                 E.nc.sbuf_tensor(f"h2b{layer}{s}", [128, T], BF16) as h2bf, \
                 E.nc.sbuf_tensor(f"lw{layer}{s}", [128, D], F32) as lnw_bc, \
                 E.nc.sbuf_tensor(f"lb{layer}{s}", [128, D], F32) as lnb_bc:
                def flw1(g, layer=layer):
                    E.dma(g, lnwrow[:, :], lng_in[layer:layer + 1, :])
                    E.dwait(g)
                E.blk(gpsimd=flw1)

                def flw2(t):
                    for dh in range(2):
                        t.mm(pads[dh][:, :], onesb[:, :],
                             lnwrow[:, dh * 512:(dh + 1) * 512])
                E.blk(tensor=flw2)

                def flw3(v):
                    for dh in range(2):
                        v.copy(lnw_bc[:, dh * 512:(dh + 1) * 512], pads[dh][:, :],
                               nodep=True)
                E.blk(vector=flw3)

                def flb1(g, layer=layer):
                    E.dma(g, lnwrow[:, :], lnbf_in[layer:layer + 1, :])
                    E.dwait(g)
                E.blk(gpsimd=flb1)

                def flb2(t):
                    for dh in range(2):
                        t.mm(pads[dh][:, :], onesb[:, :],
                             lnwrow[:, dh * 512:(dh + 1) * 512])
                E.blk(tensor=flb2)

                def flb3(v):
                    for dh in range(2):
                        v.copy(lnb_bc[:, dh * 512:(dh + 1) * 512], pads[dh][:, :],
                               nodep=True)
                E.blk(vector=flb3)

                def fbr(g, layer=layer):
                    E.dma(g, brow[:, :], b1a_in[layer:layer + 1, :])
                    E.dma(g, brow2[:, :], b1g_in[layer:layer + 1, :])
                    E.dwait(g)
                E.blk(gpsimd=fbr)

                for grp in load_shifted("m", base, xi_dram):
                    tt0 = grp * 2

                    def fsh(v, tt0=tt0):
                        if tt0 == 0:
                            v.memset(xsh[0:1, 0, :], 0.0, nodep=True)
                            v.e.drain()
                        for j in range(2):
                            v.copy(xs[:, j, 512:1024], xsh[:, j, :], nodep=True)
                    E.blk(vector=fsh)

                    emit_ln(E, xs, 2, st, xs)

                    def faf(v):
                        for j in range(2):
                            v.tt(xs[:, j, :], xs[:, j, :], lnw_bc[:, :],
                                 ALU.mult, nodep=True)
                        v.e.drain()
                        for j in range(2):
                            v.tt(xs[:, j, :], xs[:, j, :], lnb_bc[:, :],
                                 ALU.add, nodep=True)
                        v.e.drain()
                        for j in range(2):
                            v.copy(ln_bf[:, j, :], xs[:, j, :], nodep=True)
                    E.blk(vector=faf)

                    def ftr2(sy, tt0=tt0):
                        for j in range(2):
                            for c in range(8):
                                E.dma_t(sy, hlnT[:, c, (tt0 + j) * 128:
                                                (tt0 + j + 1) * 128],
                                        ln_bf[:, j, c * 128:(c + 1) * 128])
                        E.dwait(sy)
                    E.blk(sync=ftr2)

                for fg in range(6):
                    nfc = 4 if fg < 5 else 2
                    f0_ = fg * 4

                    def fws(g, fg=fg, nfc=nfc, layer=layer):
                        for k in range(8):
                            E.dma(g, wstr_a[:, k, 0:nfc * 128],
                                  w1a_in[layer, k * 128:(k + 1) * 128,
                                         fg * 512:fg * 512 + nfc * 128])
                            E.dma(g, wstr_g[:, k, 0:nfc * 128],
                                  w1g_in[layer, k * 128:(k + 1) * 128,
                                         fg * 512:fg * 512 + nfc * 128])
                        E.dwait(g)
                    E.blk(gpsimd=fws)

                    for fi in range(nfc):
                        fc = f0_ + fi

                        def fm1(t, fi=fi, fc=fc):
                            for tw in range(4):
                                pa, pg = pads[tw], pads[4 + tw]
                                for k in range(8):
                                    t.mm(pa[:, :],
                                         wstr_a[:, k, fi * 128:(fi + 1) * 128],
                                         hlnT[:, k, tw * 512:(tw + 1) * 512],
                                         start=(k == 0), stop=False)
                                    t.mm(pg[:, :],
                                         wstr_g[:, k, fi * 128:(fi + 1) * 128],
                                         hlnT[:, k, tw * 512:(tw + 1) * 512],
                                         start=(k == 0), stop=False)
                                t.mm(pa[:, :], brow[:, fc * 128:(fc + 1) * 128],
                                     onesbw[:, :], start=False, stop=True)
                                t.mm(pg[:, :], brow2[:, fc * 128:(fc + 1) * 128],
                                     onesbw[:, :], start=False, stop=True)
                        E.blk(tensor=fm1)

                        def fg1(sC):
                            for tw in range(4):
                                sC.act(gelu_t[:, tw, :], pads[4 + tw][:, :],
                                       AF.Gelu, nodep=True)
                        E.blk(scalar=fg1)

                        def fh2(v):
                            for tw in range(4):
                                v.tt(h2bf[:, tw * 512:(tw + 1) * 512],
                                     pads[tw][:, :], gelu_t[:, tw, :],
                                     ALU.mult, nodep=True)
                        E.blk(vector=fh2)

                        def fh2o(sy, fc=fc):
                            E.dma(sy, h2_dram[fc], h2bf[:, :])
                            E.dwait(sy)
                        E.blk(sync=fh2o)

            with E.nc.sbuf_tensor(f"w2s{layer}{s}", [128, FC, 512], BF16) as w2sb, \
                 E.nc.sbuf_tensor(f"h2s{layer}{s}", [128, FC, 256], BF16) as h2sb, \
                 E.nc.sbuf_tensor(f"b2r{layer}{s}", [1, 1024], BF16) as b2row:
                def fb2l(g, layer=layer):
                    E.dma(g, b2row[:, :], b2_in[layer:layer + 1, :])
                    E.dwait(g)
                E.blk(gpsimd=fb2l)
                if True:
                    for dh in range(2):
                        def fw2(g, dh=dh, layer=layer):
                            for c in range(FC):
                                E.dma(g, w2sb[:, c, :],
                                      w2p_in[layer, c * 128:(c + 1) * 128,
                                             dh * 512:(dh + 1) * 512])
                            E.dwait(g)
                        E.blk(gpsimd=fw2)

                        for tp in range(8):
                            def fhr(sy, tp=tp, dh=dh, base=base):
                                for c in range(FC):
                                    E.dma(sy, h2sb[:, c, :],
                                          h2_dram[c][:, tp * 256:(tp + 1) * 256])
                                for j in range(2):
                                    tt = tp * 2 + j
                                    E.dma(sy, xs[:, j, dh * 512:(dh + 1) * 512],
                                          xi_dram[base + tt * 128:
                                                  base + (tt + 1) * 128,
                                                  dh * 512:(dh + 1) * 512])
                                E.dwait(sy)
                            E.blk(sync=fhr)

                            def fm2(t, tp=tp, dh=dh):
                                for j in range(2):
                                    p = pads[4 + j]
                                    for c in range(FC):
                                        t.mm(p[:, :],
                                             h2sb[:, c, j * 128:(j + 1) * 128],
                                             w2sb[:, c, :], start=(c == 0),
                                             stop=False)
                                    t.mm(p[:, :], ones_col[0:1, 0:1].broadcast_to((1, 128))
                                         if False else onesb[:, :],
                                         b2row[:, dh * 512:(dh + 1) * 512]
                                         .rearrange("a f -> a f"),
                                         start=False, stop=True)
                            E.blk(tensor=fm2)

                            def fe2(v, tp=tp, dh=dh):
                                for j in range(2):
                                    v.tt(xs[:, j, dh * 512:(dh + 1) * 512],
                                         xs[:, j, dh * 512:(dh + 1) * 512],
                                         pads[4 + j][:, :], ALU.add, nodep=True)
                            E.blk(vector=fe2)

                            def fo2(sy, tp=tp, dh=dh, base=base):
                                for j in range(2):
                                    tt = tp * 2 + j
                                    E.dma(sy, xi_dram[base + tt * 128:
                                                      base + (tt + 1) * 128,
                                                      dh * 512:(dh + 1) * 512],
                                          xs[:, j, dh * 512:(dh + 1) * 512])
                                E.dwait(sy)
                            E.blk(sync=fo2)

    for s in range(B_PER_CORE):
        base = s * T
        for tp in range(8):
            def ffr(sy, tp=tp, base=base):
                for j in range(2):
                    tt = tp * 2 + j
                    E.dma(sy, xs[:, j, :],
                          x_in[base + tt * 128:base + (tt + 1) * 128, :])
                xi0 = base + tp * 256
                E.dma(sy, fin_t[:, :], xi_dram[xi0:xi0 + 128, :])
                E.dma(sy, rot_tmp[:, :], xi_dram[xi0 + 128:xi0 + 256, 0:512])
                E.dma(sy, sqs[:, :], xi_dram[xi0 + 128:xi0 + 256, 512:1024])
                E.dwait(sy)
            E.blk(sync=ffr)

            def ffa(v):
                v.tt(xs[:, 0, :], xs[:, 0, :], fin_t[:, :], ALU.add, nodep=True)
                v.tt(xs[:, 1, 0:512], xs[:, 1, 0:512], rot_tmp[:, :], ALU.add,
                     nodep=True)
                v.tt(xs[:, 1, 512:1024], xs[:, 1, 512:1024], sqs[:, :], ALU.add,
                     nodep=True)
            E.blk(vector=ffa)

            def ffo(sy, tp=tp, base=base):
                for j in range(2):
                    tt = tp * 2 + j
                    E.dma(sy, x_out[base + tt * 128:base + (tt + 1) * 128, :],
                          xs[:, j, :])
                E.dwait(sy)
            E.blk(sync=ffo)
    E.finish()
    return E.nc


def launch_l3_ffn(state, inp):
    nc = build_l3_ffn()
    HID = 2730
    w1 = inp["L3_ffn_w1"]
    w1a = np.zeros((2, 1024, 2816), np.float32)
    w1g = np.zeros((2, 1024, 2816), np.float32)
    w1a[:, :, 0:HID] = w1[:, :, 0:HID]
    w1g[:, :, 0:HID] = w1[:, :, HID:2 * HID]
    b1 = inp["L3_ffn_b1"]
    b1a = np.zeros((2, 2816), np.float32)
    b1g = np.zeros((2, 2816), np.float32)
    b1a[:, 0:HID] = b1[:, 0:HID]
    b1g[:, 0:HID] = b1[:, HID:2 * HID]
    w2p = np.zeros((2, 2816, 1024), np.float32)
    w2p[:, 0:HID, :] = inp["L3_ffn_w2"]
    qs = np.ascontiguousarray(np.broadcast_to(
        inp["L3_ffn_q_scale"][:, None, :], (2, 128, 64)))
    ks = np.ascontiguousarray(np.broadcast_to(
        inp["L3_ffn_k_scale"][:, None, :], (2, 128, 64)))
    DIM_HEAD, WINDOW, N = 64, 512, 2048
    half = np.arange(0, DIM_HEAD, 2, dtype=np.float64)
    inv_freq = 1.0 / 10000.0 ** (half / DIM_HEAD)
    t = np.arange(N, dtype=np.float64)
    freqs = t[:, None] * inv_freq[None, :]
    freqs = np.concatenate([freqs, freqs], -1)
    bse = (half + 0.4 * DIM_HEAD) / (1.4 * DIM_HEAD)
    power = (t - N // 2) / (WINDOW // 2)
    scale = bse[None, :] ** power[:, None]
    scale = np.concatenate([scale, scale], -1)
    cos, sin = np.cos(freqs), np.sin(freqs)
    rot = np.stack([cos * scale, sin * scale, cos / scale, sin / scale]).astype(np.float32)
    msk = np.zeros((8, 128, 512), np.float32)
    p = np.arange(128)[:, None]
    f = np.arange(512)[None, :]
    for c in range(8):
        offp = 128 * c if c < 4 else 512 + 128 * (c - 4)
        d = (p - f) + offp
        msk[c] = np.where((d >= 0) & (d <= 512), 0.0, -1e30)
    in_maps = [{"x": state[i], "wqkv": inp["L3_ffn_wqkv"], "qs": qs, "ks": ks,
                "wo": inp["L3_ffn_wo"], "lng": inp["L3_ffn_ln_g"],
                "lnbf": inp["L3_ffn_ln_b"], "w1a": w1a, "w1g": w1g,
                "b1a": b1a, "b1g": b1g, "w2p": w2p, "b2": inp["L3_ffn_b2"],
                "rot": rot, "msk": msk} for i in range(NCORES)]
    res = run_bass_kernel_spmd(nc, in_maps, core_ids=list(range(NCORES)))
    return [np.asarray(r["xout"]) for r in res.results]



def build_l0_attn():
    E = Em("l0attn")
    x_in = E.dram("x", [TOKENS, D], kind="ExternalInput")
    wcat_in = E.dram("wcat", [1024, 352], kind="ExternalInput")
    bcat_in = E.dram("bcat", [352], kind="ExternalInput")
    lnw_in = E.dram("lnw", [1024], kind="ExternalInput")
    lnb_in = E.dram("lnb", [1024], kind="ExternalInput")
    outw_in = E.dram("outw", [32, 1024], kind="ExternalInput")
    outb_in = E.dram("outb", [1024], kind="ExternalInput")
    io8_in = E.dram("io8", [128, 8], kind="ExternalInput")
    iden8_in = E.dram("iden8", [128, 64], kind="ExternalInput")
    x_out = E.dram("xout", [TOKENS, D], kind="ExternalOutput")

    T = SEQ
    NT2 = 16
    sd_dram = E.dram("sdscr", [B_PER_CORE, 16, 128, 4, 80])
    ys_dram = E.dram("ysscr", [128, 8, 128])

    xs = E.sbuf("xs", [128, 2, D])
    st = alloc_ln_stats(E, 2)
    iden = emit_identity(E)
    wcat = E.sbuf("wcats", [128, 8, 352])
    xnT_t = E.sbuf("xnTt", [128, 8, 128])
    bcat_r = E.sbuf("bcatr", [1, 352])
    onesf = E.sbuf("onesf", [1, 512])
    lnw_bc = E.sbuf("lnwbc", [128, D])
    lnb_bc = E.sbuf("lnbbc", [128, D])
    lnwrow = E.sbuf("lnwrow", [1, D])
    sk = E.sbuf("sk", [128, 2, 4, 64])
    ske = E.sbuf("ske", [128, 2, 4, 64])
    lse = E.sbuf("lse", [128, 2, 4, 8])
    sgb = E.sbuf("sgb", [128, 2, 4, 24])
    hard = E.sbuf("hard", [128, 2, 4, 64])
    RM = E.sbuf("RM", [128, 2, 4, 8])
    rowm = E.sbuf("rowm", [128, 2, 4, 8])
    colm = E.sbuf("colm", [128, 2, 4, 8])
    rsel = E.sbuf("rsel", [128, 2, 4, 8])
    ridx = E.sbuf("ridx", [128, 2, 4, 1])
    oh_r = E.sbuf("ohr", [128, 2, 4, 8])
    oh_c = E.sbuf("ohc", [128, 2, 4, 8])
    rowv = E.sbuf("rowv", [128, 2, 4, 8])
    tmp4 = E.sbuf("tmp4", [128, 2, 4, 64])
    io8 = E.sbuf("io8s", [128, 8])
    io9 = E.sbuf("io9s", [128, 8])
    iden8 = E.sbuf("iden8s", [128, 64])
    gsc = E.sbuf("gsc", [128, 2, 4, 16])
    slab = E.sbuf("slab", [128, 128, 80])
    Tm = E.sbuf("Tm", [128, 64])
    ev = E.sbuf("ev", [128, 8])
    Tsh = E.sbuf("Tsh", [128, 64])
    esh = E.sbuf("esh", [128, 8])
    t4 = E.sbuf("t4", [128, 512])
    t3 = E.sbuf("t3", [128, 64])
    hv = E.sbuf("hv", [128, 8])
    ysl = E.sbuf("ysl", [128, 8, 128])
    ysT = E.sbuf("ysT", [32, 2048], BF16)
    res2 = E.sbuf("res2", [128, 2, D])
    outw_sb = E.sbuf("outwsb", [32, 1024], BF16)
    outb_r = E.sbuf("outbr", [1, 1024])
    pads = [E.psum(f"p{i}", [128, 512]) for i in range(8)]

    def f0(v):
        v.memset(onesf[:, :], 1.0)
    E.blk(vector=f0)
    E.constf(LN_EPS)

    def fld(sy):
        for k in range(8):
            E.dma(sy, wcat[:, k, :], wcat_in[k * 128:(k + 1) * 128, :])
        E.dma(sy, bcat_r[:, :], bcat_in.unsqueeze(0))
        E.dma(sy, lnwrow[:, :], lnw_in.unsqueeze(0))
        E.dma(sy, io8[:, :], io8_in)
        E.dma(sy, iden8[:, :], iden8_in)
        E.dma(sy, outb_r[:, :], outb_in.unsqueeze(0))
        E.dwait(sy)
    E.blk(sync=fld)

    def fldg(g):
        E.dma(g, outw_sb[:, :], outw_in)
        E.dwait(g)
    E.blk(gpsimd=fldg)

    def fi9(v):
        v.ts(io9[:, :], io8[:, :], 1.0, -9.0, ALU.mult, ALU.add)
    E.blk(vector=fi9)

    def flw(t):
        for dh in range(2):
            t.mm(pads[dh][:, :], onesf[:, 0:128],
                 lnwrow[:, dh * 512:(dh + 1) * 512])
    E.blk(tensor=flw)

    def flwe(v):
        for dh in range(2):
            v.copy(lnw_bc[:, dh * 512:(dh + 1) * 512], pads[dh][:, :], nodep=True)
    E.blk(vector=flwe)

    def flb0(sy):
        E.dma(sy, lnwrow[:, :], lnb_in.unsqueeze(0))
        E.dwait(sy)
    E.blk(sync=flb0)

    def flb(t):
        for dh in range(2):
            t.mm(pads[dh][:, :], onesf[:, 0:128],
                 lnwrow[:, dh * 512:(dh + 1) * 512])
    E.blk(tensor=flb)

    def flbe(v):
        for dh in range(2):
            v.copy(lnb_bc[:, dh * 512:(dh + 1) * 512], pads[dh][:, :], nodep=True)
    E.blk(vector=flbe)

    for s in range(B_PER_CORE):
        base = s * T
        for grp in range(8):
            tt0 = grp * 2

            def fx(sy, tt0=tt0, base=base):
                for j in range(2):
                    E.dma(sy, xs[:, j, :],
                          x_in[base + (tt0 + j) * 128:base + (tt0 + j + 1) * 128, :])
                E.dwait(sy)
            E.blk(sync=fx)

            emit_ln(E, xs, 2, st, xs)
            emit_ln(E, xs, 2, st, xs)

            def faf(v):
                for j in range(2):
                    v.tt(xs[:, j, :], xs[:, j, :], lnw_bc[:, :], ALU.mult,
                         nodep=True)
                v.e.drain()
                for j in range(2):
                    v.tt(xs[:, j, :], xs[:, j, :], lnb_bc[:, :], ALU.add,
                         nodep=True)
            E.blk(vector=faf)

            for j in range(2):
                def ftr(t, j=j):
                    for c in range(8):
                        t.transpose(pads[c][:, 0:128], xs[:, j, c * 128:(c + 1) * 128],
                                    iden[:, :])
                E.blk(tensor=ftr)

                def fte(v, j=j):
                    for c in range(8):
                        v.copy(xnT_t[:, c, :], pads[c][:, 0:128], nodep=True)
                E.blk(vector=fte)

                def fpm(t, j=j):
                    for k in range(8):
                        t.mm(pads[0][:, 0:352], xnT_t[:, k, :],
                             wcat[:, k, :], start=(k == 0), stop=False)
                    t.mm(pads[0][:, 0:352], onesf[:, 0:128], bcat_r[:, :],
                         start=False, stop=True)
                E.blk(tensor=fpm)

                def fse(sC, j=j):
                    sC.act(sgb[:, j, :, 0:8],
                           pads[0][:, 0:32].rearrange("p (h n) -> p h n", h=4),
                           AF.Sigmoid, nodep=True)
                    sC.act(sgb[:, j, :, 8:16],
                           pads[0][:, 288:320].rearrange("p (h n) -> p h n", h=4),
                           AF.Sigmoid, nodep=True)
                E.blk(scalar=fse)

                def fsv(v, j=j):
                    v.ts(sgb[:, j, :, 0:8], sgb[:, j, :, 0:8], 2.0, -1.0,
                         ALU.mult, ALU.add)
                    v.copy(sgb[:, j, :, 16:24],
                           pads[0][:, 320:352].rearrange("p (h n) -> p h n", h=4),
                           nodep=True)
                    v.copy(sk[:, j, :, :],
                           pads[0][:, 32:288].rearrange("p (h n) -> p h n", h=4),
                           nodep=True)
                E.blk(vector=fsv)

            for it in range(5):
                def fsk1(sC):
                    sC.act(ske[:, :, :, :], sk[:, :, :, :], AF.Exp)
                E.blk(scalar=fsk1)

                def fsk2(v):
                    v.red(lse[:, :, :, :].rearrange("p a h n -> p (a h) n")
                          .unsqueeze(3),
                          ske[:, :, :, :].rearrange("p a h (i j) -> p (a h) i j",
                                                    i=8), ALU.add)
                E.blk(vector=fsk2)

                def fsk3(sC):
                    sC.act(lse[:, :, :, :], lse[:, :, :, :], AF.Ln)
                E.blk(scalar=fsk3)

                def fsk4(v):
                    v.tt(sk[:, :, :, :].rearrange("p a h (i j) -> p (a h) i j", i=8),
                         sk[:, :, :, :].rearrange("p a h (i j) -> p (a h) i j", i=8),
                         lse[:, :, :, :].rearrange("p a h n -> p (a h) n")
                         .unsqueeze(3).broadcast_to((128, 8, 8, 8)), ALU.subtract)
                E.blk(vector=fsk4)

                def fsk5(sC):
                    sC.act(ske[:, :, :, :], sk[:, :, :, :], AF.Exp)
                E.blk(scalar=fsk5)

                def fsk6(v):
                    v.red(lse[:, :, :, :].rearrange("p a h n -> p (a h) n")
                          .unsqueeze(3),
                          ske[:, :, :, :].rearrange("p a h (i j) -> p (a h) i j",
                                                    i=8).transpose([0, 1, 3, 2]),
                          ALU.add)
                E.blk(vector=fsk6)

                def fsk7(sC):
                    sC.act(lse[:, :, :, :], lse[:, :, :, :], AF.Ln)
                E.blk(scalar=fsk7)

                def fsk8(v):
                    v.tt(sk[:, :, :, :].rearrange("p a h (i j) -> p (a h) i j", i=8),
                         sk[:, :, :, :].rearrange("p a h (i j) -> p (a h) i j", i=8),
                         lse[:, :, :, :].rearrange("p a h n -> p (a h) n")
                         .unsqueeze(2).broadcast_to((128, 8, 8, 8)), ALU.subtract)
                E.blk(vector=fsk8)

            def fskf(sC):
                sC.act(sk[:, :, :, :], sk[:, :, :, :], AF.Exp)
            E.blk(scalar=fskf)

            def fgr(v):
                io8b = io8[:, :].unsqueeze(1).broadcast_to((128, 8, 8))
                io9b = io9[:, :].unsqueeze(1).broadcast_to((128, 8, 8))
                sk4 = sk[:, :, :, :].rearrange("p a h (i j) -> p (a h) i j", i=8)
                hard4 = hard[:, :, :, :].rearrange("p a h (i j) -> p (a h) i j", i=8)
                RM2 = RM[:, :, :, :].rearrange("p a h n -> p (a h) n")
                rowm2 = rowm[:, :, :, :].rearrange("p a h n -> p (a h) n")
                colm2 = colm[:, :, :, :].rearrange("p a h n -> p (a h) n")
                rsel2 = rsel[:, :, :, :].rearrange("p a h n -> p (a h) n")
                m = ridx[:, :, :, :].rearrange("p a h n -> p (a h) n")
                ohr2 = oh_r[:, :, :, :].rearrange("p a h n -> p (a h) n")
                ohc2 = oh_c[:, :, :, :].rearrange("p a h n -> p (a h) n")
                rowv2 = rowv[:, :, :, :].rearrange("p a h n -> p (a h) n")
                tmp42 = tmp4[:, :, :, :].rearrange("p a h (i j) -> p (a h) i j", i=8)
                v.memset(hard4, 0.0)
                v.memset(rowm2, 0.0)
                v.memset(colm2, 0.0)
                v.red(RM2.unsqueeze(3), sk4, ALU.max)
                for i in range(8):
                    v.tt(rsel2, RM2, rowm2, ALU.add)
                    v.red(m.unsqueeze(3)[:, :, 0:1, :], rsel2.unsqueeze(2),
                          ALU.max)
                    v.tt(rsel2, rsel2, m.broadcast_to((128, 8, 8)), ALU.is_ge)
                    v.tt(rowv2, rsel2, io9b, ALU.mult)
                    v.ts(rowv2, rowv2, 1.0, 9.0, ALU.mult, ALU.add)
                    v.red(m.unsqueeze(3)[:, :, 0:1, :], rowv2.unsqueeze(2),
                          ALU.min)
                    v.tt(ohr2, io8b, m.broadcast_to((128, 8, 8)), ALU.is_equal)
                    v.tt(tmp42, sk4,
                         ohr2.unsqueeze(3).broadcast_to((128, 8, 8, 8)), ALU.mult)
                    v.red(rowv2.unsqueeze(3), tmp42.transpose([0, 1, 3, 2]),
                          ALU.add)
                    v.tt(rowv2, rowv2, colm2, ALU.add)
                    v.red(m.unsqueeze(3)[:, :, 0:1, :], rowv2.unsqueeze(2),
                          ALU.max)
                    v.tt(ohc2, rowv2, m.broadcast_to((128, 8, 8)), ALU.is_ge)
                    v.tt(rowv2, ohc2, io9b, ALU.mult)
                    v.ts(rowv2, rowv2, 1.0, 9.0, ALU.mult, ALU.add)
                    v.red(m.unsqueeze(3)[:, :, 0:1, :], rowv2.unsqueeze(2),
                          ALU.min)
                    v.tt(ohc2, io8b, m.broadcast_to((128, 8, 8)), ALU.is_equal)
                    v.tt(tmp42, ohr2.unsqueeze(3).broadcast_to((128, 8, 8, 8)),
                         ohc2.unsqueeze(2).broadcast_to((128, 8, 8, 8)), ALU.mult)
                    v.tt(hard4, hard4, tmp42, ALU.add)
                    v.stt(rowm2, ohr2, -1e30, rowm2, ALU.mult, ALU.add)
                    v.stt(colm2, ohc2, -1e30, colm2, ALU.mult, ALU.add)
            E.blk(vector=fgr)

            def fgs(v):
                v.tt(gsc[:, :, :, 0:8], sgb[:, :, :, 8:16], sgb[:, :, :, 0:8],
                     ALU.mult)
                v.tt(gsc[:, :, :, 8:16], sgb[:, :, :, 8:16], sgb[:, :, :, 16:24],
                     ALU.mult)
                v.e.drain()
                v.tt(gsc[:, :, :, 8:16], sgb[:, :, :, 16:24], gsc[:, :, :, 8:16],
                     ALU.subtract)
            E.blk(vector=fgs)

            def fso(sy, s=s, tt0=tt0):
                for j in range(2):
                    E.dma(sy, sd_dram[s, tt0 + j, :, :, 0:64], hard[:, j, :, :])
                    E.dma(sy, sd_dram[s, tt0 + j, :, :, 64:72], gsc[:, j, :, 0:8])
                    E.dma(sy, sd_dram[s, tt0 + j, :, :, 72:80], gsc[:, j, :, 8:16])
                E.dwait(sy)
            E.blk(sync=fso)

    def fsl(sy):
        for s2 in range(2):
            for c2 in range(16):
                E.dma(sy, slab[s2 * 64 + c2 * 4:s2 * 64 + c2 * 4 + 4, :, :],
                      sd_dram[s2, c2].rearrange("t h e -> h t e"))
        E.dwait(sy)
    E.blk(sync=fsl)

    def fpb(v):
        v.copy(Tm[:, :], iden8[:, :])
        v.memset(ev[:, :], 0.0)
        v.e.drain()
        T3v = Tm[:, :].rearrange("p (j k) -> p j k", j=8)
        t4v = t4[:, :].rearrange("p (i j k) -> p i j k", i=8, j=8)
        t3v = t3[:, :].rearrange("p (i j) -> p i j", i=8)
        for step in range(128):
            P = slab[:, step, 0:64].rearrange("p (i j) -> p i j", i=8)
            gs_ = slab[:, step, 64:72]
            c_ = slab[:, step, 72:80]
            v.tt(t4v, P.unsqueeze(3).broadcast_to((128, 8, 8, 8)),
                 Tm[:, :].rearrange("p (j k) -> p j k", j=8).unsqueeze(1)
                 .broadcast_to((128, 8, 8, 8)), ALU.mult)
            v.tt(t3v, P, ev[:, :].unsqueeze(1).broadcast_to((128, 8, 8)),
                 ALU.mult)
            v.e.drain()
            v.red(Tm[:, :].rearrange("p (i k) -> p i k", i=8).unsqueeze(3),
                  t4v.transpose([0, 1, 3, 2]), ALU.add)
            v.red(ev[:, :].unsqueeze(2), t3v, ALU.add)
            v.e.drain()
            v.tt(Tm[:, :].rearrange("p (i k) -> p i k", i=8),
                 Tm[:, :].rearrange("p (i k) -> p i k", i=8),
                 gs_.unsqueeze(2).broadcast_to((128, 8, 8)), ALU.mult)
            v.tt(ev[:, :], ev[:, :], gs_, ALU.mult)
            v.e.drain()
            v.tt(ev[:, :], ev[:, :], c_, ALU.add)
            v.e.drain()
    E.blk(vector=fpb)

    for r in range(4):
        sh = 4 * (2 ** r)

        def fcp(v):
            v.copy(Tsh[:, :], iden8[:, :])
            v.memset(esh[:, :], 0.0)
        E.blk(vector=fcp)

        def fcd(sy, sh=sh):
            for s2 in range(2):
                E.dma(sy, Tsh[s2 * 64 + sh:s2 * 64 + 64, :],
                      Tm[s2 * 64:s2 * 64 + 64 - sh, :])
                E.dma(sy, esh[s2 * 64 + sh:s2 * 64 + 64, :],
                      ev[s2 * 64:s2 * 64 + 64 - sh, :])
            E.dwait(sy)
        E.blk(sync=fcd)

        def fcc(v):
            t4v = t4[:, :].rearrange("p (i j k) -> p i j k", i=8, j=8)
            t3v = t3[:, :].rearrange("p (i j) -> p i j", i=8)
            Tc = Tm[:, :].rearrange("p (i j) -> p i j", i=8)
            v.tt(t4v, Tc.unsqueeze(3).broadcast_to((128, 8, 8, 8)),
                 Tsh[:, :].rearrange("p (j k) -> p j k", j=8).unsqueeze(1)
                 .broadcast_to((128, 8, 8, 8)), ALU.mult)
            v.tt(t3v, Tc, esh[:, :].unsqueeze(1).broadcast_to((128, 8, 8)),
                 ALU.mult)
            v.e.drain()
            v.red(Tm[:, :].rearrange("p (i k) -> p i k", i=8).unsqueeze(3),
                  t4v.transpose([0, 1, 3, 2]), ALU.add)
            v.red(esh[:, :].unsqueeze(2), t3v, ALU.add)
            v.e.drain()
            v.tt(ev[:, :], ev[:, :], esh[:, :], ALU.add)
        E.blk(vector=fcc)

    def fhe(v):
        v.memset(esh[:, :], 0.0)
    E.blk(vector=fhe)

    def fhd(sy):
        for s2 in range(2):
            E.dma(sy, esh[s2 * 64 + 4:s2 * 64 + 64, :],
                  ev[s2 * 64:s2 * 64 + 60, :])
        E.dwait(sy)
    E.blk(sync=fhd)

    def fpd(v):
        v.copy(hv[:, :], esh[:, :])
        v.e.drain()
        t3v = t3[:, :].rearrange("p (i j) -> p i j", i=8)
        for step in range(128):
            P = slab[:, step, 0:64].rearrange("p (i j) -> p i j", i=8)
            gs_ = slab[:, step, 64:72]
            c_ = slab[:, step, 72:80]
            v.tt(t3v, P, hv[:, :].unsqueeze(1).broadcast_to((128, 8, 8)),
                 ALU.mult)
            v.e.drain()
            v.red(hv[:, :].unsqueeze(2), t3v, ALU.add)
            v.e.drain()
            v.tt(hv[:, :], hv[:, :], gs_, ALU.mult)
            v.e.drain()
            v.tt(hv[:, :], hv[:, :], c_, ALU.add)
            v.e.drain()
            v.copy(ysl[:, :, step:step + 1], hv[:, :].unsqueeze(2), nodep=True)
            v.e.drain()
    E.blk(vector=fpd)

    def fyo(sy):
        E.dma(sy, ys_dram, ysl[:, :, :])
        E.dwait(sy)
    E.blk(sync=fyo)

    for s in range(B_PER_CORE):
        base = s * T

        def fyt(g, s=s):
            v5 = ys_dram.rearrange("(s2 c h) n t -> s2 h n c t", s2=2, c=16)
            for h in range(4):
                E.dma(g, ysT[h * 8:(h + 1) * 8, :].rearrange(
                    "p (c t) -> p c t", c=16), v5[s, h])
            E.dwait(g)
        E.blk(gpsimd=fyt)

        for tp in range(8):
            def fxr(sy, tp=tp, base=base):
                for j in range(2):
                    tt = tp * 2 + j
                    E.dma(sy, xs[:, j, :],
                          x_in[base + tt * 128:base + (tt + 1) * 128, :])
                    E.dma(sy, res2[:, j, :],
                          x_in[base + tt * 128:base + (tt + 1) * 128, :])
                E.dwait(sy)
            E.blk(sync=fxr)

            emit_ln(E, res2, 2, st, res2)

            def fom(t, tp=tp):
                for j in range(2):
                    for dh in range(2):
                        p = pads[j * 2 + dh]
                        t.mm(p[:, :], ysT[:, (tp * 2 + j) * 128:(tp * 2 + j + 1) * 128],
                             outw_sb[:, dh * 512:(dh + 1) * 512],
                             start=True, stop=False)
                        t.mm(p[:, :], onesf[:, 0:128],
                             outb_r[:, dh * 512:(dh + 1) * 512],
                             start=False, stop=True)
            E.blk(tensor=fom)

            def fev(v, tp=tp):
                for j in range(2):
                    v.tt(xs[:, j, :], xs[:, j, :], res2[:, j, :], ALU.add,
                         nodep=True)
                v.e.drain()
                for j in range(2):
                    for dh in range(2):
                        v.tt(xs[:, j, dh * 512:(dh + 1) * 512],
                             xs[:, j, dh * 512:(dh + 1) * 512],
                             pads[j * 2 + dh][:, :], ALU.add, nodep=True)
            E.blk(vector=fev)

            def fxo(sy, tp=tp, base=base):
                for j in range(2):
                    tt = tp * 2 + j
                    E.dma(sy, x_out[base + tt * 128:base + (tt + 1) * 128, :],
                          xs[:, j, :])
                E.dwait(sy)
            E.blk(sync=fxo)

    E.finish()
    return E.nc


def launch_l0_attn(state, inp):
    nc = build_l0_attn()
    wcat = np.concatenate([inp["L0_attn_sign_W"], inp["L0_attn_perm_W"],
                           inp["L0_attn_gate_W"], inp["L0_attn_in_W"]], axis=1)
    bcat = np.concatenate([inp["L0_attn_sign_b"], inp["L0_attn_perm_b"],
                           inp["L0_attn_gate_b"], inp["L0_attn_in_b"]])
    io8 = np.ascontiguousarray(np.broadcast_to(
        np.arange(8, dtype=np.float32)[None, :], (128, 8)))
    iden8 = np.ascontiguousarray(np.broadcast_to(
        np.eye(8, dtype=np.float32).reshape(1, 64), (128, 64)))
    in_maps = [{"x": state[i], "wcat": wcat, "bcat": bcat,
                "lnw": inp["L0_attn_ln_w"], "lnb": inp["L0_attn_ln_b"],
                "outw": inp["L0_attn_out_W"], "outb": inp["L0_attn_out_b"],
                "io8": io8, "iden8": iden8} for i in range(NCORES)]
    res = run_bass_kernel_spmd(nc, in_maps, core_ids=list(range(NCORES)))
    return [np.asarray(r["xout"]) for r in res.results]



def kernel(**inp):
    x = np.asarray(inp["L0_attn_x"], np.float32)
    state = [np.ascontiguousarray(x[2 * i:2 * i + 2].reshape(TOKENS, D))
             for i in range(NCORES)]
    state = launch_l0_attn(state, inp)
    state = launch_l0_ffn(state, inp)
    state = launch_l1_attn(state, inp)
    state = launch_l1_ffn(state, inp)
    state = launch_l2_attn(state, inp)
    state = launch_l2_ffn(state, inp)
    state = launch_l3_attn(state, inp)
    state = launch_l3_ffn(state, inp)
    out = np.concatenate([s.reshape(B_PER_CORE, SEQ, D) for s in state], axis=0)
    return out.astype(np.float32)


# version 04
# speedup vs baseline: 1.0050x; 1.0050x over previous
import contextlib
import numpy as np
import concourse.bass as bass
import concourse.mybir as mybir
from concourse.bass_utils import run_bass_kernel_spmd

F32 = mybir.dt.float32
BF16 = mybir.dt.bfloat16
I32 = mybir.dt.int32
AF = mybir.ActivationFunctionType
ALU = mybir.AluOpType
AX = mybir.AxisListType

NCORES = 8
B_PER_CORE = 2
SEQ = 2048
TOKENS = B_PER_CORE * SEQ
D = 1024
LN_EPS = 1e-5


def _names(aps):
    out = set()
    for a in aps:
        if isinstance(a, bass.AP):
            out.add(a.tensor.name)
    return out


class Eng:

    def __init__(self, eng):
        self.e = eng
        self.w = set()
        self.r = set()

    def _pre(self, outs, ins, nodep=False):
        on, inn = _names(outs), _names(ins)
        if not nodep and ((self.w & (on | inn)) or (self.r & on)):
            self.e.drain()
            self.w.clear()
            self.r.clear()
        self.w |= on
        self.r |= inn

    def copy(self, out, in_, nodep=False):
        self._pre([out], [in_], nodep)
        return self.e.tensor_copy(out, in_)

    def memset(self, out, val, nodep=False):
        self._pre([out], [], nodep)
        return self.e.memset(out, val)

    def tt(self, out, a, b, op, nodep=False):
        self._pre([out], [a, b], nodep)
        return self.e.tensor_tensor(out, a, b, op)

    def ts(self, out, in0, s1, s2=None, op0=ALU.mult, op1=ALU.bypass, nodep=False):
        self._pre([out], [in0, s1, s2], nodep)
        return self.e.tensor_scalar(out, in0, s1, s2, op0, op1)

    def tss(self, out, in_, s, op, nodep=False):
        self._pre([out], [in_, s], nodep)
        return self.e.tensor_single_scalar(out, in_, s, op)

    def stt(self, out, in0, s, in1, op0, op1, nodep=False):
        self._pre([out], [in0, s, in1], nodep)
        return self.e.scalar_tensor_tensor(out, in0, s, in1, op0, op1)

    def scan(self, out, d0, d1, init, op0=ALU.mult, op1=ALU.add, nodep=False):
        self._pre([out], [d0, d1, init], nodep)
        return self.e.tensor_tensor_scan(out, d0, d1, init, op0, op1)

    def red(self, out, in_, op, absval=None, nodep=False):
        self._pre([out], [in_], nodep)
        return self.e.tensor_reduce(out, in_, AX.X, op, apply_absolute_value=absval)

    def select(self, out, mask, a, b, nodep=False):
        self._pre([out], [mask, a, b], nodep)
        return self.e.select(out, mask, a, b)

    def iota(self, out, pattern, base=None, channel_multiplier=0, nodep=False):
        self._pre([out], [], nodep)
        return self.e.iota(out, pattern, base=base, channel_multiplier=channel_multiplier,
                           allow_small_or_imprecise_dtypes=True)

    def recip(self, out, in_, nodep=False):
        self._pre([out], [in_], nodep)
        return self.e.reciprocal(out, in_)

    def act(self, out, in_, func, bias=0.0, scale=1.0, accum_out=None, nodep=False):
        self._pre([out, accum_out], [in_, bias, scale], nodep)
        return self.e.activation(out, in_, func, bias=bias, scale=scale, accum_out=accum_out)

    def mm(self, out, lhsT, rhs, start=True, stop=True):
        return self.e.matmul(out, lhsT, rhs, start=start, stop=stop)

    def transpose(self, out, in_, iden):
        return self.e.transpose(out, in_, iden)


class Em:

    def __init__(self, name="k", dma_scratch=16384):
        self.nc = bass.Bass("TRN2", target_bir_lowering=False, name=name,
                            dynamic_dma_scratch_size=dma_scratch)
        self.ctx = contextlib.ExitStack()
        self.dm = self.ctx.enter_context(self.nc.semaphore("dm"))
        self.dmc = 0
        self.dmg = self.ctx.enter_context(self.nc.semaphore("dmg"))
        self.dmgc = 0
        self._uid = 0
        self.prefix = ""
        self._scope = None
        self._semcache = {}

    def push_scope(self):
        self._scope = contextlib.ExitStack()

    def pop_scope(self):
        if self._scope is not None:
            self._scope.close()
            self._scope = None

    def scoped(self, name, shape, dtype=F32):
        return self.nc.sbuf_tensor(self.prefix + name, shape, dtype)

    def sem(self, name):
        if name not in self._semcache:
            self._semcache[name] = Sem(self, name)
        return self._semcache[name]

    def sbuf(self, name, shape, dtype=F32):
        stack = self._scope if self._scope is not None else self.ctx
        return stack.enter_context(self.nc.sbuf_tensor(self.prefix + name,
                                                       shape, dtype))

    def constf(self, val, dtype=F32):
        key = (float(val), dtype)
        if not hasattr(self, "_consts"):
            self._consts = {}
        if key not in self._consts:
            t = self.ctx.enter_context(self.nc.sbuf_tensor(
                f"cst{len(self._consts)}", [128, 1], dtype))
            self.blk(vector=lambda v: v.memset(t[:, :], val))
            self._consts[key] = t
        return self._consts[key][:, :]

    def psum(self, name, shape, dtype=F32):
        stack = self._scope if self._scope is not None else self.ctx
        return stack.enter_context(self.nc.psum_tensor(self.prefix + name,
                                                       shape, dtype))

    def dram(self, name, shape, dtype=F32, kind="Internal"):
        return self.nc.dram_tensor(self.prefix + name, shape, dtype,
                                   kind=kind).ap()

    def dma(self, eng, dst, src, **kw):
        if getattr(eng, "engine", None) == mybir.EngineType.Pool:
            eng.dma_start(out=dst, in_=src, **kw).then_inc(self.dmg, 16)
            self.dmgc += 16
        else:
            eng.dma_start(out=dst, in_=src, **kw).then_inc(self.dm, 16)
            self.dmc += 16

    def dma_t(self, eng, dst, src):
        eng.dma_start(out=dst, in_=src, transpose=True).then_inc(self.dm, 16)
        self.dmc += 16

    def dwait(self, eng):
        eng.wait_ge(self.dm, self.dmc)
        eng.wait_ge(self.dmg, self.dmgc)

    def blk(self, tensor=None, vector=None, scalar=None, gpsimd=None, sync=None,
            wait_dma=()):
        with self.nc.Block() as b:
            def _wrap(dec, fn, raw=False):
                if fn is None:
                    return
                def body(e):
                    if dec in wait_dma:
                        e.wait_ge(self.dm, self.dmc)
                    fn(e if raw else Eng(e))
                getattr(b, dec)(body)
            _wrap("sync", sync, raw=True)
            _wrap("gpsimd", gpsimd, raw=True)
            _wrap("tensor", tensor)
            _wrap("vector", vector)
            _wrap("scalar", scalar)

    def dma_block(self, fn, engine="sync", wait=True):
        def body(e):
            fn(e)
            if wait:
                self.dwait(e)
        if engine == "sync":
            self.blk(sync=body)
        else:
            self.blk(gpsimd=body)

    def finish(self):
        self.ctx.close()



def emit_ln(E, xs, nt, st, ln_out, D_=D, eps=LN_EPS):
    def v1(v):
        for j in range(nt):
            v.red(st["sum"][:, j:j + 1].unsqueeze(1), xs[:, j, :].unsqueeze(1),
                  ALU.add, nodep=True)
    E.blk(vector=v1)

    def s1(s):
        s.act(st["mean"][:, :], st["sum"][:, :], AF.Copy, scale=1.0 / D_)
        for j in range(nt):
            s.act(st["trash"][:, :], xs[:, j, :], AF.Square,
                  accum_out=st["sqs"][:, j:j + 1])
        s.act(st["msq"][:, :], st["mean"][:, :], AF.Square)
    E.blk(scalar=s1)

    def v2(v):
        v.ts(st["var"][:, :], st["sqs"][:, :], 1.0 / D_, None, ALU.mult)
        v.tt(st["var"][:, :], st["var"][:, :], st["msq"][:, :], ALU.subtract)
    E.blk(vector=v2)

    epsap = E.constf(eps)

    def s2(s):
        s.act(st["var"][:, :], st["var"][:, :], AF.Ln, bias=epsap)
        s.act(st["rstd"][:, :], st["var"][:, :], AF.Exp, scale=-0.5)
    E.blk(scalar=s2)

    def v3(v):
        for j in range(nt):
            v.ts(ln_out[:, j, :], xs[:, j, :], st["mean"][:, j:j + 1],
                 st["rstd"][:, j:j + 1], ALU.subtract, ALU.mult, nodep=True)
    E.blk(vector=v3)


def alloc_ln_stats(E, nt, D_=D, tag=""):
    return {
        "sum": E.sbuf(f"lnsum{tag}", [128, nt]),
        "mean": E.sbuf(f"lnmean{tag}", [128, nt]),
        "sqs": E.sbuf(f"lnsqs{tag}", [128, nt]),
        "msq": E.sbuf(f"lnmsq{tag}", [128, nt]),
        "var": E.sbuf(f"lnvar{tag}", [128, nt]),
        "rstd": E.sbuf(f"lnrstd{tag}", [128, nt]),
        "trash": E.sbuf(f"lntrash{tag}", [128, D_]),
    }


def emit_transpose_bf(E, ln_bf, lnT, nt, nd):
    def f(sy):
        for j in range(nt):
            for c in range(nd):
                E.dma_t(sy, lnT[:, c, j * 128:(j + 1) * 128],
                        ln_bf[:, j, c * 128:(c + 1) * 128])
        E.dwait(sy)
    E.blk(sync=f)



def build_l0_ffn(E=None, xin=None, xout=None, prefix=""):
    standalone = E is None
    if standalone:
        E = Em("l0ffn")
    E.prefix = prefix
    E.push_scope()
    x_in = xin if xin is not None else E.dram("x", [TOKENS, D], kind="ExternalInput")
    wgT = E.dram("wgT", [1024, 3072], I32, kind="ExternalInput")
    wuT = E.dram("wuT", [1024, 3072], I32, kind="ExternalInput")
    wdT = E.dram("wdT", [3072, 1024], I32, kind="ExternalInput")
    x_out = xout if xout is not None else E.dram("xout", [TOKENS, D], kind="ExternalOutput")

    MEGA = 1024
    NT = MEGA // 128
    NMEGA = TOKENS // MEGA
    FCH = 24
    KCH = 8
    SCALE = 1.0 / 127.0

    xs = E.sbuf("xs", [128, NT, D])
    ln_bf = E.sbuf("lnbf", [128, NT, D], BF16)
    lnT = E.sbuf("lnT", [128, KCH, MEGA], BF16)
    h_bf = E.sbuf("hbf", [128, FCH, MEGA], BF16)
    silu_g = E.sbuf("silug", [128, MEGA])
    wg_str = E.sbuf("wgstr", [128, KCH, 512], BF16)
    wu_str = E.sbuf("wustr", [128, KCH, 512], BF16)
    wd_sb = E.sbuf("wdsb", [128, FCH, 512], BF16)
    st = alloc_ln_stats(E, NT)
    pads = [E.psum(f"p{i}", [128, 512]) for i in range(8)]

    for mg in range(NMEGA):
        t0 = mg * MEGA
        def fx(sy, t0=t0):
            for j in range(NT):
                E.dma(sy, xs[:, j, :], x_in[t0 + j * 128: t0 + (j + 1) * 128, :])
            E.dwait(sy)
        E.blk(sync=fx)

        emit_ln(E, xs, NT, st, ln_bf)
        emit_transpose_bf(E, ln_bf, lnT, NT, KCH)

        for fg in range(FCH // 4):
            def fw(g, fg=fg):
                for k in range(KCH):
                    E.dma(g, wg_str[:, k, :], wgT[k * 128:(k + 1) * 128,
                                                  fg * 512:(fg + 1) * 512])
                    E.dma(g, wu_str[:, k, :], wuT[k * 128:(k + 1) * 128,
                                                  fg * 512:(fg + 1) * 512])
                E.dwait(g)
            E.blk(gpsimd=fw)

            for f_in in range(4):
                f = fg * 4 + f_in
                pg = pads[0:2]
                pu = pads[2:4]

                def fmm(t, f_in=f_in):
                    for k in range(KCH):
                        for th in range(2):
                            t.mm(pg[th][:, :], wg_str[:, k, f_in * 128:(f_in + 1) * 128],
                                 lnT[:, k, th * 512:(th + 1) * 512],
                                 start=(k == 0), stop=(k == KCH - 1))
                            t.mm(pu[th][:, :], wu_str[:, k, f_in * 128:(f_in + 1) * 128],
                                 lnT[:, k, th * 512:(th + 1) * 512],
                                 start=(k == 0), stop=(k == KCH - 1))
                E.blk(tensor=fmm)

                def fsilu(s):
                    for th in range(2):
                        s.act(silu_g[:, th * 512:(th + 1) * 512], pg[th][:, :],
                              AF.Silu, scale=SCALE, nodep=True)
                E.blk(scalar=fsilu)

                def fh(v, f=f):
                    for th in range(2):
                        v.stt(h_bf[:, f, th * 512:(th + 1) * 512], pu[th][:, :],
                              SCALE, silu_g[:, th * 512:(th + 1) * 512],
                              ALU.mult, ALU.mult, nodep=True)
                E.blk(vector=fh)

        for dh in range(2):
            def fwd(g, dh=dh):
                for fk in range(FCH):
                    E.dma(g, wd_sb[:, fk, :], wdT[fk * 128:(fk + 1) * 128,
                                                  dh * 512:(dh + 1) * 512])
                E.dwait(g)
            E.blk(gpsimd=fwd)

            def fdmm(t, dh=dh):
                for fk in range(FCH):
                    for tt in range(NT):
                        t.mm(pads[tt][:, :], h_bf[:, fk, tt * 128:(tt + 1) * 128],
                             wd_sb[:, fk, :], start=(fk == 0), stop=(fk == FCH - 1))
            E.blk(tensor=fdmm)

            def fdev(v, dh=dh):
                for tt in range(NT):
                    v.stt(xs[:, tt, dh * 512:(dh + 1) * 512], pads[tt][:, :], SCALE,
                          xs[:, tt, dh * 512:(dh + 1) * 512], ALU.mult, ALU.add,
                          nodep=True)
            E.blk(vector=fdev)

        def fo(sy, t0=t0):
            for j in range(NT):
                E.dma(sy, x_out[t0 + j * 128: t0 + (j + 1) * 128, :], xs[:, j, :])
            E.dwait(sy)
        E.blk(sync=fo)

    E.pop_scope()
    if standalone:
        E.finish()
        return E.nc
    return None


def launch_l0_ffn(state, inp):
    nc = build_l0_ffn()
    wgT = np.ascontiguousarray(inp["L0_ffn_w_gate"].T)
    wuT = np.ascontiguousarray(inp["L0_ffn_w_up"].T)
    wdT = np.ascontiguousarray(inp["L0_ffn_w_down"].T)
    in_maps = [{"x": state[i], "wgT": wgT, "wuT": wuT, "wdT": wdT}
               for i in range(NCORES)]
    res = run_bass_kernel_spmd(nc, in_maps, core_ids=list(range(NCORES)))
    return [np.asarray(r["xout"]) for r in res.results]



def emit_quant_fp6(E, src_views, dst_views, s_views, scr, nblk, blk=32):
    n = len(src_views)

    def vq(v):
        for i in range(n):
            g = src_views[i].rearrange("p (a b) -> p a b", b=blk)
            v.red(s_views[i].unsqueeze(2), g, ALU.max, absval=True, nodep=True)
        v.e.drain()
        for i in range(n):
            v.ts(s_views[i], s_views[i], 1.0 / 7.5, 1e-30, ALU.mult, ALU.max,
                 nodep=True)
        v.e.drain()
        for i in range(n):
            v.recip(scr["sinv"][i], s_views[i], nodep=True)
        v.e.drain()
        q, absq = scr["q"][:, :], scr["absq"][:, :]
        c2, c4 = scr["c2"][:, :], scr["c4"][:, :]
        m, minv = scr["m"][:, :], scr["minv"][:, :]
        ti, tf = scr["ti"][:, :], scr["tf"][:, :]
        for i in range(n):
            sb = scr["sinv"][i].unsqueeze(2).broadcast_to((128, nblk, blk))
            v.tt(q.rearrange("p (a b) -> p a b", b=blk),
                 src_views[i].rearrange("p (a b) -> p a b", b=blk), sb, ALU.mult)
            v.ts(absq, q, -1.0, None, ALU.mult)
            v.tt(absq, q, absq, ALU.max)
            v.tss(c2, absq, 2.0, ALU.is_ge)
            v.tss(c4, absq, 4.0, ALU.is_ge)
            v.ts(m, c2, -4.0, 8.0, ALU.mult, ALU.add)
            v.stt(m, c4, -2.0, m, ALU.mult, ALU.add)
            v.tt(q, q, m, ALU.mult)
            v.copy(ti, q)
            v.copy(tf, ti)
            v.ts(minv, c2, 0.125, 0.125, ALU.mult, ALU.add)
            v.stt(minv, c4, 0.25, minv, ALU.mult, ALU.add)
            v.tt(tf, tf, minv, ALU.mult)
            sb2 = s_views[i].unsqueeze(2).broadcast_to((128, nblk, blk))
            v.tt(dst_views[i].rearrange("p (a b) -> p a b", b=blk),
                 tf.rearrange("p (a b) -> p a b", b=blk), sb2, ALU.mult)
    E.blk(vector=vq)


def alloc_q_scratch(E, F, tag=""):
    return {
        "q": E.sbuf(f"qq{tag}", [128, F]),
        "absq": E.sbuf(f"qa{tag}", [128, F]),
        "c2": E.sbuf(f"qc2{tag}", [128, F]),
        "c4": E.sbuf(f"qc4{tag}", [128, F]),
        "m": E.sbuf(f"qm{tag}", [128, F]),
        "minv": E.sbuf(f"qmi{tag}", [128, F]),
        "ti": E.sbuf(f"qti{tag}", [128, F], I32),
        "tf": E.sbuf(f"qtf{tag}", [128, F]),
        "sinv": None,
    }


def build_l1_ffn(E=None, xin=None, xout=None, prefix=""):
    standalone = E is None
    if standalone:
        E = Em("l1ffn")
    E.prefix = prefix
    E.push_scope()
    x_in = xin if xin is not None else E.dram("x", [TOKENS, D], kind="ExternalInput")
    w_in = E.dram("w", [1024, 1024], kind="ExternalInput")
    b_in = E.dram("b", [1024], kind="ExternalInput")
    x_out = xout if xout is not None else E.dram("xout", [TOKENS, D], kind="ExternalOutput")

    MEGA = 1024
    NT = MEGA // 128
    NMEGA = TOKENS // MEGA
    NBLK = 32

    xs = E.sbuf("xs", [128, NT, D])
    lnf = E.sbuf("lnf", [128, NT, D])
    xq_bf = E.sbuf("xqbf", [128, NT, D], BF16)
    xqT = E.sbuf("xqT", [128, 8, MEGA], BF16)
    st = alloc_ln_stats(E, NT)
    scr = alloc_q_scratch(E, D)
    sx = E.sbuf("sx", [128, NT, NBLK])
    scr["sinv"] = None

    wrow = E.sbuf("wrow", [128, 1024])
    wq_bf = E.sbuf("wqbf", [128, 8, 1024], BF16)
    wqT = E.sbuf("wqT", [128, 8, 1024], BF16)
    sw = E.sbuf("sw", [128, 8, NBLK])
    bias_bc = E.sbuf("biasbc", [128, 1024])
    brow = E.sbuf("brow", [1, 1024])
    ones1 = E.sbuf("ones1", [1, 128], BF16)
    brow_bf = E.sbuf("browbf", [1, 1024], BF16)
    pads = [E.psum(f"p{i}", [128, 512]) for i in range(8)]

    def fb0(sy):
        E.dma(sy, brow[:, :], b_in.unsqueeze(0))
        E.dwait(sy)
    E.blk(sync=fb0)

    def fo1(v):
        v.memset(ones1[:, :], 1.0)
        v.copy(brow_bf[:, :], brow[:, :])
    E.blk(vector=fo1)

    sw2 = E.sbuf("sw2", [128, 8, NBLK])
    for c in range(8):
        def fw(sy, c=c):
            E.dma(sy, wrow[:, :], w_in[c * 128:(c + 1) * 128, :])
            E.dwait(sy)
        E.blk(sync=fw)
        scr["sinv"] = [sw[:, c, :]]
        emit_quant_fp6(E, [wrow[:, :]], [wq_bf[:, c, :]],
                       [sw2[:, c, :]], scr, NBLK)

    def ftw(sy):
        for c in range(8):
            for k in range(8):
                E.dma_t(sy, wqT[:, k, c * 128:(c + 1) * 128],
                        wq_bf[:, c, k * 128:(k + 1) * 128])
        E.dwait(sy)
    E.blk(sync=ftw)

    def fbb(t):
        for h in range(2):
            t.mm(pads[h][:, :], ones1[:, :], brow_bf[:, h * 512:(h + 1) * 512])
    E.blk(tensor=fbb)

    def fbe(v):
        for h in range(2):
            v.copy(bias_bc[:, h * 512:(h + 1) * 512], pads[h][:, :], nodep=True)
    E.blk(vector=fbe)

    sxv = E.sbuf("sxv", [128, NT, NBLK])

    for mg in range(NMEGA):
        t0 = mg * MEGA

        def fx(sy, t0=t0):
            for j in range(NT):
                E.dma(sy, xs[:, j, :], x_in[t0 + j * 128:t0 + (j + 1) * 128, :])
            E.dwait(sy)
        E.blk(sync=fx)

        emit_ln(E, xs, NT, st, lnf)

        scr["sinv"] = [sxv[:, j, :] for j in range(NT)]
        emit_quant_fp6(E, [lnf[:, j, :] for j in range(NT)],
                       [xq_bf[:, j, :] for j in range(NT)],
                       [sx[:, j, :] for j in range(NT)], scr, NBLK)

        def ftx(sy):
            for j in range(NT):
                for k in range(8):
                    E.dma_t(sy, xqT[:, k, j * 128:(j + 1) * 128],
                            xq_bf[:, j, k * 128:(k + 1) * 128])
            E.dwait(sy)
        E.blk(sync=ftx)

        def fmm(t):
            for j in range(NT):
                for oh in range(2):
                    p = pads[(j * 2 + oh) % 8]
                    for k in range(8):
                        t.mm(p[:, :], xqT[:, k, j * 128:(j + 1) * 128],
                             wqT[:, k, oh * 512:(oh + 1) * 512],
                             start=(k == 0), stop=(k == 7))
        def fmm_half(t, jlist):
            for j in jlist:
                for oh in range(2):
                    p = pads[((j % 4) * 2 + oh)]
                    for k in range(8):
                        t.mm(p[:, :], xqT[:, k, j * 128:(j + 1) * 128],
                             wqT[:, k, oh * 512:(oh + 1) * 512],
                             start=(k == 0), stop=(k == 7))

        def fev_half(v, jlist):
            for j in jlist:
                for oh in range(2):
                    p = pads[((j % 4) * 2 + oh)]
                    sl = slice(oh * 512, (oh + 1) * 512)
                    v.tt(xs[:, j, sl], xs[:, j, sl], bias_bc[:, sl], ALU.add,
                         nodep=True)
                    v.tt(xs[:, j, sl], xs[:, j, sl], p[:, :], ALU.add)

        for half in range(2):
            jl = list(range(half * 4, half * 4 + 4))
            E.blk(tensor=lambda t, jl=jl: fmm_half(t, jl))
            E.blk(vector=lambda v, jl=jl: fev_half(v, jl))

        def fo(sy, t0=t0):
            for j in range(NT):
                E.dma(sy, x_out[t0 + j * 128:t0 + (j + 1) * 128, :], xs[:, j, :])
            E.dwait(sy)
        E.blk(sync=fo)

    E.pop_scope()
    if standalone:
        E.finish()
        return E.nc
    return None


def launch_l1_ffn(state, inp):
    nc = build_l1_ffn()
    in_maps = [{"x": state[i], "w": inp["L1_ffn_weight"],
                "b": inp["L1_ffn_bias"]} for i in range(NCORES)]
    res = run_bass_kernel_spmd(nc, in_maps, core_ids=list(range(NCORES)))
    return [np.asarray(r["xout"]) for r in res.results]



def emit_identity(E):
    iden = E.sbuf("iden", [128, 128])

    def g1(g):
        g.iota(iden[:, :], [[1, 128]], channel_multiplier=-1)
    E.blk(gpsimd=lambda e: g1(Eng(e)))

    def v1(v):
        v.tss(iden[:, :], iden[:, :], 0.0, ALU.is_equal)
    E.blk(vector=v1)
    return iden


def build_l2_ffn(E=None, xin=None, xout=None, prefix=""):
    standalone = E is None
    if standalone:
        E = Em("l2ffn")
    E.prefix = prefix
    E.push_scope()
    x_in = xin if xin is not None else E.dram("x", [TOKENS, D], kind="ExternalInput")
    rwT_in = E.dram("rwT", [1024, 8], kind="ExternalInput")
    gw_in = E.dram("gw", [8, 1024, 2048], kind="ExternalInput")
    gb_in = E.dram("gb", [8, 2048], kind="ExternalInput")
    uw_in = E.dram("uw", [8, 1024, 2048], kind="ExternalInput")
    ub_in = E.dram("ub", [8, 2048], kind="ExternalInput")
    dw_in = E.dram("dw", [8, 2048, 1024], kind="ExternalInput")
    db_in = E.dram("db", [8, 1024], kind="ExternalInput")
    x_out = xout if xout is not None else E.dram("xout", [TOKENS, D], kind="ExternalOutput")

    NTT = TOKENS // 128
    Q = 1024
    NQ = TOKENS // Q
    QT = Q // 128

    xs = E.sbuf("xs", [128, 4, D])
    lnT = E.sbuf("lnT", [128, 8, 1024], BF16)
    yacc = E.sbuf("yacc", [128, 8, D])
    h_bf = E.sbuf("hbf", [128, 16, 1024], BF16)
    gw_sl = E.sbuf("gwsl", [128, 8, 1024], BF16)
    uw_sl = E.sbuf("uwsl", [128, 8, 1024], BF16)
    dw_sl = E.sbuf("dwsl", [128, 16, 512], BF16)
    silu_g = E.sbuf("silug", [128, 4, 512])
    wnorm = E.sbuf("wnorm", [128, 8, 8])
    probs = E.sbuf("probs", [128, 8])
    pcol = E.sbuf("pcol", [128, 6])
    pmask = E.sbuf("pmask", [128, 8])
    pm2 = E.sbuf("pm2", [128, 8])
    grow = E.sbuf("grow", [1, 2048], BF16)
    urow = E.sbuf("urow", [1, 2048], BF16)
    drow = E.sbuf("drow", [1, 1024], BF16)
    ones_row = E.sbuf("onesrow", [1, 512], BF16)
    rwT = E.sbuf("rwTsb", [128, 8, 8])
    lnT32 = E.sbuf("lnT32", [128, 8, 128])
    st = alloc_ln_stats(E, 4)
    iden = emit_identity(E)
    pads = [E.psum(f"p{i}", [128, 512]) for i in range(8)]
    prt = pads[7]

    def f0(v):
        v.memset(ones_row[:, :], 1.0)
    E.blk(vector=f0)

    def frw(sy):
        for k in range(8):
            E.dma(sy, rwT[:, k, :], rwT_in[k * 128:(k + 1) * 128, :])
        E.dwait(sy)
    E.blk(sync=frw)

    Q = 1024
    NQ = TOKENS // Q
    QT = Q // 128

    for q in range(NQ):
        for grp in range(2):
            tt0 = q * QT + grp * 4

            def fx(sy, tt0=tt0):
                for j in range(4):
                    E.dma(sy, xs[:, j, :], x_in[(tt0 + j) * 128:(tt0 + j + 1) * 128, :])
                E.dwait(sy)
            E.blk(sync=fx)

            emit_ln(E, xs, 4, st, xs)

            for j in range(4):
                lt = grp * 4 + j

                def ftr2(t, j=j):
                    for c in range(8):
                        t.transpose(pads[c][:, 0:128], xs[:, j, c * 128:(c + 1) * 128],
                                    iden[:, :])
                E.blk(tensor=ftr2)

                def fte(v, lt=lt):
                    for c in range(8):
                        v.copy(lnT[:, c, lt * 128:(lt + 1) * 128], pads[c][:, 0:128],
                               nodep=True)
                        v.copy(lnT32[:, c, :], pads[c][:, 0:128], nodep=True)
                E.blk(vector=fte)

                def frm(t, lt=lt):
                    for k in range(8):
                        t.mm(prt[:, 0:8], lnT32[:, k, :], rwT[:, k, :],
                             start=(k == 0), stop=(k == 7))
                E.blk(tensor=frm)

                def fsm(s):
                    s.act(probs[:, :], prt[:, 0:8], AF.Exp, accum_out=pcol[:, 0:1])
                E.blk(scalar=fsm)

                def fv1(v, lt=lt):
                    v.recip(pcol[:, 1:2], pcol[:, 0:1])
                    v.ts(probs[:, :], probs[:, :], pcol[:, 1:2], None, ALU.mult)
                    v.red(pcol[:, 2:3].unsqueeze(1), probs[:, :].unsqueeze(1), ALU.max)
                    v.tss(pmask[:, :], probs[:, :], pcol[:, 2:3], ALU.is_ge)
                    v.stt(pm2[:, :], pmask[:, :], -1e30, probs[:, :], ALU.mult, ALU.add)
                    v.red(pcol[:, 3:4].unsqueeze(1), pm2[:, :].unsqueeze(1), ALU.max)
                    v.tss(pm2[:, :], pm2[:, :], pcol[:, 3:4], ALU.is_ge)
                    v.tt(pmask[:, :], pmask[:, :], pm2[:, :], ALU.add)
                    v.tt(probs[:, :], probs[:, :], pmask[:, :], ALU.mult)
                    v.red(pcol[:, 4:5].unsqueeze(1), probs[:, :].unsqueeze(1), ALU.add)
                    v.ts(pcol[:, 4:5], pcol[:, 4:5], 1.0, 1e-9, ALU.mult, ALU.add)
                    v.recip(pcol[:, 5:6], pcol[:, 4:5])
                    v.ts(wnorm[:, lt, :], probs[:, :], pcol[:, 5:6], None, ALU.mult)
                E.blk(vector=fv1)

        def fz(v):
            v.memset(yacc[:, :, :], 0.0)
        E.blk(vector=fz)

        for e in range(8):
            def frows(g, e=e):
                E.dma(g, grow[:, :], gb_in[e:e + 1, :])
                E.dma(g, urow[:, :], ub_in[e:e + 1, :])
                E.dma(g, drow[:, :], db_in[e:e + 1, :])
                E.dwait(g)
            E.blk(gpsimd=frows)

            for fh in range(2):
                def fgw(g, e=e, fh=fh):
                    for k in range(8):
                        E.dma(g, gw_sl[:, k, :],
                              gw_in[e, k * 128:(k + 1) * 128,
                                    fh * 1024:(fh + 1) * 1024])
                        E.dma(g, uw_sl[:, k, :],
                              uw_in[e, k * 128:(k + 1) * 128,
                                    fh * 1024:(fh + 1) * 1024])
                    E.dwait(g)
                E.blk(gpsimd=fgw)

                for fgrp in range(4):
                    fs = [fgrp * 2, fgrp * 2 + 1]

                    def fmm(t, fs=fs, fh=fh):
                        for fi, f_in in enumerate(fs):
                            fg_off = f_in * 128
                            fglob = fh * 1024 + fg_off
                            for th in range(2):
                                tban = fi * 4 + th * 2
                                pg, pu = pads[tban], pads[tban + 1]
                                for k in range(8):
                                    t.mm(pg[:, :], gw_sl[:, k, fg_off:fg_off + 128],
                                         lnT[:, k, th * 512:(th + 1) * 512],
                                         start=(k == 0), stop=False)
                                    t.mm(pu[:, :], uw_sl[:, k, fg_off:fg_off + 128],
                                         lnT[:, k, th * 512:(th + 1) * 512],
                                         start=(k == 0), stop=False)
                                t.mm(pg[:, :], grow[:, fglob:fglob + 128],
                                     ones_row[:, :], start=False, stop=True)
                                t.mm(pu[:, :], urow[:, fglob:fglob + 128],
                                     ones_row[:, :], start=False, stop=True)
                    E.blk(tensor=fmm)

                    def fsl(s, fs=fs):
                        for fi in range(2):
                            for th in range(2):
                                s.act(silu_g[:, fi * 2 + th, :],
                                      pads[fi * 4 + th * 2][:, :], AF.Silu,
                                      nodep=True)
                    E.blk(scalar=fsl)

                    def fhm(v, fs=fs, fh=fh):
                        for fi, f_in in enumerate(fs):
                            fglob_c = fh * 8 + f_in
                            for th in range(2):
                                v.tt(h_bf[:, fglob_c, th * 512:(th + 1) * 512],
                                     pads[fi * 4 + th * 2 + 1][:, :],
                                     silu_g[:, fi * 2 + th, :], ALU.mult, nodep=True)
                    E.blk(vector=fhm)

            for dh in range(2):
                def fdw(g, e=e, dh=dh):
                    for fk in range(16):
                        E.dma(g, dw_sl[:, fk, :],
                              dw_in[e, fk * 128:(fk + 1) * 128,
                                    dh * 512:(dh + 1) * 512])
                    E.dwait(g)
                E.blk(gpsimd=fdw)

                def fdm(t, dh=dh):
                    for fk in range(16):
                        for tt in range(QT):
                            t.mm(pads[tt][:, :], h_bf[:, fk, tt * 128:(tt + 1) * 128],
                                 dw_sl[:, fk, :], start=(fk == 0), stop=False)
                    for tt in range(QT):
                        t.mm(pads[tt][:, :], ones_row[:, 0:128],
                             drow[:, dh * 512:(dh + 1) * 512], start=False, stop=True)
                E.blk(tensor=fdm)

                def fde(v, dh=dh, e=e):
                    for tt in range(QT):
                        v.stt(yacc[:, tt, dh * 512:(dh + 1) * 512], pads[tt][:, :],
                              wnorm[:, tt, e:e + 1],
                              yacc[:, tt, dh * 512:(dh + 1) * 512],
                              ALU.mult, ALU.add, nodep=True)
                E.blk(vector=fde)

        for g4 in range(2):
            def fxr(sy, q=q, g4=g4):
                for j in range(4):
                    tt = q * QT + g4 * 4 + j
                    E.dma(sy, xs[:, j, :], x_in[tt * 128:(tt + 1) * 128, :])
                E.dwait(sy)
            E.blk(sync=fxr)

            def fadd(v, g4=g4):
                for j in range(4):
                    v.tt(xs[:, j, :], xs[:, j, :], yacc[:, g4 * 4 + j, :], ALU.add,
                         nodep=True)
            E.blk(vector=fadd)

            def fxo(sy, q=q, g4=g4):
                for j in range(4):
                    tt = q * QT + g4 * 4 + j
                    E.dma(sy, x_out[tt * 128:(tt + 1) * 128, :], xs[:, j, :])
                E.dwait(sy)
            E.blk(sync=fxo)

    E.pop_scope()
    if standalone:
        E.finish()
        return E.nc
    return None


def launch_l2_ffn(state, inp):
    nc = build_l2_ffn()
    rwT = np.ascontiguousarray(inp["L2_ffn_router_w"].T)
    in_maps = [{"x": state[i], "rwT": rwT,
                "gw": inp["L2_ffn_gate_w"], "gb": inp["L2_ffn_gate_b"],
                "uw": inp["L2_ffn_up_w"], "ub": inp["L2_ffn_up_b"],
                "dw": inp["L2_ffn_down_w"], "db": inp["L2_ffn_down_b"]}
               for i in range(NCORES)]
    res = run_bass_kernel_spmd(nc, in_maps, core_ids=list(range(NCORES)))
    return [np.asarray(r["xout"]) for r in res.results]



class Sem:
    def __init__(self, E, name):
        self.h = E.ctx.enter_context(E.nc.semaphore(name))
        self.c = 0

    def inc(self, inst, by=1):
        inst.then_inc(self.h, by)
        self.c += by
        return inst

    def wait(self, eng, val=None):
        eng.wait_ge(self.h, self.c if val is None else val)


def _alibi_slopes(n):
    import math
    def pow2_slopes(m):
        start = 2 ** (-2 ** (-(math.log2(m) - 3)))
        return [start * start ** i for i in range(m)]
    if math.log2(n).is_integer():
        return pow2_slopes(n)
    c = 2 ** math.floor(math.log2(n))
    return pow2_slopes(c) + _alibi_slopes(2 * c)[0::2][:n - c]


def build_l1_attn(E=None, xin=None, xout=None, prefix=""):
    standalone = E is None
    if standalone:
        E = Em("l1attn")
    E.prefix = prefix
    E.push_scope()
    x_in = xin if xin is not None else E.dram("x", [TOKENS, D], kind="ExternalInput")
    w_in = E.dram("w", [1024, 3072], kind="ExternalInput")
    b_in = E.dram("b", [3072], kind="ExternalInput")
    wo_in = E.dram("wo", [1024, 1024], kind="ExternalInput")
    bo_in = E.dram("bo", [1024], kind="ExternalInput")
    bq_in = E.dram("bq", [4, 128, 512], kind="ExternalInput")
    x_out = xout if xout is not None else E.dram("xout", [TOKENS, D], kind="ExternalOutput")

    H = 16
    L = SEQ
    NT = L // 128
    slopes = _alibi_slopes(H)

    xs = E.sbuf("xs", [128, 2, D])
    ln_bf = E.sbuf("lnbf", [128, 2, D], BF16)
    lnT = E.sbuf("lnT", [128, 8, L], BF16)
    qkvT = E.sbuf("qkvT", [128, 12, L], BF16)
    outT = E.sbuf("outT", [128, 8, L], BF16)
    wstr = E.sbuf("wstr", [128, 8, 512], BF16)
    browf = E.sbuf("brow", [1, 512])
    borow = E.sbuf("borow", [1, 512], BF16)
    borow2 = E.sbuf("borow2", [1, 512], BF16)
    ones_row = E.sbuf("onesr", [1, 512], BF16)
    onesf = E.sbuf("onesf", [1, 512])
    arow = E.sbuf("arow", [1, L])
    negf = E.sbuf("negf", [1, 512])
    bq_t = E.sbuf("bqt", [128, 4, 512])
    expS = E.sbuf("expS", [128, L], BF16)
    expF = E.sbuf("expF", [128, 512])
    expST = E.sbuf("expST", [128, 16, 128], BF16)
    vtok = E.sbuf("vtok", [128, 16, 128], BF16)
    ones_col = E.sbuf("onescol", [128, 1], BF16)
    drow0 = E.sbuf("drow0", [1, 128])
    onesf64 = E.sbuf("onesf64", [1, 64])
    otmp = E.sbuf("otmp", [64, 128])
    wo_sb = E.sbuf("wosb", [128, 8, 1024], BF16)
    st = alloc_ln_stats(E, 2)
    pads = [E.psum(f"p{i}", [128, 512]) for i in range(8)]
    pav = pads[6]
    pbc = pads[7]

    s_s = E.sem("ss")
    s_d = E.sem("sd")
    s_a = E.sem("sa")
    s_t = E.sem("st2")
    s_av = E.sem("sav")
    s_n1 = E.sem("sn1")
    s_n2 = E.sem("sn2")
    s_nf = E.sem("snf")

    def finit(v):
        v.memset(ones_row[:, :], 1.0)
        v.memset(onesf[:, :], 1.0)
        v.memset(onesf64[:, :], 1.0)
        v.memset(negf[:, :], -1.0)
        v.memset(ones_col[:, :], 1.0)
    E.blk(vector=finit)



    def fld(sy):
        for c in range(4):
            E.dma(sy, bq_t[:, c, :], bq_in[c])
        E.dwait(sy)
    E.blk(sync=fld)

    def fldb(g):
        E.dma(g, borow[:, :], bo_in[0:512].unsqueeze(0))
        E.dma(g, borow2[:, :], bo_in[512:1024].unsqueeze(0))
        for c in range(8):
            E.dma(g, wo_sb[:, c, :], wo_in[c * 128:(c + 1) * 128, :])
        E.dwait(g)
    E.blk(gpsimd=fldb)

    sched = []
    for qt in range(NT):
        nw = qt // 4 + 1
        for w in range(nw):
            sched.append((qt, w, w == nw - 1))
    NWIN = len(sched)
    cumw = {}
    cw = 0
    for qt in range(NT):
        cw += qt // 4 + 1
        cumw[qt] = cw
    cumtr = {}
    ct = 0
    for qt in range(NT):
        ct += qt + 1
        cumtr[qt] = ct

    for s in range(B_PER_CORE):
        base = s * L

        for grp in range(NT // 2):
            tt0 = grp * 2

            def fx(sy, tt0=tt0):
                for j in range(2):
                    E.dma(sy, xs[:, j, :],
                          x_in[base + (tt0 + j) * 128: base + (tt0 + j + 1) * 128, :])
                E.dwait(sy)
            E.blk(sync=fx)

            emit_ln(E, xs, 2, st, ln_bf)

            def ftr(sy, tt0=tt0):
                for j in range(2):
                    for c in range(8):
                        E.dma_t(sy, lnT[:, c, (tt0 + j) * 128:(tt0 + j + 1) * 128],
                                ln_bf[:, j, c * 128:(c + 1) * 128])
                E.dwait(sy)
            E.blk(sync=ftr)

        for hg in range(2):
            for p3 in range(3):
                fbase = p3 * 1024 + hg * 512

                def fw(g, fbase=fbase):
                    for k in range(8):
                        E.dma(g, wstr[:, k, 0:512],
                              w_in[k * 128:(k + 1) * 128, fbase:fbase + 512])
                    E.dwait(g)
                E.blk(gpsimd=fw)

                def fwb(sy, fbase=fbase):
                    E.dma(sy, browf[:, 0:512], b_in[fbase:fbase + 512].unsqueeze(0))
                    E.dwait(sy)
                E.blk(sync=fwb)

                for cpair in range(2):
                    def fmm(t, cpair=cpair, p3=p3):
                        for ci in range(2):
                            cl = cpair * 2 + ci
                            for tw in range(4):
                                p = pads[ci * 4 + tw]
                                for k in range(8):
                                    t.mm(p[:, :], wstr[:, k, cl * 128:(cl + 1) * 128],
                                         lnT[:, k, tw * 512:(tw + 1) * 512],
                                         start=(k == 0), stop=False)
                                t.mm(p[:, :],
                                     browf[:, cl * 128:(cl + 1) * 128],
                                     onesf[:, :], start=False, stop=True)
                    E.blk(tensor=fmm)

                    def fev(s_, cpair=cpair, p3=p3):
                        for ci in range(2):
                            cl = cpair * 2 + ci
                            c = p3 * 4 + cl
                            sc = 0.125 if p3 == 0 else 1.0
                            for tw in range(4):
                                s_.act(qkvT[:, c, tw * 512:(tw + 1) * 512],
                                       pads[ci * 4 + tw][:, :], AF.Copy, scale=sc,
                                       nodep=True)
                    E.blk(scalar=fev)

            for hl in range(8):
                h = hg * 8 + hl
                slope = float(slopes[h])
                qc, qr = hl // 2, (hl % 2) * 64
                kc_, kr = 4 + hl // 2, (hl % 2) * 64
                vc, vr = 8 + hl // 2, (hl % 2) * 64
                oc, orow = h // 2, (h % 2) * 64

                s_s0, s_d0, s_a0 = s_s.c, s_d.c, s_a.c
                s_t0, s_av0 = s_t.c, s_av.c
                s_n10, s_n20, s_nf0 = s_n1.c, s_n2.c, s_nf.c
                diag_idx = [i for i, (_, _, d) in enumerate(sched) if d]

                def grows(e):
                    g = Eng(e)
                    g.iota(arow[:, :], [[1, L]], channel_multiplier=0)
                E.blk(gpsimd=grows)

                def vrows(e):
                    e.ts(arow[:, :], arow[:, :], -slope, None, ALU.mult)
                E.blk(vector=vrows)

                def tsec(t):
                    for i, (qt, w, diag) in enumerate(sched):
                        q0 = qt * 128
                        k0 = w * 512
                        b = pads[i % 6]
                        if i >= 6:
                            t.e.wait_ge(s_a.h, s_a0 + i - 5)
                        if diag:
                            t.mm(b[:, :], qkvT[qr:qr + 64, qc, q0:q0 + 128],
                                 qkvT[kr:kr + 64, kc_, k0:k0 + 512],
                                 start=True, stop=True).then_inc(s_s.h, 1)
                        else:
                            t.mm(b[:, :], qkvT[qr:qr + 64, qc, q0:q0 + 128],
                                 qkvT[kr:kr + 64, kc_, k0:k0 + 512],
                                 start=True, stop=False)
                            t.mm(b[:, :], arow[:, q0:q0 + 128], onesf[:, :],
                                 start=False, stop=False)
                            t.mm(b[:, :], negf[:, 0:128], arow[:, k0:k0 + 512],
                                 start=False, stop=True).then_inc(s_s.h, 1)
                        if diag:
                            t.e.wait_ge(s_t.h, s_t0 + 16 * (16 + cumtr[qt]))
                            if qt > 0:
                                t.e.wait_ge(s_nf.h, s_nf0 + qt)
                            for kk in range(qt + 1):
                                t.mm(pav[0:64, 0:128], vtok[:, kk, vr:vr + 64],
                                     expST[:, kk, :], start=(kk == 0),
                                     stop=(kk == qt))
                                mmden = t.mm(pbc[0:1, 0:128], ones_col[:, :],
                                             expST[:, kk, :], start=(kk == 0),
                                             stop=(kk == qt))
                                if kk == qt:
                                    mmden.then_inc(s_av.h, 1)
                            t.e.wait_ge(s_n1.h, s_n10 + qt + 1)
                            t.mm(pbc[0:64, 0:128], onesf64[:, :],
                                 drow0[:, :]).then_inc(s_n2.h, 1)

                def vsec(e):
                    v = e
                    nd = 0
                    for i, (qt, w, diag) in enumerate(sched):
                        if diag:
                            if nd > 0:
                                v.e.wait_ge(s_a.h, s_a0 + prev_consumed[0])
                            v.e.wait_ge(s_s.h, s_s0 + i + 1)
                            cls = qt % 4
                            v.stt(expF[:, :], bq_t[:, cls, :], slope,
                                  pads[i % 6][:, :], ALU.mult, ALU.add)
                            v.e.drain()
                            v.e.sem_inc(s_d.h, 1)
                            nd += 1
                            prev_consumed[0] = cumw[qt]
                            v.e.wait_ge(s_av.h, s_av0 + qt + 1)
                            v.copy(drow0[:, :], pbc[0:1, 0:128])
                            v.recip(drow0[:, :], drow0[:, :])
                            v.e.drain()
                            v.e.sem_inc(s_n1.h, 1)
                            v.e.wait_ge(s_n2.h, s_n20 + qt + 1)
                            v.copy(otmp[:, :], pav[0:64, 0:128])
                            v.tt(outT[orow:orow + 64, oc, qt * 128:(qt + 1) * 128],
                                 otmp[:, :], pbc[0:64, 0:128], ALU.mult)
                            v.e.drain()
                            v.e.sem_inc(s_nf.h, 1)
                prev_consumed = [0]

                def asec(e):
                    sC = e
                    ndone = 0
                    for i, (qt, w, diag) in enumerate(sched):
                        k0 = w * 512
                        if w == 0 and qt >= 1:
                            sC.e.wait_ge(s_t.h, s_t0 + 16 * (16 + cumtr[qt - 1]))
                        if diag:
                            ndone += 1
                            sC.e.wait_ge(s_d.h, s_d0 + ndone)
                            sC.act(expS[:, k0:k0 + 512], expF[:, :], AF.Exp)
                        else:
                            sC.e.wait_ge(s_s.h, s_s0 + i + 1)
                            sC.act(expS[:, k0:k0 + 512], pads[i % 6][:, :], AF.Exp)
                        sC.e.drain()
                        sC.e.sem_inc(s_a.h, 1)

                def ssec2(sy):
                    for kk in range(16):
                        sy.dma_start(out=vtok[:, kk, :],
                                     in_=qkvT[:, vc, kk * 128:(kk + 1) * 128],
                                     transpose=True).then_inc(s_t.h, 16)
                    for qt in range(NT):
                        sy.wait_ge(s_a.h, s_a0 + cumw[qt])
                        if qt > 0:
                            sy.wait_ge(s_av.h, s_av0 + qt)
                        for kk in range(qt + 1):
                            sy.dma_start(
                                out=expST[:, kk, :],
                                in_=expS[:, kk * 128:(kk + 1) * 128],
                                transpose=True).then_inc(s_t.h, 16)

                E.blk(tensor=tsec, vector=vsec, scalar=asec, sync=ssec2)
                s_s.c += NWIN
                s_d.c += NT
                s_a.c += NWIN
                s_t.c += 16 * (16 + cumtr[NT - 1])
                s_av.c += NT
                s_n1.c += NT
                s_n2.c += NT
                s_nf.c += NT

        for tg in range(8):
            def fxr(sy, tg=tg):
                for j in range(2):
                    tt = tg * 2 + j
                    E.dma(sy, xs[:, j, :],
                          x_in[base + tt * 128: base + (tt + 1) * 128, :])
                E.dwait(sy)
            E.blk(sync=fxr)

            def fom(t, tg=tg):
                for j in range(2):
                    tt = tg * 2 + j
                    for dh in range(2):
                        p = pads[j * 2 + dh]
                        for c in range(8):
                            t.mm(p[:, :], outT[:, c, tt * 128:(tt + 1) * 128],
                                 wo_sb[:, c, dh * 512:(dh + 1) * 512],
                                 start=(c == 0), stop=False)
                        t.mm(p[:, :], ones_row[:, 0:128],
                             (borow if dh == 0 else borow2)[:, :],
                             start=False, stop=True)
            E.blk(tensor=fom)

            def fde(v, tg=tg):
                for j in range(2):
                    for dh in range(2):
                        v.tt(xs[:, j, dh * 512:(dh + 1) * 512],
                             xs[:, j, dh * 512:(dh + 1) * 512],
                             pads[j * 2 + dh][:, :], ALU.add, nodep=True)
            E.blk(vector=fde)

            def fxo(sy, tg=tg):
                for j in range(2):
                    tt = tg * 2 + j
                    E.dma(sy, x_out[base + tt * 128: base + (tt + 1) * 128, :],
                          xs[:, j, :])
                E.dwait(sy)
            E.blk(sync=fxo)

    E.pop_scope()
    if standalone:
        E.finish()
        return E.nc
    return None


def launch_l1_attn(state, inp):
    nc = build_l1_attn()
    bq = np.zeros((4, 128, 512), np.float32)
    f = np.arange(512)[None, :]
    p = np.arange(128)[:, None]
    for c in range(4):
        rel = f - p - 128.0 * c
        bq[c] = np.where(rel <= 0, rel, -1e30)
    in_maps = [{"x": state[i], "w": inp["L1_attn_w_in"], "b": inp["L1_attn_b_in"],
                "wo": inp["L1_attn_w_out"], "bo": inp["L1_attn_b_out"],
                "bq": bq} for i in range(NCORES)]
    res = run_bass_kernel_spmd(nc, in_maps, core_ids=list(range(NCORES)))
    return [np.asarray(r["xout"]) for r in res.results]



def emit_mamba_chunk_scan(E, sems, delta_c, u_c, A_col, B_bc, C_bc, y_out,
                          dA2, xin2, h2, du, ytmp, NS, hinit=None, hsave=None,
                          B_src=None, C_src=None):
    s_da, s_xi, s_sc, s_y = sems["da"], sems["xi"], sems["sc"], sems["y"]
    da0, xi0, sc0, y0 = s_da.c, s_xi.c, s_sc.c, s_y.c
    stream = B_src is not None
    if stream:
        sbp = [sems["b0"], sems["b1"]]
        scp = [sems["c0"], sems["c1"]]
        sbp0 = [sbp[0].c, sbp[1].c]
        scp0 = [scp[0].c, scp[1].c]
        Bv = lambda n: B_bc[n % 2][:, :]
        Cv = lambda n: C_bc[n % 2][:, :]
    else:
        Bv, Cv = B_bc, C_bc
    T = delta_c.shape[-1]

    def asec(sC):
        for n in range(NS):
            if n >= 2:
                sC.e.wait_ge(s_sc.h, sc0 + n - 1)
            sC.act(dA2[n % 2][:, :], delta_c, AF.Exp, scale=A_col(n), nodep=True)
            sC.e.drain()
            sC.e.sem_inc(s_da.h, 1)

    def ssec(sy):
        for n in range(NS):
            if n >= 2:
                sy.wait_ge(s_xi.h, xi0 + n - 1)
                sy.wait_ge(s_y.h, y0 + n - 1)
            sy.dma_start(out=B_bc[n % 2][:, :], in_=B_src(n)).then_inc(
                sbp[n % 2].h, 16)
            sy.dma_start(out=C_bc[n % 2][:, :], in_=C_src(n)).then_inc(
                scp[n % 2].h, 16)

    def gsec(e):
        g = Eng(e)
        g.tt(du[:, :], delta_c, u_c, ALU.mult)
        g.e.drain()
        for n in range(NS):
            if n >= 2:
                e.wait_ge(s_sc.h, sc0 + n - 1)
            if stream:
                e.wait_ge(sbp[n % 2].h, sbp0[n % 2] + 16 * (n // 2 + 1))
            g.tt(xin2[n % 2][:, :], du[:, :], Bv(n), ALU.mult, nodep=True)
            e.drain()
            e.sem_inc(s_xi.h, 1)

    def vsec(v):
        for n in range(NS):
            v.e.wait_ge(s_da.h, da0 + n + 1)
            v.e.wait_ge(s_xi.h, xi0 + n + 1)
            init = 0.0 if hinit is None else hinit(n)
            v.scan(h2[n % 2][:, :], dA2[n % 2][:, :], xin2[n % 2][:, :], init)
            v.e.drain()
            v.e.sem_inc(s_sc.h, 1)
            if stream:
                v.e.wait_ge(scp[n % 2].h, scp0[n % 2] + 16 * (n // 2 + 1))
            if n == 0:
                v.tt(y_out, h2[0][:, :], Cv(0), ALU.mult)
            else:
                v.tt(ytmp[:, :], h2[n % 2][:, :], Cv(n), ALU.mult)
                v.tt(y_out, y_out, ytmp[:, :], ALU.add)
            if hsave is not None:
                v.copy(hsave(n), h2[n % 2][:, T - 1:T], nodep=True)
            v.e.drain()
            v.e.sem_inc(s_y.h, 1)

    if stream:
        E.blk(scalar=asec, vector=vsec, gpsimd=gsec, sync=ssec)
        for n in range(NS):
            sbp[n % 2].c += 16
            scp[n % 2].c += 16
    else:
        E.blk(scalar=asec, vector=vsec, gpsimd=gsec)
    s_da.c += NS
    s_xi.c += NS
    s_sc.c += NS
    s_y.c += NS



def build_l2_attn(E=None, xin=None, xout=None, prefix=""):
    standalone = E is None
    if standalone:
        E = Em("l2attn")
    E.prefix = prefix
    E.push_scope()
    x_in = xin if xin is not None else E.dram("x", [TOKENS, D], kind="ExternalInput")
    ipT_in = E.dram("ipT", [1024, 4096], kind="ExternalInput")
    cw_in = E.dram("cw", [128, 64], kind="ExternalInput")
    cb_in = E.dram("cb", [128, 16], kind="ExternalInput")
    xpwT_in = E.dram("xpwT", [2048, 80], kind="ExternalInput")
    dtwT_in = E.dram("dtwT", [64, 2048], kind="ExternalInput")
    dtb_in = E.dram("dtb", [2048], kind="ExternalInput")
    alog_in = E.dram("alog", [128, 128], kind="ExternalInput")
    ut_in = E.dram("ut", [8, 2048], kind="ExternalInput")
    v_in = E.dram("vp", [8, 8], kind="ExternalInput")
    ls_in = E.dram("ls", [1, 1], kind="ExternalInput")
    dcol_in = E.dram("dcol", [128, 16], kind="ExternalInput")
    opT_in = E.dram("opT", [2048, 1024], kind="ExternalInput")
    x_out = xout if xout is not None else E.dram("xout", [TOKENS, D], kind="ExternalOutput")

    NC = 16
    T = SEQ
    TH = T // 2

    xz_dram = E.dram("xzscr", [32, 128, T], BF16)
    u_dram = E.dram("uscr", [NC, 128, T], BF16)
    y_dram = E.dram("yscr", [NC, 128, T], BF16)
    bc_dram = E.dram("bcscr", [16, T], BF16)

    xs = E.sbuf("xs", [128, 2, D])
    st = alloc_ln_stats(E, 2)
    ln_bf = E.sbuf("lnbf", [128, 2, D], BF16)
    delta_c = E.sbuf("deltac", [128, TH])
    du = E.sbuf("du", [128, TH])
    dA2 = [E.sbuf(f"dA{i}", [128, TH]) for i in range(2)]
    xin2 = [E.sbuf(f"xin{i}", [128, TH]) for i in range(2)]
    h2 = [E.sbuf(f"hh{i}", [128, TH]) for i in range(2)]
    ytmp = E.sbuf("ytmp", [128, TH])
    y_c = E.sbuf("yc", [128, TH])
    hsave_t = E.sbuf("hsavet", [128, NC, 8])
    B_bc = E.sbuf("Bbc", [128, 8, TH], BF16)
    C_bc = E.sbuf("Cbc", [128, 8, TH], BF16)
    zs = E.sbuf("zs", [128, TH])
    zbf = E.sbuf("zbf", [128, TH], BF16)
    ybf = E.sbuf("ybf", [128, TH], BF16)
    u_c = E.sbuf("uc", [128, TH], BF16)
    xc_bf = E.sbuf("xcbf", [128, 3 + T], BF16)
    uacc = E.sbuf("uacc", [128, T])
    A_t = E.sbuf("At", [128, NC, 8])
    dcol_t = E.sbuf("dcolt", [128, NC])
    cw_t = E.sbuf("cwt", [128, NC * 4])
    cb_t = E.sbuf("cbt", [128, NC])
    ls_col = E.sbuf("lscol", [128, 1])
    ls_one = E.sbuf("lsone", [1, 1], BF16)
    onesb = E.sbuf("onesb", [1, 128], BF16)
    dtb_row = E.sbuf("dtbrow", [1, 2048])
    onesf = E.sbuf("onesf2", [1, 512])
    dtlr = E.sbuf("dtlr", [64, T], BF16)
    bc_t = E.sbuf("bct", [16, T], BF16)
    bcrow2 = [E.sbuf(f"bcrow{i}", [1, TH], BF16) for i in range(2)]
    dtw_sb = E.sbuf("dtwsb", [64, 2048], BF16)
    xpw_sb = E.sbuf("xpwsb", [128, 16, 80], BF16)
    ipsb = E.sbuf("ipsb", [128, 8, 256], BF16)
    zstage = E.sbuf("zstage", [128, T], BF16)
    u_sb2 = [E.sbuf(f"usb{i}", [128, T], BF16) for i in range(2)]
    utsb = E.sbuf("utsb", [8, 2048], BF16)
    alog_sb = E.sbuf("alogsb", [128, 128])
    vsb = E.sbuf("vsb", [8, 8], BF16)
    pads = [E.psum(f"p{i}", [128, 512]) for i in range(8)]
    sems = {k: E.sem(f"sm_{k}") for k in ["da", "xi", "sc", "y", "r0", "r1",
                                           "m", "e2", "xd0", "xd1", "xm"]}

    def f0(v):
        v.memset(onesb[:, :], 1.0)
        v.memset(onesf[:, :], 1.0)
        v.memset(xc_bf[:, 0:3], 0.0)
    E.blk(vector=f0)
    E.constf(LN_EPS)

    def fld(sy):
        E.dma(sy, cw_t[:, :], cw_in)
        E.dma(sy, cb_t[:, :], cb_in)
        E.dma(sy, dcol_t[:, :], dcol_in)
        E.dma(sy, dtb_row[:, :], dtb_in.unsqueeze(0))
        E.dma(sy, alog_sb[:, :], alog_in)
        E.dwait(sy)
    E.blk(sync=fld)

    def fldg(g):
        E.dma(g, ls_one[:, :], ls_in)
        E.dma(g, utsb[:, :], ut_in)
        E.dma(g, vsb[:, :], v_in)
        E.dma(g, dtw_sb[:, :], dtwT_in)
        for c in range(16):
            E.dma(g, xpw_sb[:, c, :], xpwT_in[c * 128:(c + 1) * 128, :])
        E.dwait(g)
    E.blk(gpsimd=fldg)

    def fls(t):
        t.mm(pads[0][:, 0:1], onesb[:, :], ls_one[:, :])
    E.blk(tensor=fls)

    def flse(v):
        v.ts(ls_col[:, :], pads[0][:, 0:1], 1.0 / (8.0 ** 0.5), None, ALU.mult)
    E.blk(vector=flse)

    for cg in range(2):
        def fuv(t, cg=cg):
            for ci in range(8):
                c = cg * 8 + ci
                t.mm(pads[ci][:, 0:8], utsb[:, c * 128:(c + 1) * 128], vsb[:, :])
        E.blk(tensor=fuv)

        def fexp(sC, cg=cg):
            for ci in range(8):
                c = cg * 8 + ci
                sC.act(A_t[:, c, :], alog_sb[:, c * 8:(c + 1) * 8], AF.Exp,
                       nodep=True)
        E.blk(scalar=fexp)

        def fcomb(v, cg=cg):
            for ci in range(8):
                c = cg * 8 + ci
                v.stt(A_t[:, c, :], pads[ci][:, 0:8], ls_col[:, :],
                      A_t[:, c, :], ALU.mult, ALU.subtract, nodep=True)
        E.blk(vector=fcomb)

    for s in range(B_PER_CORE):
        base = s * T

        with E.scoped(f"lnT{s}", [128, 8, T], BF16) as lnT:
            for grp in range(8):
                tt0 = grp * 2

                def fx(sy, tt0=tt0):
                    for j in range(2):
                        E.dma(sy, xs[:, j, :],
                              x_in[base + (tt0 + j) * 128:
                                   base + (tt0 + j + 1) * 128, :])
                    E.dwait(sy)
                E.blk(sync=fx)

                emit_ln(E, xs, 2, st, ln_bf)

                def ftr(sy, tt0=tt0):
                    for j in range(2):
                        for c in range(8):
                            E.dma_t(sy, lnT[:, c, (tt0 + j) * 128:
                                            (tt0 + j + 1) * 128],
                                    ln_bf[:, j, c * 128:(c + 1) * 128])
                    E.dwait(sy)
                E.blk(sync=ftr)

            for eg in range(16):
                def fwz(g, eg=eg):
                    for k in range(8):
                        E.dma(g, ipsb[:, k, :],
                              ipT_in[k * 128:(k + 1) * 128,
                                     eg * 256:(eg + 1) * 256])
                    E.dwait(g)
                E.blk(gpsimd=fwz)

                for ci in range(2):
                    def fmm(t, eg=eg, ci=ci):
                        for tw in range(4):
                            p = pads[ci * 4 + tw]
                            for k in range(8):
                                t.mm(p[:, :], ipsb[:, k, ci * 128:(ci + 1) * 128],
                                     lnT[:, k, tw * 512:(tw + 1) * 512],
                                     start=(k == 0), stop=(k == 7))
                    E.blk(tensor=fmm)

                    def fev(sC, ci=ci):
                        for tw in range(4):
                            sC.act(zstage[:, tw * 512:(tw + 1) * 512],
                                   pads[ci * 4 + tw][:, :], AF.Copy, nodep=True)
                    E.blk(scalar=fev)

                    def fzo(sy, eg=eg, ci=ci):
                        E.dma(sy, xz_dram[eg * 2 + ci], zstage[:, :])
                        E.dwait(sy)
                    E.blk(sync=fzo)

        for c in range(NC):
            def fxc(sy, c=c):
                E.dma(sy, xc_bf[:, 3:3 + T], xz_dram[c])
                E.dwait(sy)
            E.blk(sync=fxc)

            def fcv(v, c=c):
                v.ts(uacc[:, :], xc_bf[:, 3:3 + T], cw_t[:, c * 4 + 3:c * 4 + 4],
                     None, ALU.mult)
                for k in range(3):
                    v.stt(uacc[:, :], xc_bf[:, k:k + T],
                          cw_t[:, c * 4 + k:c * 4 + k + 1], uacc[:, :],
                          ALU.mult, ALU.add)
            E.blk(vector=fcv)

            def fsu(sC, c=c):
                sC.act(zstage[:, :], uacc[:, :], AF.Silu, bias=cb_t[:, c:c + 1])
            E.blk(scalar=fsu)

            def fuo(sy, c=c):
                E.dma(sy, u_dram[c], zstage[:, :])
                E.dwait(sy)
            E.blk(sync=fuo)

        sxp = [sems["xd0"], sems["xd1"]]
        sm = sems["xm"]
        xdp0 = [sxp[0].c, sxp[1].c]
        xm0 = sm.c

        def fxd_s(sy):
            for k in range(NC):
                if k >= 2:
                    sy.wait_ge(sm.h, xm0 + k - 1)
                sy.dma_start(out=u_sb2[k % 2][:, :],
                             in_=u_dram[k]).then_inc(sxp[k % 2].h, 16)

        def fxd_t(t):
            for k in range(NC):
                t.e.wait_ge(sxp[k % 2].h, xdp0[k % 2] + 16 * (k // 2 + 1))
                for tw in range(4):
                    last = (tw == 3)
                    mm = t.mm(pads[tw][0:80, :], xpw_sb[:, k, :],
                              u_sb2[k % 2][:, tw * 512:(tw + 1) * 512],
                              start=(k == 0), stop=(k == NC - 1))
                    if last:
                        mm.then_inc(sm.h, 1)
        E.blk(sync=fxd_s, tensor=fxd_t)
        sxp[0].c += 16 * (NC // 2)
        sxp[1].c += 16 * (NC // 2)
        sm.c += NC

        def fxd_e(sC):
            for tw in range(4):
                sC.act(dtlr[:, tw * 512:(tw + 1) * 512], pads[tw][0:64, :],
                       AF.Copy, nodep=True)
        E.blk(scalar=fxd_e)

        def fxd_e2(v):
            for tw in range(4):
                v.copy(bc_t[:, tw * 512:(tw + 1) * 512], pads[tw][64:80, :],
                       nodep=True)
        E.blk(vector=fxd_e2)

        def fbco(sy):
            E.dma(sy, bc_dram, bc_t[:, :])
            E.dwait(sy)
        E.blk(sync=fbco)

        for th in range(2):
            t0 = th * TH
            srp = [sems["r0"], sems["r1"]]
            sm2, se2 = sems["m"], sems["e2"]
            rp0 = [srp[0].c, srp[1].c]
            m0, e0 = sm2.c, se2.c

            def fbc_g(g):
                for i in range(16):
                    if i >= 2:
                        g.wait_ge(se2.h, e0 + i - 1)
                    g.dma_start(out=bcrow2[i % 2][:, :],
                                in_=bc_dram[i:i + 1, t0:t0 + TH]).then_inc(
                                    srp[i % 2].h, 16)

            def fbc_t(t):
                for i in range(16):
                    t.e.wait_ge(srp[i % 2].h, rp0[i % 2] + 16 * (i // 2 + 1))
                    for w2 in range(2):
                        mm = t.mm(pads[(i % 2) * 2 + w2][:, :], onesb[:, :],
                                  bcrow2[i % 2][:, w2 * 512:(w2 + 1) * 512])
                        if w2 == 1:
                            mm.then_inc(sm2.h, 1)

            def fbc_v(v):
                for i in range(16):
                    v.e.wait_ge(sm2.h, m0 + i + 1)
                    dst = B_bc if i < 8 else C_bc
                    for w2 in range(2):
                        v.copy(dst[:, i % 8, w2 * 512:(w2 + 1) * 512],
                               pads[(i % 2) * 2 + w2][:, :], nodep=True)
                    v.e.drain()
                    v.e.sem_inc(se2.h, 1)
            E.blk(gpsimd=fbc_g, tensor=fbc_t, vector=fbc_v)
            srp[0].c += 128
            srp[1].c += 128
            sm2.c += 16
            se2.c += 16

            for c in range(NC):
                def fzu(sy, c=c, t0=t0):
                    E.dma(sy, u_c[:, :], u_dram[c][:, t0:t0 + TH])
                    E.dma(sy, zbf[:, :], xz_dram[16 + c][:, t0:t0 + TH])
                    E.dwait(sy)
                E.blk(sync=fzu)

                def fdm(t, c=c, t0=t0):
                    for tw in range(2):
                        p = pads[4 + tw]
                        t.mm(p[:, :], dtw_sb[:, c * 128:(c + 1) * 128],
                             dtlr[:, t0 + tw * 512:t0 + (tw + 1) * 512],
                             start=True, stop=False)
                        t.mm(p[:, :], dtb_row[:, c * 128:(c + 1) * 128],
                             onesf[:, :], start=False, stop=True)
                E.blk(tensor=fdm)

                def fsp(sC, c=c):
                    for tw in range(2):
                        sC.act(delta_c[:, tw * 512:(tw + 1) * 512],
                               pads[4 + tw][:, :], AF.Exp, nodep=True)
                    sC.e.drain()
                    for tw in range(2):
                        sC.act(delta_c[:, tw * 512:(tw + 1) * 512],
                               delta_c[:, tw * 512:(tw + 1) * 512],
                               AF.Ln, bias=1.0, nodep=True)
                E.blk(scalar=fsp)

                hin = None
                hsv = None
                if th == 0:
                    hsv = lambda n, c=c: hsave_t[:, c, n:n + 1]
                else:
                    hin = lambda n, c=c: hsave_t[:, c, n:n + 1]
                emit_mamba_chunk_scan(
                    E, sems, delta_c[:, :], u_c[:, :],
                    lambda n, c=c: A_t[:, c, n:n + 1],
                    lambda n: B_bc[:, n, :], lambda n: C_bc[:, n, :],
                    y_c[:, :], dA2, xin2, h2, du, ytmp, 8,
                    hinit=hin, hsave=hsv)

                def fsz(sC, c=c):
                    sC.act(zs[:, :], zbf[:, :], AF.Silu)
                E.blk(scalar=fsz)

                def fgt(v, c=c):
                    v.stt(y_c[:, :], u_c[:, :], dcol_t[:, c:c + 1], y_c[:, :],
                          ALU.mult, ALU.add)
                    v.tt(ybf[:, :], y_c[:, :], zs[:, :], ALU.mult)
                E.blk(vector=fgt)

                def fyo(sy, c=c, t0=t0):
                    E.dma(sy, y_dram[c][:, t0:t0 + TH], ybf[:, :])
                    E.dwait(sy)
                E.blk(sync=fyo)

        with E.scoped(f"opsb{s}", [128, 16, 512], BF16) as op_sb, \
             E.scoped(f"ysb{s}", [128, 16, 256], BF16) as ysb:
          for dh in range(2):
              def fop(g, dh=dh):
                  for c in range(16):
                      E.dma(g, op_sb[:, c, :],
                            opT_in[c * 128:(c + 1) * 128,
                                   dh * 512:(dh + 1) * 512])
                  E.dwait(g)
              E.blk(gpsimd=fop)

              for tp in range(8):
                  def fyr(sy, tp=tp, dh=dh):
                      for c in range(16):
                          E.dma(sy, ysb[:, c, :],
                                y_dram[c][:, tp * 256:(tp + 1) * 256])
                      for j in range(2):
                          tt = tp * 2 + j
                          E.dma(sy, xs[:, j, dh * 512:(dh + 1) * 512],
                                x_in[base + tt * 128:base + (tt + 1) * 128,
                                     dh * 512:(dh + 1) * 512])
                      E.dwait(sy)
                  E.blk(sync=fyr)

                  def fomm(t, tp=tp):
                      for j in range(2):
                          p = pads[j]
                          for c in range(16):
                              t.mm(p[:, :], ysb[:, c, j * 128:(j + 1) * 128],
                                   op_sb[:, c, :], start=(c == 0), stop=(c == 15))
                  E.blk(tensor=fomm)

                  def foev(v, tp=tp, dh=dh):
                      for j in range(2):
                          v.tt(xs[:, j, dh * 512:(dh + 1) * 512],
                               xs[:, j, dh * 512:(dh + 1) * 512],
                               pads[j][:, :], ALU.add, nodep=True)
                  E.blk(vector=foev)

                  def foo(sy, tp=tp, dh=dh):
                      for j in range(2):
                          tt = tp * 2 + j
                          E.dma(sy, x_out[base + tt * 128:base + (tt + 1) * 128,
                                          dh * 512:(dh + 1) * 512],
                                xs[:, j, dh * 512:(dh + 1) * 512])
                      E.dwait(sy)
                  E.blk(sync=foo)

    E.pop_scope()
    if standalone:
        E.finish()
        return E.nc
    return None


def launch_l2_attn(state, inp):
    nc = build_l2_attn()
    ipT = np.ascontiguousarray(inp["L2_attn_in_proj_w"].T)
    cw = np.ascontiguousarray(
        inp["L2_attn_conv_w"].reshape(16, 128, 4).transpose(1, 0, 2).reshape(128, 64))
    cb = np.ascontiguousarray(inp["L2_attn_conv_b"].reshape(16, 128).T)
    xpwT = np.ascontiguousarray(inp["L2_attn_x_proj_w"].T)
    dtwT = np.ascontiguousarray(inp["L2_attn_dt_proj_w"].T)
    alog = np.ascontiguousarray(
        inp["L2_attn_A_log"].reshape(16, 128, 8).transpose(1, 0, 2).reshape(128, 128))
    ut = np.ascontiguousarray(inp["L2_attn_U_param"].T)
    dcol = np.ascontiguousarray(inp["L2_attn_D"].reshape(16, 128).T)
    opT = np.ascontiguousarray(inp["L2_attn_out_proj_w"].T)
    ls = np.asarray(inp["L2_attn_low_rank_scale"], np.float32).reshape(1, 1)
    in_maps = [{"x": state[i], "ipT": ipT, "cw": cw, "cb": cb, "xpwT": xpwT,
                "dtwT": dtwT, "dtb": inp["L2_attn_dt_proj_b"], "alog": alog,
                "ut": ut, "vp": inp["L2_attn_V_param"], "ls": ls,
                "dcol": dcol, "opT": opT} for i in range(NCORES)]
    res = run_bass_kernel_spmd(nc, in_maps, core_ids=list(range(NCORES)))
    return [np.asarray(r["xout"]) for r in res.results]



L3_DEBUG = False
L3_LAYERS = 4

def build_l3_attn(E=None, xin=None, xout=None, prefix=""):
    standalone = E is None
    if standalone:
        E = Em("l3attn")
    E.prefix = prefix
    E.push_scope()
    x_in = xin if xin is not None else E.dram("x", [TOKENS, D], kind="ExternalInput")
    ip_in = E.dram("ip", [4, 1024, 4096], kind="ExternalInput")
    cw_in = E.dram("cw", [4, 128, 64], kind="ExternalInput")
    cb_in = E.dram("cb", [4, 128, 16], kind="ExternalInput")
    xpw_in = E.dram("xpw", [4, 2048, 96], kind="ExternalInput")
    dtw_in = E.dram("dtw", [4, 64, 2048], kind="ExternalInput")
    dtb_in = E.dram("dtb", [4, 2048], kind="ExternalInput")
    alog_in = E.dram("alog", [4, 128, 256], kind="ExternalInput")
    dcol_in = E.dram("dcol", [4, 128, 16], kind="ExternalInput")
    ow_in = E.dram("ow", [4, 2048, 1024], kind="ExternalInput")
    lnw_in = E.dram("lnw", [4, 1024], kind="ExternalInput")
    lnb_in = E.dram("lnb", [4, 1024], kind="ExternalInput")
    x_out = xout if xout is not None else E.dram("xout", [TOKENS, D], kind="ExternalOutput")

    NC = 16
    NS = 16
    T = SEQ
    TH = T // 2

    xi_dram = E.dram("xiscr", [TOKENS, D])
    dbg = E.dram("dbg", [5, TOKENS, D], kind="ExternalOutput") if L3_DEBUG else None
    xz_dram = E.dram("xzscr", [32, 128, T], BF16)
    u_dram = E.dram("uscr", [NC, 128, T], BF16)
    y_dram = E.dram("yscr", [NC, 128, T], BF16)
    bc_dram = E.dram("bcscr", [32, T], BF16)
    bcb_dram = E.dram("bcbscr", [32, 128, T], BF16)

    xs = E.sbuf("xs", [128, 2, D])
    st = alloc_ln_stats(E, 2)
    ln_bf = E.sbuf("lnbf", [128, 2, D], BF16)
    delta_c = E.sbuf("deltac", [128, TH])
    du = E.sbuf("du", [128, TH])
    dA2 = [E.sbuf(f"dA{i}", [128, TH]) for i in range(2)]
    xin2 = [E.sbuf(f"xin{i}", [128, TH]) for i in range(2)]
    h2 = [E.sbuf(f"hh{i}", [128, TH]) for i in range(2)]
    ytmp = E.sbuf("ytmp", [128, TH])
    y_c = E.sbuf("yc", [128, TH])
    hsave_t = E.sbuf("hsavet", [128, NC, NS])
    Bpl = [E.sbuf(f"Bpl{i}", [128, TH], BF16) for i in range(2)]
    Cpl = [E.sbuf(f"Cpl{i}", [128, TH], BF16) for i in range(2)]
    pstage = [E.sbuf(f"pst{i}", [128, T], BF16) for i in range(2)]
    zs = E.sbuf("zs", [128, TH])
    zbf = E.sbuf("zbf", [128, TH], BF16)
    ybf = E.sbuf("ybf", [128, TH], BF16)
    u_c = E.sbuf("uc", [128, TH], BF16)
    xc_bf = E.sbuf("xcbf", [128, 3 + T], BF16)
    uacc = E.sbuf("uacc", [128, TH])
    A_t = E.sbuf("At", [128, NC, NS])
    dcol_t = E.sbuf("dcolt", [128, NC])
    cw_t = E.sbuf("cwt", [128, NC * 4])
    cb_t = E.sbuf("cbt", [128, NC])
    onesb = E.sbuf("onesb", [1, 128], BF16)
    dtb_row = E.sbuf("dtbrow", [1, 2048])
    onesf = E.sbuf("onesf2", [1, 512])
    dtlr = E.sbuf("dtlr", [64, T], BF16)
    bc_t = E.sbuf("bct", [32, T], BF16)
    bcrow2 = [E.sbuf(f"bcrow{i}", [1, T], BF16) for i in range(2)]
    dtw_sb = E.sbuf("dtwsb", [64, 2048], BF16)
    xpw_sb = E.sbuf("xpwsb", [128, 16, 96], BF16)
    ipsb = E.sbuf("ipsb", [128, 8, 256], BF16)
    zstage = E.sbuf("zstage", [128, T], BF16)
    u_sb2 = [E.sbuf(f"usb{i}", [128, T], BF16) for i in range(2)]
    alog_sb = E.sbuf("alogsb", [128, 256])
    lnw_bc = E.sbuf("lnwbc", [128, D])
    lnb_bc = E.sbuf("lnbbc", [128, D])
    lnwrow = E.sbuf("lnwrow", [1, D], BF16)
    pads = [E.psum(f"p{i}", [128, 512]) for i in range(8)]
    sems = {k: E.sem(f"sm_{k}") for k in ["da", "xi", "sc", "y", "r0", "r1",
                                           "m", "e2", "xd0", "xd1", "xm",
                                           "b0", "b1", "c0", "c1", "p0", "p1"]}

    def f0(v):
        v.memset(onesb[:, :], 1.0)
        v.memset(onesf[:, :], 1.0)
        v.memset(xc_bf[:, 0:3], 0.0)
    E.blk(vector=f0)
    E.constf(LN_EPS)

    for s in range(B_PER_CORE):
        base = s * T
        for grp in range(8):
            tt0 = grp * 2

            def fx(sy, tt0=tt0, base=base):
                for j in range(2):
                    E.dma(sy, xs[:, j, :],
                          x_in[base + (tt0 + j) * 128:
                               base + (tt0 + j + 1) * 128, :])
                E.dwait(sy)
            E.blk(sync=fx)

            emit_ln(E, xs, 2, st, xs)

            def fxo(sy, tt0=tt0, base=base):
                for j in range(2):
                    E.dma(sy, xi_dram[base + (tt0 + j) * 128:
                                      base + (tt0 + j + 1) * 128, :],
                          xs[:, j, :])
                    if L3_DEBUG:
                        E.dma(sy, dbg[0, base + (tt0 + j) * 128:
                                      base + (tt0 + j + 1) * 128, :],
                              xs[:, j, :])
                E.dwait(sy)
            E.blk(sync=fxo)

    for layer in range(L3_LAYERS):
        def fld(sy, layer=layer):
            E.dma(sy, cw_t[:, :], cw_in[layer])
            E.dma(sy, cb_t[:, :], cb_in[layer])
            E.dma(sy, dcol_t[:, :], dcol_in[layer])
            E.dma(sy, dtb_row[:, :], dtb_in[layer:layer + 1, :])
            E.dma(sy, alog_sb[:, :], alog_in[layer])
            E.dwait(sy)
        E.blk(sync=fld)

        def fldg(g, layer=layer):
            E.dma(g, dtw_sb[:, :], dtw_in[layer])
            for c in range(16):
                E.dma(g, xpw_sb[:, c, :], xpw_in[layer, c * 128:(c + 1) * 128, :])
            E.dma(g, lnwrow[:, :], lnw_in[layer:layer + 1, :])
            E.dwait(g)
        E.blk(gpsimd=fldg)

        def fA(sC):
            for c in range(NC):
                sC.act(A_t[:, c, :], alog_sb[:, c * NS:(c + 1) * NS], AF.Exp,
                       nodep=True)
        E.blk(scalar=fA)

        def fA2(v):
            v.ts(A_t[:, :, :], A_t[:, :, :], -1.0, None, ALU.mult)
        E.blk(vector=fA2)

        def flw(t):
            for dh in range(2):
                t.mm(pads[dh][:, :], onesb[:, :],
                     lnwrow[:, dh * 512:(dh + 1) * 512])
        E.blk(tensor=flw)

        def flwe(v):
            for dh in range(2):
                v.copy(lnw_bc[:, dh * 512:(dh + 1) * 512], pads[dh][:, :],
                       nodep=True)
        E.blk(vector=flwe)

        def fldb(g, layer=layer):
            E.dma(g, lnwrow[:, :], lnb_in[layer:layer + 1, :])
            E.dwait(g)
        E.blk(gpsimd=fldb)

        def flb(t):
            for dh in range(2):
                t.mm(pads[dh][:, :], onesb[:, :],
                     lnwrow[:, dh * 512:(dh + 1) * 512])
        E.blk(tensor=flb)

        def flbe(v):
            for dh in range(2):
                v.copy(lnb_bc[:, dh * 512:(dh + 1) * 512], pads[dh][:, :],
                       nodep=True)
        E.blk(vector=flbe)

        for s in range(B_PER_CORE):
            base = s * T

            with E.scoped(f"lnT{layer}_{s}", [128, 8, T], BF16) as lnT:
                for grp in range(8):
                    tt0 = grp * 2

                    def fx(sy, tt0=tt0, base=base):
                        for j in range(2):
                            E.dma(sy, xs[:, j, :],
                                  xi_dram[base + (tt0 + j) * 128:
                                          base + (tt0 + j + 1) * 128, :])
                        E.dwait(sy)
                    E.blk(sync=fx)

                    def fcv2(v):
                        for j in range(2):
                            v.copy(ln_bf[:, j, :], xs[:, j, :], nodep=True)
                    E.blk(vector=fcv2)

                    def ftr(sy, tt0=tt0):
                        for j in range(2):
                            for c in range(8):
                                E.dma_t(sy, lnT[:, c, (tt0 + j) * 128:
                                                (tt0 + j + 1) * 128],
                                        ln_bf[:, j, c * 128:(c + 1) * 128])
                        E.dwait(sy)
                    E.blk(sync=ftr)

                for eg in range(16):
                    def fwz(g, eg=eg, layer=layer):
                        for k in range(8):
                            E.dma(g, ipsb[:, k, :],
                                  ip_in[layer, k * 128:(k + 1) * 128,
                                        eg * 256:(eg + 1) * 256])
                        E.dwait(g)
                    E.blk(gpsimd=fwz)

                    for ci in range(2):
                        def fmm(t, eg=eg, ci=ci):
                            for tw in range(4):
                                p = pads[ci * 4 + tw]
                                for k in range(8):
                                    t.mm(p[:, :],
                                         ipsb[:, k, ci * 128:(ci + 1) * 128],
                                         lnT[:, k, tw * 512:(tw + 1) * 512],
                                         start=(k == 0), stop=(k == 7))
                        E.blk(tensor=fmm)

                        def fev(sC, ci=ci):
                            for tw in range(4):
                                sC.act(zstage[:, tw * 512:(tw + 1) * 512],
                                       pads[ci * 4 + tw][:, :], AF.Copy,
                                       nodep=True)
                        E.blk(scalar=fev)

                        def fzo(sy, eg=eg, ci=ci):
                            E.dma(sy, xz_dram[eg * 2 + ci], zstage[:, :])
                            E.dwait(sy)
                        E.blk(sync=fzo)


            for c in range(NC):
                def fxc(sy, c=c):
                    E.dma(sy, xc_bf[:, 3:3 + T], xz_dram[c])
                    E.dwait(sy)
                E.blk(sync=fxc)

                for th2 in range(2):
                    o = th2 * TH

                    def fcv(v, c=c, o=o):
                        v.ts(uacc[:, :], xc_bf[:, 3 + o:3 + o + TH],
                             cw_t[:, c * 4 + 3:c * 4 + 4], None, ALU.mult)
                        for k in range(3):
                            v.stt(uacc[:, :], xc_bf[:, k + o:k + o + TH],
                                  cw_t[:, c * 4 + k:c * 4 + k + 1], uacc[:, :],
                                  ALU.mult, ALU.add)
                    E.blk(vector=fcv)

                    def fsu(sC, c=c, o=o):
                        sC.act(zstage[:, o:o + TH], uacc[:, :], AF.Silu,
                               bias=cb_t[:, c:c + 1])
                    E.blk(scalar=fsu)

                def fuo(sy, c=c):
                    E.dma(sy, u_dram[c], zstage[:, :])
                    E.dwait(sy)
                E.blk(sync=fuo)

            sxp = [sems["xd0"], sems["xd1"]]
            sm = sems["xm"]
            xdp0 = [sxp[0].c, sxp[1].c]
            xm0 = sm.c

            def fxd_s(sy):
                for k in range(NC):
                    if k >= 2:
                        sy.wait_ge(sm.h, xm0 + k - 1)
                    sy.dma_start(out=u_sb2[k % 2][:, :],
                                 in_=u_dram[k]).then_inc(sxp[k % 2].h, 16)

            def fxd_t(t):
                for k in range(NC):
                    t.e.wait_ge(sxp[k % 2].h, xdp0[k % 2] + 16 * (k // 2 + 1))
                    for tw in range(4):
                        mm = t.mm(pads[tw][0:96, :], xpw_sb[:, k, :],
                                  u_sb2[k % 2][:, tw * 512:(tw + 1) * 512],
                                  start=(k == 0), stop=(k == NC - 1))
                        if tw == 3:
                            mm.then_inc(sm.h, 1)
            E.blk(sync=fxd_s, tensor=fxd_t)
            sxp[0].c += 16 * (NC // 2)
            sxp[1].c += 16 * (NC // 2)
            sm.c += NC

            def fxd_e(sC):
                for tw in range(4):
                    sC.act(dtlr[:, tw * 512:(tw + 1) * 512], pads[tw][0:64, :],
                           AF.Copy, nodep=True)
            E.blk(scalar=fxd_e)

            def fxd_e2(v):
                for tw in range(4):
                    v.copy(bc_t[:, tw * 512:(tw + 1) * 512], pads[tw][64:96, :],
                           nodep=True)
            E.blk(vector=fxd_e2)

            def fbco(sy):
                E.dma(sy, bc_dram, bc_t[:, :])
                E.dwait(sy)
            E.blk(sync=fbco)

            srp = [sems["r0"], sems["r1"]]
            sm2, se2 = sems["m"], sems["e2"]
            spp = [sems["p0"], sems["p1"]]
            rp0 = [srp[0].c, srp[1].c]
            m0, e0 = sm2.c, se2.c
            pp0 = [spp[0].c, spp[1].c]

            def fbc_g(g):
                for i in range(32):
                    if i >= 2:
                        g.wait_ge(se2.h, e0 + i - 1)
                    g.dma_start(out=bcrow2[i % 2][:, :],
                                in_=bc_dram[i:i + 1, :]).then_inc(
                                    srp[i % 2].h, 16)

            def fbc_t(t):
                for i in range(32):
                    t.e.wait_ge(srp[i % 2].h, rp0[i % 2] + 16 * (i // 2 + 1))
                    if i >= 2:
                        t.e.wait_ge(se2.h, e0 + i - 1)
                    for tw in range(4):
                        mm = t.mm(pads[(i % 2) * 4 + tw][:, :], onesb[:, :],
                                  bcrow2[i % 2][:, tw * 512:(tw + 1) * 512])
                        if tw == 3:
                            mm.then_inc(sm2.h, 1)

            def fbc_v(v):
                for i in range(32):
                    v.e.wait_ge(sm2.h, m0 + i + 1)
                    if i >= 2:
                        v.e.wait_ge(spp[i % 2].h, pp0[i % 2] + 16 * (i // 2))
                    for tw in range(4):
                        v.copy(pstage[i % 2][:, tw * 512:(tw + 1) * 512],
                               pads[(i % 2) * 4 + tw][:, :], nodep=True)
                    v.e.drain()
                    v.e.sem_inc(se2.h, 1)

            def fbc_s(sy):
                for i in range(32):
                    sy.wait_ge(se2.h, e0 + i + 1)
                    sy.dma_start(out=bcb_dram[i], in_=pstage[i % 2][:, :]
                                 ).then_inc(spp[i % 2].h, 16)
                sy.wait_ge(spp[0].h, pp0[0] + 256)
                sy.wait_ge(spp[1].h, pp0[1] + 256)
            E.blk(gpsimd=fbc_g, tensor=fbc_t, vector=fbc_v, sync=fbc_s)
            srp[0].c += 256
            srp[1].c += 256
            sm2.c += 32
            se2.c += 32
            spp[0].c += 256
            spp[1].c += 256

            for th in range(2):
                t0 = th * TH
                for c in range(NC):
                    def fzu(sy, c=c, t0=t0):
                        E.dma(sy, u_c[:, :], u_dram[c][:, t0:t0 + TH])
                        E.dma(sy, zbf[:, :], xz_dram[16 + c][:, t0:t0 + TH])
                        E.dwait(sy)
                    E.blk(sync=fzu)

                    def fdm(t, c=c, t0=t0):
                        for tw in range(2):
                            p = pads[4 + tw]
                            t.mm(p[:, :], dtw_sb[:, c * 128:(c + 1) * 128],
                                 dtlr[:, t0 + tw * 512:t0 + (tw + 1) * 512],
                                 start=True, stop=False)
                            t.mm(p[:, :], dtb_row[:, c * 128:(c + 1) * 128],
                                 onesf[:, :], start=False, stop=True)
                    E.blk(tensor=fdm)

                    def fsp(sC, c=c):
                        for tw in range(2):
                            sC.act(delta_c[:, tw * 512:(tw + 1) * 512],
                                   pads[4 + tw][:, :], AF.Exp, nodep=True)
                        sC.e.drain()
                        for tw in range(2):
                            sC.act(delta_c[:, tw * 512:(tw + 1) * 512],
                                   delta_c[:, tw * 512:(tw + 1) * 512],
                                   AF.Ln, bias=1.0, nodep=True)
                    E.blk(scalar=fsp)

                    hin = None
                    hsv = None
                    if th == 0:
                        hsv = lambda n, c=c: hsave_t[:, c, n:n + 1]
                    else:
                        hin = lambda n, c=c: hsave_t[:, c, n:n + 1]
                    emit_mamba_chunk_scan(
                        E, sems, delta_c[:, :], u_c[:, :],
                        lambda n, c=c: A_t[:, c, n:n + 1],
                        Bpl, Cpl, y_c[:, :], dA2, xin2, h2, du, ytmp, NS,
                        hinit=hin, hsave=hsv,
                        B_src=lambda n, t0=t0: bcb_dram[n][:, t0:t0 + TH],
                        C_src=lambda n, t0=t0: bcb_dram[16 + n][:, t0:t0 + TH])

                    def fsz(sC, c=c):
                        sC.act(zs[:, :], zbf[:, :], AF.Silu)
                    E.blk(scalar=fsz)

                    def fgt(v, c=c):
                        v.stt(y_c[:, :], u_c[:, :], dcol_t[:, c:c + 1], y_c[:, :],
                              ALU.mult, ALU.add)
                        v.tt(ybf[:, :], y_c[:, :], zs[:, :], ALU.mult)
                    E.blk(vector=fgt)

                    def fyo(sy, c=c, t0=t0):
                        E.dma(sy, y_dram[c][:, t0:t0 + TH], ybf[:, :])
                        E.dwait(sy)
                    E.blk(sync=fyo)

            with E.scoped(f"owsb{layer}{s}", [128, 16, 1024], BF16) as ow_sb, \
                 E.scoped(f"ysb{layer}{s}", [128, 16, 256], BF16) as ysb:
                def fop(g, layer=layer):
                    for c in range(16):
                        E.dma(g, ow_sb[:, c, :],
                              ow_in[layer, c * 128:(c + 1) * 128, :])
                    E.dwait(g)
                E.blk(gpsimd=fop)

                for tp in range(8):
                    def fyr(sy, tp=tp, base=base):
                        for c in range(16):
                            E.dma(sy, ysb[:, c, :],
                                  y_dram[c][:, tp * 256:(tp + 1) * 256])
                        for j in range(2):
                            tt = tp * 2 + j
                            E.dma(sy, xs[:, j, :],
                                  xi_dram[base + tt * 128:
                                          base + (tt + 1) * 128, :])
                        E.dwait(sy)
                    E.blk(sync=fyr)

                    def fomm(t, tp=tp):
                        for j in range(2):
                            for dh in range(2):
                                p = pads[4 + j * 2 + dh]
                                for c in range(16):
                                    t.mm(p[:, :],
                                         ysb[:, c, j * 128:(j + 1) * 128],
                                         ow_sb[:, c, dh * 512:(dh + 1) * 512],
                                         start=(c == 0), stop=(c == 15))
                    E.blk(tensor=fomm)

                    def foev(v, tp=tp):
                        for j in range(2):
                            for dh in range(2):
                                v.tt(xs[:, j, dh * 512:(dh + 1) * 512],
                                     xs[:, j, dh * 512:(dh + 1) * 512],
                                     pads[4 + j * 2 + dh][:, :], ALU.add,
                                     nodep=True)
                    E.blk(vector=foev)

                    emit_ln(E, xs, 2, st, xs)

                    def faf(v):
                        for j in range(2):
                            v.tt(xs[:, j, :], xs[:, j, :], lnw_bc[:, :],
                                 ALU.mult, nodep=True)
                        v.e.drain()
                        for j in range(2):
                            v.tt(xs[:, j, :], xs[:, j, :], lnb_bc[:, :],
                                 ALU.add, nodep=True)
                    E.blk(vector=faf)

                    def fxio(sy, tp=tp, base=base, layer=layer):
                        for j in range(2):
                            tt = tp * 2 + j
                            E.dma(sy, xi_dram[base + tt * 128:
                                              base + (tt + 1) * 128, :],
                                  xs[:, j, :])
                            if L3_DEBUG:
                                E.dma(sy, dbg[layer + 1, base + tt * 128:
                                              base + (tt + 1) * 128, :],
                                      xs[:, j, :])
                        E.dwait(sy)
                    E.blk(sync=fxio)

    for s in range(B_PER_CORE):
        base = s * T
        for tp in range(8):
            def ffr(sy, tp=tp, base=base):
                for j in range(2):
                    tt = tp * 2 + j
                    E.dma(sy, xs[:, j, :],
                          x_in[base + tt * 128:base + (tt + 1) * 128, :])
                xi0 = base + tp * 256
                E.dma(sy, uacc[:, :].rearrange("p (j f) -> p j f", j=1)
                      if False else uacc[:, :],
                      xi_dram[xi0:xi0 + 128, :].rearrange("p f -> p f")
                      if False else xi_dram[xi0:xi0 + 128, :])
                E.dma(sy, zs[:, :], xi_dram[xi0 + 128:xi0 + 256, :])
                E.dwait(sy)
            E.blk(sync=ffr)

            def ffa(v):
                v.tt(xs[:, 0, :], xs[:, 0, :], uacc[:, :], ALU.add, nodep=True)
                v.tt(xs[:, 1, :], xs[:, 1, :], zs[:, :], ALU.add, nodep=True)
            E.blk(vector=ffa)

            def ffo(sy, tp=tp, base=base):
                for j in range(2):
                    tt = tp * 2 + j
                    E.dma(sy, x_out[base + tt * 128:base + (tt + 1) * 128, :],
                          xs[:, j, :])
                E.dwait(sy)
            E.blk(sync=ffo)
    E.pop_scope()
    if standalone:
        E.finish()
        return E.nc
    return None


def launch_l3_attn(state, inp):
    nc = build_l3_attn()
    def cols(a, w):
        L, DI = a.shape
        return np.ascontiguousarray(a.reshape(L, DI // 128, 128).transpose(0, 2, 1))
    cw = np.ascontiguousarray(
        inp["L3_attn_conv_w"].reshape(4, 16, 128, 4).transpose(0, 2, 1, 3).reshape(4, 128, 64))
    cb = np.ascontiguousarray(
        inp["L3_attn_conv_b"].reshape(4, 16, 128).transpose(0, 2, 1))
    alog = np.ascontiguousarray(
        inp["L3_attn_A_log"].reshape(4, 16, 128, 16).transpose(0, 2, 1, 3).reshape(4, 128, 256))
    dcol = np.ascontiguousarray(
        inp["L3_attn_D"].reshape(4, 16, 128).transpose(0, 2, 1))
    in_maps = [{"x": state[i], "ip": inp["L3_attn_in_proj_w"], "cw": cw,
                "cb": cb, "xpw": inp["L3_attn_x_proj_w"],
                "dtw": inp["L3_attn_dt_proj_w"], "dtb": inp["L3_attn_dt_proj_b"],
                "alog": alog, "dcol": dcol, "ow": inp["L3_attn_out_proj_w"],
                "lnw": inp["L3_attn_ln_w"], "lnb": inp["L3_attn_ln_b"]}
               for i in range(NCORES)]
    res = run_bass_kernel_spmd(nc, in_maps, core_ids=list(range(NCORES)))
    return [np.asarray(r["xout"]) for r in res.results]



def build_l3_ffn(E=None, xin=None, xout=None, prefix=""):
    standalone = E is None
    if standalone:
        E = Em("l3ffn")
    E.prefix = prefix
    E.push_scope()
    x_in = xin if xin is not None else E.dram("x", [TOKENS, D], kind="ExternalInput")
    wqkv_in = E.dram("wqkv", [2, 1024, 1536], kind="ExternalInput")
    qs_in = E.dram("qs", [2, 128, 64], kind="ExternalInput")
    ks_in = E.dram("ks", [2, 128, 64], kind="ExternalInput")
    wo_in = E.dram("wo", [2, 512, 1024], kind="ExternalInput")
    lng_in = E.dram("lng", [2, 1024], kind="ExternalInput")
    lnbf_in = E.dram("lnbf", [2, 1024], kind="ExternalInput")
    w1a_in = E.dram("w1a", [2, 1024, 2816], kind="ExternalInput")
    w1g_in = E.dram("w1g", [2, 1024, 2816], kind="ExternalInput")
    b1a_in = E.dram("b1a", [2, 2816], kind="ExternalInput")
    b1g_in = E.dram("b1g", [2, 2816], kind="ExternalInput")
    w2p_in = E.dram("w2p", [2, 2816, 1024], kind="ExternalInput")
    b2_in = E.dram("b2", [2, 1024], kind="ExternalInput")
    rot_in = E.dram("rot", [4, 2048, 64], kind="ExternalInput")
    msk_in = E.dram("msk", [8, 128, 512], kind="ExternalInput")
    x_out = xout if xout is not None else E.dram("xout", [TOKENS, D], kind="ExternalOutput")

    T = SEQ
    NT = 16
    FC = 22

    xi_dram = E.dram("xiscr", [TOKENS, D])
    h2_dram = E.dram("h2scr", [FC, 128, T], BF16)

    xs = E.sbuf("xs", [128, 2, D])
    xsh = E.sbuf("xsh", [128, 2, 512])
    st = alloc_ln_stats(E, 2)
    ln_bf = E.sbuf("lnbf2", [128, 2, D], BF16)
    rot_tmp = E.sbuf("rottmp", [128, 512])
    sqs = E.sbuf("sqs", [128, 512])
    n2 = E.sbuf("n2", [128, 16])
    rstd = E.sbuf("rstd", [128, 16])
    qT = E.sbuf("qT", [128, 4, T], BF16)
    kT = E.sbuf("kT", [128, 4, T], BF16)
    v_tm = E.sbuf("vtm", [128, NT, 512], BF16)
    o_tm = E.sbuf("otm", [128, NT, 512], BF16)
    oT = E.sbuf("oT", [128, 4, T], BF16)
    rot_t = E.sbuf("rott", [128, 4, 64], BF16)
    qs_t = E.sbuf("qst", [128, 64])
    ks_t = E.sbuf("kst", [128, 64])
    obuf = E.sbuf("obuf", [128, 128], BF16)
    den = E.sbuf("den", [128, 2])
    onesf = E.sbuf("onesf", [1, 512])
    onesbw = E.sbuf("onesbw", [1, 512], BF16)
    onesb = E.sbuf("onesb", [1, 128], BF16)
    ones_col = E.sbuf("onescol", [128, 1], BF16)
    lnwrow = E.sbuf("lnwrow", [1, D], BF16)
    pads = [E.psum(f"p{i}", [128, 512]) for i in range(8)]

    def f0(v):
        v.memset(onesf[:, :], 1.0)
        v.memset(onesbw[:, :], 1.0)
        v.memset(onesb[:, :], 1.0)
        v.memset(ones_col[:, :], 1.0)
    E.blk(vector=f0)
    E.constf(LN_EPS)

    for s in range(B_PER_CORE):
        base = s * T
        for grp in range(8):
            tt0 = grp * 2

            def fx(sy, tt0=tt0, base=base):
                for j in range(2):
                    E.dma(sy, xs[:, j, :],
                          x_in[base + (tt0 + j) * 128:base + (tt0 + j + 1) * 128, :])
                E.dwait(sy)
            E.blk(sync=fx)
            emit_ln(E, xs, 2, st, xs)

            def fxo(sy, tt0=tt0, base=base):
                for j in range(2):
                    E.dma(sy, xi_dram[base + (tt0 + j) * 128:
                                      base + (tt0 + j + 1) * 128, :], xs[:, j, :])
                E.dwait(sy)
            E.blk(sync=fxo)

    def load_shifted(tag, base, src):
        for grp in range(8):
            tt0 = grp * 2

            def fx(sy, tt0=tt0, base=base):
                for j in range(2):
                    r0 = base + (tt0 + j) * 128
                    E.dma(sy, xs[:, j, :], src[r0:r0 + 128, :])
                    if tt0 + j == 0:
                        E.dma(sy, xsh[1:128, j, :], src[r0:r0 + 127, 512:1024])
                    else:
                        E.dma(sy, xsh[:, j, :], src[r0 - 1:r0 + 127, 512:1024])
                E.dwait(sy)
            E.blk(sync=fx)
            yield grp

    for layer in range(2):
        def flq(g, layer=layer):
            E.dma(g, qs_t[:, :], qs_in[layer])
            E.dma(g, ks_t[:, :], ks_in[layer])
            E.dwait(g)
        E.blk(gpsimd=flq)

        for s in range(B_PER_CORE):
            base = s * T

            with E.scoped(f"qT{layer}{s}", [128, 4, T], BF16) as qT, \
                 E.scoped(f"kT{layer}{s}", [128, 4, T], BF16) as kT, \
                 E.scoped(f"vtm{layer}{s}", [128, NT, 512], BF16) as v_tm, \
                 E.scoped(f"oT{layer}{s}", [128, 4, T], BF16) as oT:
              with E.scoped(f"wq{layer}{s}", [128, 8, 1536], BF16) as wq_sb, \
                   E.scoped(f"hT2{layer}{s}", [128, 8, 128], BF16) as hT2, \
                   E.scoped(f"qb{layer}{s}", [128, 1024], BF16) as qkv_bf, \
                   E.scoped(f"qtm{layer}{s}", [128, 1024], F32) as qkv_tm_t:
                qkv_tm = qkv_tm_t
                def fwq(g, layer=layer):
                    for k in range(8):
                        E.dma(g, wq_sb[:, k, :],
                              wqkv_in[layer, k * 128:(k + 1) * 128, :])
                    E.dwait(g)
                E.blk(gpsimd=fwq)

                for grp in load_shifted("a", base, xi_dram):
                    tt0 = grp * 2

                    def fcv(v, tt0=tt0):
                        if tt0 == 0:
                            v.memset(xsh[0:1, 0, :], 0.0, nodep=True)
                            v.e.drain()
                        for j in range(2):
                            v.copy(ln_bf[:, j, 0:512], xs[:, j, 0:512], nodep=True)
                            v.copy(ln_bf[:, j, 512:1024], xsh[:, j, :], nodep=True)
                    E.blk(vector=fcv)

                    for j in range(2):
                        tt = tt0 + j

                        def ftr(sy, j=j):
                            for c in range(8):
                                E.dma_t(sy, hT2[:, c, :],
                                        ln_bf[:, j, c * 128:(c + 1) * 128])
                            E.dwait(sy)
                        E.blk(sync=ftr)

                        def fqm(t):
                            for fc in range(3):
                                p = pads[fc]
                                for k in range(8):
                                    t.mm(p[:, :], hT2[:, k, :],
                                         wq_sb[:, k, fc * 512:(fc + 1) * 512],
                                         start=(k == 0), stop=(k == 7))
                        E.blk(tensor=fqm)

                        def fqe(v, tt=tt):
                            for fc in range(2):
                                v.copy(qkv_tm[:, fc * 512:(fc + 1) * 512],
                                       pads[fc][:, :], nodep=True)
                            v.copy(v_tm[:, tt, :], pads[2][:, :], nodep=True)
                        E.blk(vector=fqe)

                        def frt(g, tt=tt):
                            for r in range(4):
                                E.dma(g, rot_t[:, r, :],
                                      rot_in[r, tt * 128:(tt + 1) * 128, :])
                            E.dwait(g)
                        E.blk(gpsimd=frt)

                        def fn1(v):
                            for side in range(2):
                                qv = qkv_tm[:, side * 512:side * 512 + 512]\
                                    .rearrange("p (h d) -> p h d", h=8)
                                v.tt(sqs[:, :].rearrange("p (h d) -> p h d", h=8),
                                     qv, qv, ALU.mult)
                                v.e.drain()
                                v.red(n2[:, side * 8:side * 8 + 8].unsqueeze(2),
                                      sqs[:, :].rearrange("p (h d) -> p h d", h=8),
                                      ALU.add)
                                v.e.drain()
                        E.blk(vector=fn1)

                        def fs1(sC):
                            sC.act(n2[:, :], n2[:, :], AF.Sqrt)
                        E.blk(scalar=fs1)

                        def fn2(v):
                            v.ts(n2[:, :], n2[:, :], 1.0, 1e-12, ALU.mult, ALU.max)
                            v.recip(rstd[:, :], n2[:, :])
                            v.ts(rstd[:, 0:8], rstd[:, 0:8], 8.0, None, ALU.mult)
                        E.blk(vector=fn2)

                        def fn3(v):
                            for side in range(2):
                                off = side * 512
                                sct = qs_t if side == 0 else ks_t
                                qv3 = qkv_tm[:, off:off + 512].rearrange(
                                    "p (h d) -> p h d", h=8)
                                rst3 = rstd[:, side * 8:side * 8 + 8]\
                                    .unsqueeze(2).broadcast_to((128, 8, 64))
                                sc3 = sct[:, :].unsqueeze(1)\
                                    .broadcast_to((128, 8, 64))
                                v.tt(qv3, qv3, rst3, ALU.mult)
                                v.e.drain()
                                v.tt(qv3, qv3, sc3, ALU.mult)
                                v.e.drain()
                                v.copy(rot_tmp[:, :], qkv_tm[:, off:off + 512])
                                v.e.drain()
                                r3 = rot_tmp[:, :].rearrange("p (h d) -> p h d", h=8)
                                cosr = rot_t[:, side * 2, :]
                                sinr = rot_t[:, side * 2 + 1, :]
                                cl = cosr[:, 0:32].unsqueeze(1)\
                                    .broadcast_to((128, 8, 32))
                                ch = cosr[:, 32:64].unsqueeze(1)\
                                    .broadcast_to((128, 8, 32))
                                sl = sinr[:, 0:32].unsqueeze(1)\
                                    .broadcast_to((128, 8, 32))
                                sh = sinr[:, 32:64].unsqueeze(1)\
                                    .broadcast_to((128, 8, 32))
                                s3 = sqs[:, :].rearrange("p (h d) -> p h d", h=8)
                                v.tt(qv3[:, :, 0:32], r3[:, :, 0:32], cl, ALU.mult)
                                v.tt(s3[:, :, 0:32], r3[:, :, 32:64], sl, ALU.mult,
                                     nodep=True)
                                v.e.drain()
                                v.tt(qv3[:, :, 0:32], qv3[:, :, 0:32],
                                     s3[:, :, 0:32], ALU.subtract)
                                v.tt(qv3[:, :, 32:64], r3[:, :, 32:64], ch,
                                     ALU.mult, nodep=True)
                                v.tt(s3[:, :, 32:64], r3[:, :, 0:32], sh,
                                     ALU.mult, nodep=True)
                                v.e.drain()
                                v.tt(qv3[:, :, 32:64], qv3[:, :, 32:64],
                                     s3[:, :, 32:64], ALU.add)
                                v.e.drain()
                        E.blk(vector=fn3)

                        def fcb(v, tt=tt):
                            v.copy(qkv_bf[:, :], qkv_tm[:, 0:1024])
                        E.blk(vector=fcb)

                        def ftq(sy, tt=tt):
                            for c in range(4):
                                E.dma_t(sy, qT[:, c, tt * 128:(tt + 1) * 128],
                                        qkv_bf[:, c * 128:(c + 1) * 128])
                                E.dma_t(sy, kT[:, c, tt * 128:(tt + 1) * 128],
                                        qkv_bf[:, 512 + c * 128:
                                               512 + (c + 1) * 128])
                            E.dwait(sy)
                        E.blk(sync=ftq)

              with E.scoped(f"eS{layer}{s}", [128, 2, 1024], BF16) as expS, \
                   E.scoped(f"eT{layer}{s}", [128, 16, 128], BF16) as expST, \
                   E.scoped(f"mk{layer}{s}", [128, 8, 512], BF16) as msk_t:
                def fldm(g):
                    for c in range(8):
                        E.dma(g, msk_t[:, c, :], msk_in[c])
                    E.dwait(g)
                E.blk(gpsimd=fldm)

                for hg in range(4):
                    for qt in range(NT):
                        w = qt // 4
                        wparts = [1] if w == 0 else [0, 1]

                        def fsc(t, hg=hg, qt=qt, w=w, wparts=wparts):
                            for hi in range(2):
                                h = hg * 2 + hi
                                c, rb = h // 2, (h % 2) * 64
                                for wp in wparts:
                                    k0 = (w - 1 + wp) * 512
                                    p = pads[hi * 2 + wp]
                                    t.mm(p[:, :],
                                         qT[rb:rb + 64, c, qt * 128:(qt + 1) * 128],
                                         kT[rb:rb + 64, c, k0:k0 + 512],
                                         start=True, stop=True)
                        E.blk(tensor=fsc)

                        def fmk(v, qt=qt, wparts=wparts):
                            for hi in range(2):
                                for wp in wparts:
                                    cls = (qt % 4) + (4 if wp == 0 else 0)
                                    p = pads[hi * 2 + wp]
                                    v.tt(p[:, :], p[:, :],
                                         msk_t[:, cls, :], ALU.add, nodep=True)
                        E.blk(vector=fmk)

                        def fex(sC, wparts=wparts):
                            for hi in range(2):
                                for wp in wparts:
                                    sC.act(expS[:, hi, wp * 512:(wp + 1) * 512],
                                           pads[hi * 2 + wp][:, :],
                                           AF.Exp, nodep=True)
                        E.blk(scalar=fex)

                        def ftx(sy, wparts=wparts):
                            for hi in range(2):
                                for wp in wparts:
                                    for kc in range(4):
                                        E.dma_t(
                                            sy,
                                            expST[:, hi * 8 + wp * 4 + kc, :],
                                            expS[:, hi, wp * 512 + kc * 128:
                                                 wp * 512 + (kc + 1) * 128])
                            E.dwait(sy)
                        E.blk(sync=ftx)

                        def fav(t, hg=hg, w=w, wparts=wparts):
                            for hi in range(2):
                                h = hg * 2 + hi
                                first = True
                                for wp in wparts:
                                    for kc in range(4):
                                        ktile = (w - 1 + wp) * 4 + kc
                                        e = expST[:, hi * 8 + wp * 4 + kc, :]
                                        last = (wp == wparts[-1] and kc == 3)
                                        t.mm(pads[4][:, hi * 64:(hi + 1) * 64],
                                             e, v_tm[:, ktile,
                                                     h * 64:(h + 1) * 64],
                                             start=first, stop=last)
                                        t.mm(pads[5][:, hi:hi + 1], e,
                                             ones_col[:, :], start=first,
                                             stop=last)
                                        first = False
                        E.blk(tensor=fav)

                        def fno(v):
                            v.copy(den[:, 0:2], pads[5][:, 0:2])
                            v.e.drain()
                            v.recip(den[:, 0:2], den[:, 0:2])
                            v.e.drain()
                            for hi in range(2):
                                v.ts(obuf[:, hi * 64:(hi + 1) * 64],
                                     pads[4][:, hi * 64:(hi + 1) * 64],
                                     den[:, hi:hi + 1], None, ALU.mult,
                                     nodep=True)
                        E.blk(vector=fno)

                        def fob(sy, hg=hg, qt=qt):
                            E.dma_t(sy, oT[:, hg, qt * 128:(qt + 1) * 128],
                                    obuf[:, :])
                            E.dwait(sy)
                        E.blk(sync=fob)

              with E.scoped(f"wo{layer}{s}", [128, 4, 1024], BF16) as wosb:
                def fwol(g, layer=layer):
                    for c in range(4):
                        E.dma(g, wosb[:, c, :], wo_in[layer, c * 128:(c + 1) * 128, :])
                    E.dwait(g)
                E.blk(gpsimd=fwol)

                for tp in range(8):
                    def fxr(sy, tp=tp, base=base):
                        for j in range(2):
                            tt = tp * 2 + j
                            E.dma(sy, xs[:, j, :],
                                  xi_dram[base + tt * 128:base + (tt + 1) * 128, :])
                        E.dwait(sy)
                    E.blk(sync=fxr)

                    def fwo(t, tp=tp):
                        for j in range(2):
                            for dh in range(2):
                                p = pads[4 + j * 2 + dh]
                                for c in range(4):
                                    t.mm(p[:, :],
                                         oT[:, c, (tp * 2 + j) * 128:
                                            (tp * 2 + j + 1) * 128],
                                         wosb[:, c, dh * 512:(dh + 1) * 512],
                                         start=(c == 0), stop=(c == 3))
                    E.blk(tensor=fwo)

                    def fwe(v, tp=tp):
                        for j in range(2):
                            for dh in range(2):
                                v.tt(xs[:, j, dh * 512:(dh + 1) * 512],
                                     xs[:, j, dh * 512:(dh + 1) * 512],
                                     pads[4 + j * 2 + dh][:, :], ALU.add,
                                     nodep=True)
                    E.blk(vector=fwe)

                    def fxw(sy, tp=tp, base=base):
                        for j in range(2):
                            tt = tp * 2 + j
                            E.dma(sy, xi_dram[base + tt * 128:
                                              base + (tt + 1) * 128, :],
                                  xs[:, j, :])
                        E.dwait(sy)
                    E.blk(sync=fxw)

            with E.scoped(f"hlnT{layer}{s}", [128, 8, T], BF16) as hlnT, \
                 E.scoped(f"wsa{layer}{s}", [128, 8, 512], BF16) as wstr_a, \
                 E.scoped(f"wsg{layer}{s}", [128, 8, 512], BF16) as wstr_g, \
                 E.scoped(f"br{layer}{s}", [1, 2816], BF16) as brow, \
                 E.scoped(f"br2{layer}{s}", [1, 2816], BF16) as brow2, \
                 E.scoped(f"gl{layer}{s}", [128, 4, 512], F32) as gelu_t, \
                 E.scoped(f"h2b{layer}{s}", [128, T], BF16) as h2bf, \
                 E.scoped(f"lw{layer}{s}", [128, D], F32) as lnw_bc, \
                 E.scoped(f"lb{layer}{s}", [128, D], F32) as lnb_bc:
                def flw1(g, layer=layer):
                    E.dma(g, lnwrow[:, :], lng_in[layer:layer + 1, :])
                    E.dwait(g)
                E.blk(gpsimd=flw1)

                def flw2(t):
                    for dh in range(2):
                        t.mm(pads[dh][:, :], onesb[:, :],
                             lnwrow[:, dh * 512:(dh + 1) * 512])
                E.blk(tensor=flw2)

                def flw3(v):
                    for dh in range(2):
                        v.copy(lnw_bc[:, dh * 512:(dh + 1) * 512], pads[dh][:, :],
                               nodep=True)
                E.blk(vector=flw3)

                def flb1(g, layer=layer):
                    E.dma(g, lnwrow[:, :], lnbf_in[layer:layer + 1, :])
                    E.dwait(g)
                E.blk(gpsimd=flb1)

                def flb2(t):
                    for dh in range(2):
                        t.mm(pads[dh][:, :], onesb[:, :],
                             lnwrow[:, dh * 512:(dh + 1) * 512])
                E.blk(tensor=flb2)

                def flb3(v):
                    for dh in range(2):
                        v.copy(lnb_bc[:, dh * 512:(dh + 1) * 512], pads[dh][:, :],
                               nodep=True)
                E.blk(vector=flb3)

                def fbr(g, layer=layer):
                    E.dma(g, brow[:, :], b1a_in[layer:layer + 1, :])
                    E.dma(g, brow2[:, :], b1g_in[layer:layer + 1, :])
                    E.dwait(g)
                E.blk(gpsimd=fbr)

                for grp in load_shifted("m", base, xi_dram):
                    tt0 = grp * 2

                    def fsh(v, tt0=tt0):
                        if tt0 == 0:
                            v.memset(xsh[0:1, 0, :], 0.0, nodep=True)
                            v.e.drain()
                        for j in range(2):
                            v.copy(xs[:, j, 512:1024], xsh[:, j, :], nodep=True)
                    E.blk(vector=fsh)

                    emit_ln(E, xs, 2, st, xs)

                    def faf(v):
                        for j in range(2):
                            v.tt(xs[:, j, :], xs[:, j, :], lnw_bc[:, :],
                                 ALU.mult, nodep=True)
                        v.e.drain()
                        for j in range(2):
                            v.tt(xs[:, j, :], xs[:, j, :], lnb_bc[:, :],
                                 ALU.add, nodep=True)
                        v.e.drain()
                        for j in range(2):
                            v.copy(ln_bf[:, j, :], xs[:, j, :], nodep=True)
                    E.blk(vector=faf)

                    def ftr2(sy, tt0=tt0):
                        for j in range(2):
                            for c in range(8):
                                E.dma_t(sy, hlnT[:, c, (tt0 + j) * 128:
                                                (tt0 + j + 1) * 128],
                                        ln_bf[:, j, c * 128:(c + 1) * 128])
                        E.dwait(sy)
                    E.blk(sync=ftr2)

                for fg in range(6):
                    nfc = 4 if fg < 5 else 2
                    f0_ = fg * 4

                    def fws(g, fg=fg, nfc=nfc, layer=layer):
                        for k in range(8):
                            E.dma(g, wstr_a[:, k, 0:nfc * 128],
                                  w1a_in[layer, k * 128:(k + 1) * 128,
                                         fg * 512:fg * 512 + nfc * 128])
                            E.dma(g, wstr_g[:, k, 0:nfc * 128],
                                  w1g_in[layer, k * 128:(k + 1) * 128,
                                         fg * 512:fg * 512 + nfc * 128])
                        E.dwait(g)
                    E.blk(gpsimd=fws)

                    for fi in range(nfc):
                        fc = f0_ + fi

                        def fm1(t, fi=fi, fc=fc):
                            for tw in range(4):
                                pa, pg = pads[tw], pads[4 + tw]
                                for k in range(8):
                                    t.mm(pa[:, :],
                                         wstr_a[:, k, fi * 128:(fi + 1) * 128],
                                         hlnT[:, k, tw * 512:(tw + 1) * 512],
                                         start=(k == 0), stop=False)
                                    t.mm(pg[:, :],
                                         wstr_g[:, k, fi * 128:(fi + 1) * 128],
                                         hlnT[:, k, tw * 512:(tw + 1) * 512],
                                         start=(k == 0), stop=False)
                                t.mm(pa[:, :], brow[:, fc * 128:(fc + 1) * 128],
                                     onesbw[:, :], start=False, stop=True)
                                t.mm(pg[:, :], brow2[:, fc * 128:(fc + 1) * 128],
                                     onesbw[:, :], start=False, stop=True)
                        E.blk(tensor=fm1)

                        def fg1(sC):
                            for tw in range(4):
                                sC.act(gelu_t[:, tw, :], pads[4 + tw][:, :],
                                       AF.Gelu, nodep=True)
                        E.blk(scalar=fg1)

                        def fh2(v):
                            for tw in range(4):
                                v.tt(h2bf[:, tw * 512:(tw + 1) * 512],
                                     pads[tw][:, :], gelu_t[:, tw, :],
                                     ALU.mult, nodep=True)
                        E.blk(vector=fh2)

                        def fh2o(sy, fc=fc):
                            E.dma(sy, h2_dram[fc], h2bf[:, :])
                            E.dwait(sy)
                        E.blk(sync=fh2o)

            with E.scoped(f"w2s{layer}{s}", [128, FC, 512], BF16) as w2sb, \
                 E.scoped(f"h2s{layer}{s}", [128, FC, 256], BF16) as h2sb, \
                 E.scoped(f"b2r{layer}{s}", [1, 1024], BF16) as b2row:
                def fb2l(g, layer=layer):
                    E.dma(g, b2row[:, :], b2_in[layer:layer + 1, :])
                    E.dwait(g)
                E.blk(gpsimd=fb2l)
                if True:
                    for dh in range(2):
                        def fw2(g, dh=dh, layer=layer):
                            for c in range(FC):
                                E.dma(g, w2sb[:, c, :],
                                      w2p_in[layer, c * 128:(c + 1) * 128,
                                             dh * 512:(dh + 1) * 512])
                            E.dwait(g)
                        E.blk(gpsimd=fw2)

                        for tp in range(8):
                            def fhr(sy, tp=tp, dh=dh, base=base):
                                for c in range(FC):
                                    E.dma(sy, h2sb[:, c, :],
                                          h2_dram[c][:, tp * 256:(tp + 1) * 256])
                                for j in range(2):
                                    tt = tp * 2 + j
                                    E.dma(sy, xs[:, j, dh * 512:(dh + 1) * 512],
                                          xi_dram[base + tt * 128:
                                                  base + (tt + 1) * 128,
                                                  dh * 512:(dh + 1) * 512])
                                E.dwait(sy)
                            E.blk(sync=fhr)

                            def fm2(t, tp=tp, dh=dh):
                                for j in range(2):
                                    p = pads[4 + j]
                                    for c in range(FC):
                                        t.mm(p[:, :],
                                             h2sb[:, c, j * 128:(j + 1) * 128],
                                             w2sb[:, c, :], start=(c == 0),
                                             stop=False)
                                    t.mm(p[:, :], ones_col[0:1, 0:1].broadcast_to((1, 128))
                                         if False else onesb[:, :],
                                         b2row[:, dh * 512:(dh + 1) * 512]
                                         .rearrange("a f -> a f"),
                                         start=False, stop=True)
                            E.blk(tensor=fm2)

                            def fe2(v, tp=tp, dh=dh):
                                for j in range(2):
                                    v.tt(xs[:, j, dh * 512:(dh + 1) * 512],
                                         xs[:, j, dh * 512:(dh + 1) * 512],
                                         pads[4 + j][:, :], ALU.add, nodep=True)
                            E.blk(vector=fe2)

                            def fo2(sy, tp=tp, dh=dh, base=base):
                                for j in range(2):
                                    tt = tp * 2 + j
                                    E.dma(sy, xi_dram[base + tt * 128:
                                                      base + (tt + 1) * 128,
                                                      dh * 512:(dh + 1) * 512],
                                          xs[:, j, dh * 512:(dh + 1) * 512])
                                E.dwait(sy)
                            E.blk(sync=fo2)

    with E.scoped("fint", [128, 1024], F32) as fin_t:
      for s in range(B_PER_CORE):
        base = s * T
        for tp in range(8):
            def ffr(sy, tp=tp, base=base):
                for j in range(2):
                    tt = tp * 2 + j
                    E.dma(sy, xs[:, j, :],
                          x_in[base + tt * 128:base + (tt + 1) * 128, :])
                xi0 = base + tp * 256
                E.dma(sy, fin_t[:, :], xi_dram[xi0:xi0 + 128, :])
                E.dma(sy, rot_tmp[:, :], xi_dram[xi0 + 128:xi0 + 256, 0:512])
                E.dma(sy, sqs[:, :], xi_dram[xi0 + 128:xi0 + 256, 512:1024])
                E.dwait(sy)
            E.blk(sync=ffr)

            def ffa(v):
                v.tt(xs[:, 0, :], xs[:, 0, :], fin_t[:, :], ALU.add, nodep=True)
                v.tt(xs[:, 1, 0:512], xs[:, 1, 0:512], rot_tmp[:, :], ALU.add,
                     nodep=True)
                v.tt(xs[:, 1, 512:1024], xs[:, 1, 512:1024], sqs[:, :], ALU.add,
                     nodep=True)
            E.blk(vector=ffa)

            def ffo(sy, tp=tp, base=base):
                for j in range(2):
                    tt = tp * 2 + j
                    E.dma(sy, x_out[base + tt * 128:base + (tt + 1) * 128, :],
                          xs[:, j, :])
                E.dwait(sy)
            E.blk(sync=ffo)
    E.pop_scope()
    if standalone:
        E.finish()
        return E.nc
    return None


def launch_l3_ffn(state, inp):
    nc = build_l3_ffn()
    HID = 2730
    w1 = inp["L3_ffn_w1"]
    w1a = np.zeros((2, 1024, 2816), np.float32)
    w1g = np.zeros((2, 1024, 2816), np.float32)
    w1a[:, :, 0:HID] = w1[:, :, 0:HID]
    w1g[:, :, 0:HID] = w1[:, :, HID:2 * HID]
    b1 = inp["L3_ffn_b1"]
    b1a = np.zeros((2, 2816), np.float32)
    b1g = np.zeros((2, 2816), np.float32)
    b1a[:, 0:HID] = b1[:, 0:HID]
    b1g[:, 0:HID] = b1[:, HID:2 * HID]
    w2p = np.zeros((2, 2816, 1024), np.float32)
    w2p[:, 0:HID, :] = inp["L3_ffn_w2"]
    qs = np.ascontiguousarray(np.broadcast_to(
        inp["L3_ffn_q_scale"][:, None, :], (2, 128, 64)))
    ks = np.ascontiguousarray(np.broadcast_to(
        inp["L3_ffn_k_scale"][:, None, :], (2, 128, 64)))
    DIM_HEAD, WINDOW, N = 64, 512, 2048
    half = np.arange(0, DIM_HEAD, 2, dtype=np.float64)
    inv_freq = 1.0 / 10000.0 ** (half / DIM_HEAD)
    t = np.arange(N, dtype=np.float64)
    freqs = t[:, None] * inv_freq[None, :]
    freqs = np.concatenate([freqs, freqs], -1)
    bse = (half + 0.4 * DIM_HEAD) / (1.4 * DIM_HEAD)
    power = (t - N // 2) / (WINDOW // 2)
    scale = bse[None, :] ** power[:, None]
    scale = np.concatenate([scale, scale], -1)
    cos, sin = np.cos(freqs), np.sin(freqs)
    rot = np.stack([cos * scale, sin * scale, cos / scale, sin / scale]).astype(np.float32)
    msk = np.zeros((8, 128, 512), np.float32)
    p = np.arange(128)[:, None]
    f = np.arange(512)[None, :]
    for c in range(8):
        offp = 128 * c if c < 4 else 512 + 128 * (c - 4)
        d = (p - f) + offp
        msk[c] = np.where((d >= 0) & (d <= 512), 0.0, -1e30)
    in_maps = [{"x": state[i], "wqkv": inp["L3_ffn_wqkv"], "qs": qs, "ks": ks,
                "wo": inp["L3_ffn_wo"], "lng": inp["L3_ffn_ln_g"],
                "lnbf": inp["L3_ffn_ln_b"], "w1a": w1a, "w1g": w1g,
                "b1a": b1a, "b1g": b1g, "w2p": w2p, "b2": inp["L3_ffn_b2"],
                "rot": rot, "msk": msk} for i in range(NCORES)]
    res = run_bass_kernel_spmd(nc, in_maps, core_ids=list(range(NCORES)))
    return [np.asarray(r["xout"]) for r in res.results]



def build_l0_attn(E=None, xin=None, xout=None, prefix=""):
    standalone = E is None
    if standalone:
        E = Em("l0attn")
    E.prefix = prefix
    E.push_scope()
    x_in = xin if xin is not None else E.dram("x", [TOKENS, D], kind="ExternalInput")
    wcat_in = E.dram("wcat", [1024, 352], kind="ExternalInput")
    bcat_in = E.dram("bcat", [352], kind="ExternalInput")
    lnw_in = E.dram("lnw", [1024], kind="ExternalInput")
    lnb_in = E.dram("lnb", [1024], kind="ExternalInput")
    outw_in = E.dram("outw", [32, 1024], kind="ExternalInput")
    outb_in = E.dram("outb", [1024], kind="ExternalInput")
    io8_in = E.dram("io8", [128, 8], kind="ExternalInput")
    iden8_in = E.dram("iden8", [128, 64], kind="ExternalInput")
    x_out = xout if xout is not None else E.dram("xout", [TOKENS, D], kind="ExternalOutput")

    T = SEQ
    NT2 = 16
    sd_dram = E.dram("sdscr", [B_PER_CORE, 16, 128, 4, 80])
    ys_dram = E.dram("ysscr", [128, 8, 128])

    xs = E.sbuf("xs", [128, 2, D])
    st = alloc_ln_stats(E, 2)
    iden = emit_identity(E)
    wcat = E.sbuf("wcats", [128, 8, 352])
    xnT_t = E.sbuf("xnTt", [128, 8, 128])
    bcat_r = E.sbuf("bcatr", [1, 352])
    onesf = E.sbuf("onesf", [1, 512])
    lnw_bc = E.sbuf("lnwbc", [128, D])
    lnb_bc = E.sbuf("lnbbc", [128, D])
    lnwrow = E.sbuf("lnwrow", [1, D])
    sk = E.sbuf("sk", [128, 2, 4, 64])
    ske = E.sbuf("ske", [128, 2, 4, 64])
    lse = E.sbuf("lse", [128, 2, 4, 8])
    sgb = E.sbuf("sgb", [128, 2, 4, 24])
    hard = E.sbuf("hard", [128, 2, 4, 64])
    RM = E.sbuf("RM", [128, 2, 4, 8])
    rowm = E.sbuf("rowm", [128, 2, 4, 8])
    colm = E.sbuf("colm", [128, 2, 4, 8])
    rsel = E.sbuf("rsel", [128, 2, 4, 8])
    ridx = E.sbuf("ridx", [128, 2, 4, 1])
    oh_r = E.sbuf("ohr", [128, 2, 4, 8])
    oh_c = E.sbuf("ohc", [128, 2, 4, 8])
    rowv = E.sbuf("rowv", [128, 2, 4, 8])
    tmp4 = E.sbuf("tmp4", [128, 2, 4, 64])
    io8 = E.sbuf("io8s", [128, 8])
    io9 = E.sbuf("io9s", [128, 8])
    iden8 = E.sbuf("iden8s", [128, 64])
    gsc = E.sbuf("gsc", [128, 2, 4, 16])
    slab = E.sbuf("slab", [128, 128, 80])
    Tm = E.sbuf("Tm", [128, 64])
    ev = E.sbuf("ev", [128, 8])
    Tsh = E.sbuf("Tsh", [128, 64])
    esh = E.sbuf("esh", [128, 8])
    t4 = E.sbuf("t4", [128, 512])
    t3 = E.sbuf("t3", [128, 64])
    hv = E.sbuf("hv", [128, 8])
    ysl = E.sbuf("ysl", [128, 8, 128])
    ysT = E.sbuf("ysT", [32, 2048], BF16)
    res2 = E.sbuf("res2", [128, 2, D])
    outw_sb = E.sbuf("outwsb", [32, 1024], BF16)
    outb_r = E.sbuf("outbr", [1, 1024])
    pads = [E.psum(f"p{i}", [128, 512]) for i in range(8)]

    def f0(v):
        v.memset(onesf[:, :], 1.0)
    E.blk(vector=f0)
    E.constf(LN_EPS)

    def fld(sy):
        for k in range(8):
            E.dma(sy, wcat[:, k, :], wcat_in[k * 128:(k + 1) * 128, :])
        E.dma(sy, bcat_r[:, :], bcat_in.unsqueeze(0))
        E.dma(sy, lnwrow[:, :], lnw_in.unsqueeze(0))
        E.dma(sy, io8[:, :], io8_in)
        E.dma(sy, iden8[:, :], iden8_in)
        E.dma(sy, outb_r[:, :], outb_in.unsqueeze(0))
        E.dwait(sy)
    E.blk(sync=fld)

    def fldg(g):
        E.dma(g, outw_sb[:, :], outw_in)
        E.dwait(g)
    E.blk(gpsimd=fldg)

    def fi9(v):
        v.ts(io9[:, :], io8[:, :], 1.0, -9.0, ALU.mult, ALU.add)
    E.blk(vector=fi9)

    def flw(t):
        for dh in range(2):
            t.mm(pads[dh][:, :], onesf[:, 0:128],
                 lnwrow[:, dh * 512:(dh + 1) * 512])
    E.blk(tensor=flw)

    def flwe(v):
        for dh in range(2):
            v.copy(lnw_bc[:, dh * 512:(dh + 1) * 512], pads[dh][:, :], nodep=True)
    E.blk(vector=flwe)

    def flb0(sy):
        E.dma(sy, lnwrow[:, :], lnb_in.unsqueeze(0))
        E.dwait(sy)
    E.blk(sync=flb0)

    def flb(t):
        for dh in range(2):
            t.mm(pads[dh][:, :], onesf[:, 0:128],
                 lnwrow[:, dh * 512:(dh + 1) * 512])
    E.blk(tensor=flb)

    def flbe(v):
        for dh in range(2):
            v.copy(lnb_bc[:, dh * 512:(dh + 1) * 512], pads[dh][:, :], nodep=True)
    E.blk(vector=flbe)

    for s in range(B_PER_CORE):
        base = s * T
        for grp in range(8):
            tt0 = grp * 2

            def fx(sy, tt0=tt0, base=base):
                for j in range(2):
                    E.dma(sy, xs[:, j, :],
                          x_in[base + (tt0 + j) * 128:base + (tt0 + j + 1) * 128, :])
                E.dwait(sy)
            E.blk(sync=fx)

            emit_ln(E, xs, 2, st, xs)
            emit_ln(E, xs, 2, st, xs)

            def faf(v):
                for j in range(2):
                    v.tt(xs[:, j, :], xs[:, j, :], lnw_bc[:, :], ALU.mult,
                         nodep=True)
                v.e.drain()
                for j in range(2):
                    v.tt(xs[:, j, :], xs[:, j, :], lnb_bc[:, :], ALU.add,
                         nodep=True)
            E.blk(vector=faf)

            for j in range(2):
                def ftr(t, j=j):
                    for c in range(8):
                        t.transpose(pads[c][:, 0:128], xs[:, j, c * 128:(c + 1) * 128],
                                    iden[:, :])
                E.blk(tensor=ftr)

                def fte(v, j=j):
                    for c in range(8):
                        v.copy(xnT_t[:, c, :], pads[c][:, 0:128], nodep=True)
                E.blk(vector=fte)

                def fpm(t, j=j):
                    for k in range(8):
                        t.mm(pads[0][:, 0:352], xnT_t[:, k, :],
                             wcat[:, k, :], start=(k == 0), stop=False)
                    t.mm(pads[0][:, 0:352], onesf[:, 0:128], bcat_r[:, :],
                         start=False, stop=True)
                E.blk(tensor=fpm)

                def fse(sC, j=j):
                    sC.act(sgb[:, j, :, 0:8],
                           pads[0][:, 0:32].rearrange("p (h n) -> p h n", h=4),
                           AF.Sigmoid, nodep=True)
                    sC.act(sgb[:, j, :, 8:16],
                           pads[0][:, 288:320].rearrange("p (h n) -> p h n", h=4),
                           AF.Sigmoid, nodep=True)
                E.blk(scalar=fse)

                def fsv(v, j=j):
                    v.ts(sgb[:, j, :, 0:8], sgb[:, j, :, 0:8], 2.0, -1.0,
                         ALU.mult, ALU.add)
                    v.copy(sgb[:, j, :, 16:24],
                           pads[0][:, 320:352].rearrange("p (h n) -> p h n", h=4),
                           nodep=True)
                    v.copy(sk[:, j, :, :],
                           pads[0][:, 32:288].rearrange("p (h n) -> p h n", h=4),
                           nodep=True)
                E.blk(vector=fsv)

            for it in range(5):
                def fsk1(sC):
                    sC.act(ske[:, :, :, :], sk[:, :, :, :], AF.Exp)
                E.blk(scalar=fsk1)

                def fsk2(v):
                    v.red(lse[:, :, :, :].rearrange("p a h n -> p (a h) n")
                          .unsqueeze(3),
                          ske[:, :, :, :].rearrange("p a h (i j) -> p (a h) i j",
                                                    i=8), ALU.add)
                E.blk(vector=fsk2)

                def fsk3(sC):
                    sC.act(lse[:, :, :, :], lse[:, :, :, :], AF.Ln)
                E.blk(scalar=fsk3)

                def fsk4(v):
                    v.tt(sk[:, :, :, :].rearrange("p a h (i j) -> p (a h) i j", i=8),
                         sk[:, :, :, :].rearrange("p a h (i j) -> p (a h) i j", i=8),
                         lse[:, :, :, :].rearrange("p a h n -> p (a h) n")
                         .unsqueeze(3).broadcast_to((128, 8, 8, 8)), ALU.subtract)
                E.blk(vector=fsk4)

                def fsk5(sC):
                    sC.act(ske[:, :, :, :], sk[:, :, :, :], AF.Exp)
                E.blk(scalar=fsk5)

                def fsk6(v):
                    v.red(lse[:, :, :, :].rearrange("p a h n -> p (a h) n")
                          .unsqueeze(3),
                          ske[:, :, :, :].rearrange("p a h (i j) -> p (a h) i j",
                                                    i=8).transpose([0, 1, 3, 2]),
                          ALU.add)
                E.blk(vector=fsk6)

                def fsk7(sC):
                    sC.act(lse[:, :, :, :], lse[:, :, :, :], AF.Ln)
                E.blk(scalar=fsk7)

                def fsk8(v):
                    v.tt(sk[:, :, :, :].rearrange("p a h (i j) -> p (a h) i j", i=8),
                         sk[:, :, :, :].rearrange("p a h (i j) -> p (a h) i j", i=8),
                         lse[:, :, :, :].rearrange("p a h n -> p (a h) n")
                         .unsqueeze(2).broadcast_to((128, 8, 8, 8)), ALU.subtract)
                E.blk(vector=fsk8)

            def fskf(sC):
                sC.act(sk[:, :, :, :], sk[:, :, :, :], AF.Exp)
            E.blk(scalar=fskf)

            def fgr(v):
                io8b = io8[:, :].unsqueeze(1).broadcast_to((128, 8, 8))
                io9b = io9[:, :].unsqueeze(1).broadcast_to((128, 8, 8))
                sk4 = sk[:, :, :, :].rearrange("p a h (i j) -> p (a h) i j", i=8)
                hard4 = hard[:, :, :, :].rearrange("p a h (i j) -> p (a h) i j", i=8)
                RM2 = RM[:, :, :, :].rearrange("p a h n -> p (a h) n")
                rowm2 = rowm[:, :, :, :].rearrange("p a h n -> p (a h) n")
                colm2 = colm[:, :, :, :].rearrange("p a h n -> p (a h) n")
                rsel2 = rsel[:, :, :, :].rearrange("p a h n -> p (a h) n")
                m = ridx[:, :, :, :].rearrange("p a h n -> p (a h) n")
                ohr2 = oh_r[:, :, :, :].rearrange("p a h n -> p (a h) n")
                ohc2 = oh_c[:, :, :, :].rearrange("p a h n -> p (a h) n")
                rowv2 = rowv[:, :, :, :].rearrange("p a h n -> p (a h) n")
                tmp42 = tmp4[:, :, :, :].rearrange("p a h (i j) -> p (a h) i j", i=8)
                v.memset(hard4, 0.0)
                v.memset(rowm2, 0.0)
                v.memset(colm2, 0.0)
                v.red(RM2.unsqueeze(3), sk4, ALU.max)
                for i in range(8):
                    v.tt(rsel2, RM2, rowm2, ALU.add)
                    v.red(m.unsqueeze(3)[:, :, 0:1, :], rsel2.unsqueeze(2),
                          ALU.max)
                    v.tt(rsel2, rsel2, m.broadcast_to((128, 8, 8)), ALU.is_ge)
                    v.tt(rowv2, rsel2, io9b, ALU.mult)
                    v.ts(rowv2, rowv2, 1.0, 9.0, ALU.mult, ALU.add)
                    v.red(m.unsqueeze(3)[:, :, 0:1, :], rowv2.unsqueeze(2),
                          ALU.min)
                    v.tt(ohr2, io8b, m.broadcast_to((128, 8, 8)), ALU.is_equal)
                    v.tt(tmp42, sk4,
                         ohr2.unsqueeze(3).broadcast_to((128, 8, 8, 8)), ALU.mult)
                    v.red(rowv2.unsqueeze(3), tmp42.transpose([0, 1, 3, 2]),
                          ALU.add)
                    v.tt(rowv2, rowv2, colm2, ALU.add)
                    v.red(m.unsqueeze(3)[:, :, 0:1, :], rowv2.unsqueeze(2),
                          ALU.max)
                    v.tt(ohc2, rowv2, m.broadcast_to((128, 8, 8)), ALU.is_ge)
                    v.tt(rowv2, ohc2, io9b, ALU.mult)
                    v.ts(rowv2, rowv2, 1.0, 9.0, ALU.mult, ALU.add)
                    v.red(m.unsqueeze(3)[:, :, 0:1, :], rowv2.unsqueeze(2),
                          ALU.min)
                    v.tt(ohc2, io8b, m.broadcast_to((128, 8, 8)), ALU.is_equal)
                    v.tt(tmp42, ohr2.unsqueeze(3).broadcast_to((128, 8, 8, 8)),
                         ohc2.unsqueeze(2).broadcast_to((128, 8, 8, 8)), ALU.mult)
                    v.tt(hard4, hard4, tmp42, ALU.add)
                    v.stt(rowm2, ohr2, -1e30, rowm2, ALU.mult, ALU.add)
                    v.stt(colm2, ohc2, -1e30, colm2, ALU.mult, ALU.add)
            E.blk(vector=fgr)

            def fgs(v):
                v.tt(gsc[:, :, :, 0:8], sgb[:, :, :, 8:16], sgb[:, :, :, 0:8],
                     ALU.mult)
                v.tt(gsc[:, :, :, 8:16], sgb[:, :, :, 8:16], sgb[:, :, :, 16:24],
                     ALU.mult)
                v.e.drain()
                v.tt(gsc[:, :, :, 8:16], sgb[:, :, :, 16:24], gsc[:, :, :, 8:16],
                     ALU.subtract)
            E.blk(vector=fgs)

            def fso(sy, s=s, tt0=tt0):
                for j in range(2):
                    E.dma(sy, sd_dram[s, tt0 + j, :, :, 0:64], hard[:, j, :, :])
                    E.dma(sy, sd_dram[s, tt0 + j, :, :, 64:72], gsc[:, j, :, 0:8])
                    E.dma(sy, sd_dram[s, tt0 + j, :, :, 72:80], gsc[:, j, :, 8:16])
                E.dwait(sy)
            E.blk(sync=fso)

    def fsl(sy):
        for s2 in range(2):
            for c2 in range(16):
                E.dma(sy, slab[s2 * 64 + c2 * 4:s2 * 64 + c2 * 4 + 4, :, :],
                      sd_dram[s2, c2].rearrange("t h e -> h t e"))
        E.dwait(sy)
    E.blk(sync=fsl)

    def fpb(v):
        v.copy(Tm[:, :], iden8[:, :])
        v.memset(ev[:, :], 0.0)
        v.e.drain()
        T3v = Tm[:, :].rearrange("p (j k) -> p j k", j=8)
        t4v = t4[:, :].rearrange("p (i j k) -> p i j k", i=8, j=8)
        t3v = t3[:, :].rearrange("p (i j) -> p i j", i=8)
        for step in range(128):
            P = slab[:, step, 0:64].rearrange("p (i j) -> p i j", i=8)
            gs_ = slab[:, step, 64:72]
            c_ = slab[:, step, 72:80]
            v.tt(t4v, P.unsqueeze(3).broadcast_to((128, 8, 8, 8)),
                 Tm[:, :].rearrange("p (j k) -> p j k", j=8).unsqueeze(1)
                 .broadcast_to((128, 8, 8, 8)), ALU.mult)
            v.tt(t3v, P, ev[:, :].unsqueeze(1).broadcast_to((128, 8, 8)),
                 ALU.mult)
            v.e.drain()
            v.red(Tm[:, :].rearrange("p (i k) -> p i k", i=8).unsqueeze(3),
                  t4v.transpose([0, 1, 3, 2]), ALU.add)
            v.red(ev[:, :].unsqueeze(2), t3v, ALU.add)
            v.e.drain()
            v.tt(Tm[:, :].rearrange("p (i k) -> p i k", i=8),
                 Tm[:, :].rearrange("p (i k) -> p i k", i=8),
                 gs_.unsqueeze(2).broadcast_to((128, 8, 8)), ALU.mult)
            v.tt(ev[:, :], ev[:, :], gs_, ALU.mult)
            v.e.drain()
            v.tt(ev[:, :], ev[:, :], c_, ALU.add)
            v.e.drain()
    E.blk(vector=fpb)

    for r in range(4):
        sh = 4 * (2 ** r)

        def fcp(v):
            v.copy(Tsh[:, :], iden8[:, :])
            v.memset(esh[:, :], 0.0)
        E.blk(vector=fcp)

        def fcd(sy, sh=sh):
            for s2 in range(2):
                E.dma(sy, Tsh[s2 * 64 + sh:s2 * 64 + 64, :],
                      Tm[s2 * 64:s2 * 64 + 64 - sh, :])
                E.dma(sy, esh[s2 * 64 + sh:s2 * 64 + 64, :],
                      ev[s2 * 64:s2 * 64 + 64 - sh, :])
            E.dwait(sy)
        E.blk(sync=fcd)

        def fcc(v):
            t4v = t4[:, :].rearrange("p (i j k) -> p i j k", i=8, j=8)
            t3v = t3[:, :].rearrange("p (i j) -> p i j", i=8)
            Tc = Tm[:, :].rearrange("p (i j) -> p i j", i=8)
            v.tt(t4v, Tc.unsqueeze(3).broadcast_to((128, 8, 8, 8)),
                 Tsh[:, :].rearrange("p (j k) -> p j k", j=8).unsqueeze(1)
                 .broadcast_to((128, 8, 8, 8)), ALU.mult)
            v.tt(t3v, Tc, esh[:, :].unsqueeze(1).broadcast_to((128, 8, 8)),
                 ALU.mult)
            v.e.drain()
            v.red(Tm[:, :].rearrange("p (i k) -> p i k", i=8).unsqueeze(3),
                  t4v.transpose([0, 1, 3, 2]), ALU.add)
            v.red(esh[:, :].unsqueeze(2), t3v, ALU.add)
            v.e.drain()
            v.tt(ev[:, :], ev[:, :], esh[:, :], ALU.add)
        E.blk(vector=fcc)

    def fhe(v):
        v.memset(esh[:, :], 0.0)
    E.blk(vector=fhe)

    def fhd(sy):
        for s2 in range(2):
            E.dma(sy, esh[s2 * 64 + 4:s2 * 64 + 64, :],
                  ev[s2 * 64:s2 * 64 + 60, :])
        E.dwait(sy)
    E.blk(sync=fhd)

    def fpd(v):
        v.copy(hv[:, :], esh[:, :])
        v.e.drain()
        t3v = t3[:, :].rearrange("p (i j) -> p i j", i=8)
        for step in range(128):
            P = slab[:, step, 0:64].rearrange("p (i j) -> p i j", i=8)
            gs_ = slab[:, step, 64:72]
            c_ = slab[:, step, 72:80]
            v.tt(t3v, P, hv[:, :].unsqueeze(1).broadcast_to((128, 8, 8)),
                 ALU.mult)
            v.e.drain()
            v.red(hv[:, :].unsqueeze(2), t3v, ALU.add)
            v.e.drain()
            v.tt(hv[:, :], hv[:, :], gs_, ALU.mult)
            v.e.drain()
            v.tt(hv[:, :], hv[:, :], c_, ALU.add)
            v.e.drain()
            v.copy(ysl[:, :, step:step + 1], hv[:, :].unsqueeze(2), nodep=True)
            v.e.drain()
    E.blk(vector=fpd)

    def fyo(sy):
        E.dma(sy, ys_dram, ysl[:, :, :])
        E.dwait(sy)
    E.blk(sync=fyo)

    for s in range(B_PER_CORE):
        base = s * T

        def fyt(g, s=s):
            v5 = ys_dram.rearrange("(s2 c h) n t -> s2 h n c t", s2=2, c=16)
            for h in range(4):
                E.dma(g, ysT[h * 8:(h + 1) * 8, :].rearrange(
                    "p (c t) -> p c t", c=16), v5[s, h])
            E.dwait(g)
        E.blk(gpsimd=fyt)

        for tp in range(8):
            def fxr(sy, tp=tp, base=base):
                for j in range(2):
                    tt = tp * 2 + j
                    E.dma(sy, xs[:, j, :],
                          x_in[base + tt * 128:base + (tt + 1) * 128, :])
                    E.dma(sy, res2[:, j, :],
                          x_in[base + tt * 128:base + (tt + 1) * 128, :])
                E.dwait(sy)
            E.blk(sync=fxr)

            emit_ln(E, res2, 2, st, res2)

            def fom(t, tp=tp):
                for j in range(2):
                    for dh in range(2):
                        p = pads[j * 2 + dh]
                        t.mm(p[:, :], ysT[:, (tp * 2 + j) * 128:(tp * 2 + j + 1) * 128],
                             outw_sb[:, dh * 512:(dh + 1) * 512],
                             start=True, stop=False)
                        t.mm(p[:, :], onesf[:, 0:128],
                             outb_r[:, dh * 512:(dh + 1) * 512],
                             start=False, stop=True)
            E.blk(tensor=fom)

            def fev(v, tp=tp):
                for j in range(2):
                    v.tt(xs[:, j, :], xs[:, j, :], res2[:, j, :], ALU.add,
                         nodep=True)
                v.e.drain()
                for j in range(2):
                    for dh in range(2):
                        v.tt(xs[:, j, dh * 512:(dh + 1) * 512],
                             xs[:, j, dh * 512:(dh + 1) * 512],
                             pads[j * 2 + dh][:, :], ALU.add, nodep=True)
            E.blk(vector=fev)

            def fxo(sy, tp=tp, base=base):
                for j in range(2):
                    tt = tp * 2 + j
                    E.dma(sy, x_out[base + tt * 128:base + (tt + 1) * 128, :],
                          xs[:, j, :])
                E.dwait(sy)
            E.blk(sync=fxo)

    E.pop_scope()
    if standalone:
        E.finish()
        return E.nc
    return None


def launch_l0_attn(state, inp):
    nc = build_l0_attn()
    wcat = np.concatenate([inp["L0_attn_sign_W"], inp["L0_attn_perm_W"],
                           inp["L0_attn_gate_W"], inp["L0_attn_in_W"]], axis=1)
    bcat = np.concatenate([inp["L0_attn_sign_b"], inp["L0_attn_perm_b"],
                           inp["L0_attn_gate_b"], inp["L0_attn_in_b"]])
    io8 = np.ascontiguousarray(np.broadcast_to(
        np.arange(8, dtype=np.float32)[None, :], (128, 8)))
    iden8 = np.ascontiguousarray(np.broadcast_to(
        np.eye(8, dtype=np.float32).reshape(1, 64), (128, 64)))
    in_maps = [{"x": state[i], "wcat": wcat, "bcat": bcat,
                "lnw": inp["L0_attn_ln_w"], "lnb": inp["L0_attn_ln_b"],
                "outw": inp["L0_attn_out_W"], "outb": inp["L0_attn_out_b"],
                "io8": io8, "iden8": iden8} for i in range(NCORES)]
    res = run_bass_kernel_spmd(nc, in_maps, core_ids=list(range(NCORES)))
    return [np.asarray(r["xout"]) for r in res.results]





def prep_l0_attn(inp):
    wcat = np.concatenate([inp["L0_attn_sign_W"], inp["L0_attn_perm_W"],
                           inp["L0_attn_gate_W"], inp["L0_attn_in_W"]], axis=1)
    bcat = np.concatenate([inp["L0_attn_sign_b"], inp["L0_attn_perm_b"],
                           inp["L0_attn_gate_b"], inp["L0_attn_in_b"]])
    io8 = np.ascontiguousarray(np.broadcast_to(
        np.arange(8, dtype=np.float32)[None, :], (128, 8)))
    iden8 = np.ascontiguousarray(np.broadcast_to(
        np.eye(8, dtype=np.float32).reshape(1, 64), (128, 64)))
    return {"wcat": np.ascontiguousarray(wcat, np.float32),
            "bcat": np.ascontiguousarray(bcat, np.float32),
            "lnw": inp["L0_attn_ln_w"], "lnb": inp["L0_attn_ln_b"],
            "outw": inp["L0_attn_out_W"], "outb": inp["L0_attn_out_b"],
            "io8": io8, "iden8": iden8}


def prep_l0_ffn(inp):
    return {"wgT": np.ascontiguousarray(inp["L0_ffn_w_gate"].T),
            "wuT": np.ascontiguousarray(inp["L0_ffn_w_up"].T),
            "wdT": np.ascontiguousarray(inp["L0_ffn_w_down"].T)}


def prep_l1_attn(inp):
    bq = np.zeros((4, 128, 512), np.float32)
    f = np.arange(512)[None, :]
    p = np.arange(128)[:, None]
    for c in range(4):
        rel = f - p - 128.0 * c
        bq[c] = np.where(rel <= 0, rel, -1e30)
    return {"w": inp["L1_attn_w_in"], "b": inp["L1_attn_b_in"],
            "wo": inp["L1_attn_w_out"], "bo": inp["L1_attn_b_out"], "bq": bq}


def prep_l1_ffn(inp):
    return {"w": inp["L1_ffn_weight"], "b": inp["L1_ffn_bias"]}


def prep_l2_attn(inp):
    return {"ipT": np.ascontiguousarray(inp["L2_attn_in_proj_w"].T),
            "cw": np.ascontiguousarray(inp["L2_attn_conv_w"].reshape(16, 128, 4)
                                       .transpose(1, 0, 2).reshape(128, 64)),
            "cb": np.ascontiguousarray(inp["L2_attn_conv_b"].reshape(16, 128).T),
            "xpwT": np.ascontiguousarray(inp["L2_attn_x_proj_w"].T),
            "dtwT": np.ascontiguousarray(inp["L2_attn_dt_proj_w"].T),
            "dtb": inp["L2_attn_dt_proj_b"],
            "alog": np.ascontiguousarray(inp["L2_attn_A_log"].reshape(16, 128, 8)
                                         .transpose(1, 0, 2).reshape(128, 128)),
            "ut": np.ascontiguousarray(inp["L2_attn_U_param"].T),
            "vp": inp["L2_attn_V_param"],
            "ls": np.asarray(inp["L2_attn_low_rank_scale"], np.float32).reshape(1, 1),
            "dcol": np.ascontiguousarray(inp["L2_attn_D"].reshape(16, 128).T),
            "opT": np.ascontiguousarray(inp["L2_attn_out_proj_w"].T)}


def prep_l2_ffn(inp):
    return {"rwT": np.ascontiguousarray(inp["L2_ffn_router_w"].T),
            "gw": inp["L2_ffn_gate_w"], "gb": inp["L2_ffn_gate_b"],
            "uw": inp["L2_ffn_up_w"], "ub": inp["L2_ffn_up_b"],
            "dw": inp["L2_ffn_down_w"], "db": inp["L2_ffn_down_b"]}


def prep_l3_attn(inp):
    return {"ip": inp["L3_attn_in_proj_w"],
            "cw": np.ascontiguousarray(inp["L3_attn_conv_w"].reshape(4, 16, 128, 4)
                                       .transpose(0, 2, 1, 3).reshape(4, 128, 64)),
            "cb": np.ascontiguousarray(inp["L3_attn_conv_b"].reshape(4, 16, 128)
                                       .transpose(0, 2, 1)),
            "xpw": inp["L3_attn_x_proj_w"], "dtw": inp["L3_attn_dt_proj_w"],
            "dtb": inp["L3_attn_dt_proj_b"],
            "alog": np.ascontiguousarray(inp["L3_attn_A_log"].reshape(4, 16, 128, 16)
                                         .transpose(0, 2, 1, 3).reshape(4, 128, 256)),
            "dcol": np.ascontiguousarray(inp["L3_attn_D"].reshape(4, 16, 128)
                                         .transpose(0, 2, 1)),
            "ow": inp["L3_attn_out_proj_w"],
            "lnw": inp["L3_attn_ln_w"], "lnb": inp["L3_attn_ln_b"]}


def prep_l3_ffn(inp):
    HID = 2730
    w1 = inp["L3_ffn_w1"]
    w1a = np.zeros((2, 1024, 2816), np.float32)
    w1g = np.zeros((2, 1024, 2816), np.float32)
    w1a[:, :, 0:HID] = w1[:, :, 0:HID]
    w1g[:, :, 0:HID] = w1[:, :, HID:2 * HID]
    b1 = inp["L3_ffn_b1"]
    b1a = np.zeros((2, 2816), np.float32)
    b1g = np.zeros((2, 2816), np.float32)
    b1a[:, 0:HID] = b1[:, 0:HID]
    b1g[:, 0:HID] = b1[:, HID:2 * HID]
    w2p = np.zeros((2, 2816, 1024), np.float32)
    w2p[:, 0:HID, :] = inp["L3_ffn_w2"]
    qs = np.ascontiguousarray(np.broadcast_to(
        inp["L3_ffn_q_scale"][:, None, :], (2, 128, 64)))
    ks = np.ascontiguousarray(np.broadcast_to(
        inp["L3_ffn_k_scale"][:, None, :], (2, 128, 64)))
    DIM_HEAD, WINDOW, N = 64, 512, 2048
    half = np.arange(0, DIM_HEAD, 2, dtype=np.float64)
    inv_freq = 1.0 / 10000.0 ** (half / DIM_HEAD)
    t = np.arange(N, dtype=np.float64)
    freqs = t[:, None] * inv_freq[None, :]
    freqs = np.concatenate([freqs, freqs], -1)
    bse = (half + 0.4 * DIM_HEAD) / (1.4 * DIM_HEAD)
    power = (t - N // 2) / (WINDOW // 2)
    scale = bse[None, :] ** power[:, None]
    scale = np.concatenate([scale, scale], -1)
    cos, sin = np.cos(freqs), np.sin(freqs)
    rot = np.stack([cos * scale, sin * scale, cos / scale,
                    sin / scale]).astype(np.float32)
    msk = np.zeros((8, 128, 512), np.float32)
    p = np.arange(128)[:, None]
    f = np.arange(512)[None, :]
    for c in range(8):
        offp = 128 * c if c < 4 else 512 + 128 * (c - 4)
        d = (p - f) + offp
        msk[c] = np.where((d >= 0) & (d <= 512), 0.0, -1e30)
    return {"wqkv": inp["L3_ffn_wqkv"], "qs": qs, "ks": ks,
            "wo": inp["L3_ffn_wo"], "lng": inp["L3_ffn_ln_g"],
            "lnbf": inp["L3_ffn_ln_b"], "w1a": w1a, "w1g": w1g,
            "b1a": b1a, "b1g": b1g, "w2p": w2p, "b2": inp["L3_ffn_b2"],
            "rot": rot, "msk": msk}


_BLOCKS = [
    ("b0_", build_l0_attn, prep_l0_attn),
    ("b1_", build_l0_ffn, prep_l0_ffn),
    ("b2_", build_l1_attn, prep_l1_attn),
    ("b3_", build_l1_ffn, prep_l1_ffn),
    ("b4_", build_l2_attn, prep_l2_attn),
    ("b5_", build_l2_ffn, prep_l2_ffn),
    ("b6_", build_l3_attn, prep_l3_attn),
    ("b7_", build_l3_ffn, prep_l3_ffn),
]


def build_fused():
    E = Em("mega")
    E.constf(LN_EPS)
    E.constf(1e-12)
    x0 = E.dram("x0", [TOKENS, D], kind="ExternalInput")
    xf = E.dram("xf", [TOKENS, D], kind="ExternalOutput")
    streams = [x0]
    for i in range(7):
        streams.append(E.dram(f"strm{i}", [TOKENS, D]))
    streams.append(xf)
    for i, (pf, build, _) in enumerate(_BLOCKS):
        build(E=E, xin=streams[i], xout=streams[i + 1], prefix=pf)
    E.prefix = ""
    E.finish()
    return E.nc


def kernel(**inp):
    x = np.asarray(inp["L0_attn_x"], np.float32)
    nc = build_fused()
    in_maps = []
    preps = [(pf, prep(inp)) for pf, _, prep in _BLOCKS]
    for i in range(NCORES):
        m = {"x0": np.ascontiguousarray(x[2 * i:2 * i + 2].reshape(TOKENS, D))}
        for pf, d in preps:
            for k, v in d.items():
                m[pf + k] = v
        in_maps.append(m)
    res = run_bass_kernel_spmd(nc, in_maps, core_ids=list(range(NCORES)))
    out = np.concatenate([np.asarray(r["xf"]).reshape(B_PER_CORE, SEQ, D)
                          for r in res.results], axis=0)
    return out.astype(np.float32)


def kernel_unfused(**inp):
    x = np.asarray(inp["L0_attn_x"], np.float32)
    state = [np.ascontiguousarray(x[2 * i:2 * i + 2].reshape(TOKENS, D))
             for i in range(NCORES)]
    state = launch_l0_attn(state, inp)
    state = launch_l0_ffn(state, inp)
    state = launch_l1_attn(state, inp)
    state = launch_l1_ffn(state, inp)
    state = launch_l2_attn(state, inp)
    state = launch_l2_ffn(state, inp)
    state = launch_l3_attn(state, inp)
    state = launch_l3_ffn(state, inp)
    out = np.concatenate([s.reshape(B_PER_CORE, SEQ, D) for s in state], axis=0)
    return out.astype(np.float32)
